# version 01
# baseline (speedup 1.0000x reference)
import contextlib
import numpy as np
import concourse.bass as bass
import concourse.mybir as mybir
from concourse.bass_utils import run_bass_kernel_spmd

F32 = mybir.dt.float32
BF16 = mybir.dt.bfloat16
F16 = mybir.dt.float16
F8 = mybir.dt.float8e4
I32 = mybir.dt.int32
AF = mybir.ActivationFunctionType
ALU = mybir.AluOpType

LN2 = 0.6931471805599453
INVLN2 = 1.4426950408889634
EPS_ATTN = 1.1920929e-07
EPS_LN = 1e-5
TINY = 1e-38
CSH = 20.0
T = 2048
D = 1024
NT = 16
NSEQ = 2

TAP = None
import os
DBG_HEADS = int(os.environ.get("MK_HEADS", "16"))
DBG_SEQ = int(os.environ.get("MK_SEQ", str(NSEQ)))
DBG_FFN = os.environ.get("MK_FFN", "1") == "1"
DBG_LN1 = os.environ.get("MK_LN1", "1") == "1"
DBG_ATTN = os.environ.get("MK_ATTN", "1") == "1"
DBG_HSTAGE = int(os.environ.get("MK_HSTAGE", "9"))
DBG_G = int(os.environ.get("MK_G", "4"))
DBG_NOA1 = os.environ.get("MK_NOA1", "0") == "1"
DBG_NOD1 = os.environ.get("MK_NOD1", "0") == "1"
DBG_D1SEL = int(os.environ.get("MK_D1SEL", "7"))
DBG_NOGPS = os.environ.get("MK_NOGPS", "0") == "1"
DBG_SERIAL = os.environ.get("MK_SERIAL", "0") != "0"


def build_program():
    nc = bass.Bass("TRN2", target_bir_lowering=False, debug=False)

    x_tok = nc.dram_tensor("x_tok", [NSEQ, T, D], F32, kind="ExternalInput").ap()
    x_feat = nc.dram_tensor("x_feat", [NSEQ, D, T], F32, kind="ExternalInput").ap()
    wdq = nc.dram_tensor("wdq", [128, 256], F32, kind="ExternalInput").ap()
    wuqt = nc.dram_tensor("wuqt", [128, 256], F32, kind="ExternalInput").ap()
    wdkv = nc.dram_tensor("wdkv", [128, 512], F32, kind="ExternalInput").ap()
    wukvt = nc.dram_tensor("wukvt", [128, 1024], F32, kind="ExternalInput").ap()
    wqrt_p = nc.dram_tensor("wqrt_p", [128, 128], F32, kind="ExternalInput").ap()
    wkrt_e = nc.dram_tensor("wkrt_e", [128, 128], F32, kind="ExternalInput").ap()
    wkrt_o = nc.dram_tensor("wkrt_o", [128, 128], F32, kind="ExternalInput").ap()
    wot = nc.dram_tensor("wot", [64, 64], F32, kind="ExternalInput").ap()
    wq_b = nc.dram_tensor("wq_b", [128, 64], F32, kind="ExternalInput").ap()
    wk_b = nc.dram_tensor("wk_b", [128, 64], F32, kind="ExternalInput").ap()
    w1tl = nc.dram_tensor("w1tl", [32, 128, 1024], F32, kind="ExternalInput").ap()
    w2t = nc.dram_tensor("w2t", [4096, D], F32, kind="ExternalInput").ap()
    b1_t = nc.dram_tensor("b1_t", [128, 32], F32, kind="ExternalInput").ap()
    b2_t = nc.dram_tensor("b2_t", [128, 8], F32, kind="ExternalInput").ap()
    identf = nc.dram_tensor("identf", [128, 128], F32, kind="ExternalInput").ap()
    maskd = nc.dram_tensor("maskd", [128, 128], BF16, kind="ExternalInput").ap()
    out_tok = nc.dram_tensor("out_tok", [NSEQ, T, D], F32, kind="ExternalOutput").ap()
    y_spill = nc.dram_tensor("y_spill", [T, D], F32, kind="Internal").ap()
    weff_scr = nc.dram_tensor("weff_scr", [64, 192], F32, kind="Internal").ap()
    tap_out = None
    if TAP is not None:
        tap_out = nc.dram_tensor("tap", [NSEQ, T, D], F32, kind="ExternalOutput").ap()

    ctx = contextlib.ExitStack()
    with ctx:
        sb = lambda name, shape, dt: ctx.enter_context(nc.sbuf_tensor(name, shape, dt))
        psb = lambda name, shape: ctx.enter_context(nc.psum_tensor(name, shape, F32))

        cpool = sb("cpool", [128, 16], F32)
        ident_bf = sb("ident_bf", [128, 128], BF16)
        ident_h = sb("ident_h", [128, 128], F16)
        maskd_sb = sb("maskd_sb", [128, 128], BF16)
        wqb_sb = sb("wqb_sb", [128, 64], F32)
        wkb_sb = sb("wkb_sb", [128, 64], F32)
        wdq_bf = sb("wdq_bf", [128, 256], BF16)
        wuqt_bf = sb("wuqt_bf", [128, 256], BF16)
        wdkv_bf = sb("wdkv_bf", [128, 512], BF16)
        wukvt_bf = sb("wukvt_bf", [128, 1024], BF16)
        wqrt_bf = sb("wqrt_bf", [128, 128], BF16)
        wkrt_bf = [sb("wkrt_bf0", [128, 128], BF16), sb("wkrt_bf1", [128, 128], BF16)]
        wot_bf = sb("wot_bf", [64, 64], BF16)
        wqeff = [sb("wqeff0", [128, 64], BF16), sb("wqeff1", [128, 64], BF16)]
        wkveff = [sb("wkveff0", [128, 128], BF16), sb("wkveff1", [128, 128], BF16)]
        b1_sb = sb("b1_sb", [128, 32], F32)
        b2_sb = sb("b2_sb", [128, 8], F32)
        b2q_sb = sb("b2q_sb", [128, 8], F32)

        featbuf = sb("featbuf", [128, 8 * T], BF16)
        attn_bf = sb("attn_bf", [128, NT * D], BF16)
        w2qT = sb("w2qT", [128, 32 * D], F8)
        hqT = sb("hqT", [128, 32 * 1024], F8)

        kT_bf = sb("kT_bf", [128, T], BF16)
        qT_bf = sb("qT_bf", [128, T], BF16)
        v_aug = sb("v_aug", [128, NT * 66], BF16)
        kc_bf = sb("kc_bf", [128, NT * 64], BF16)
        qc_bf = sb("qc_bf", [128, NT * 64], BF16)
        svec = sb("svec", [128, 128], F32)
        PT = [sb("PT0", [128, 1024], BF16), sb("PT1", [128, 1024], BF16)]
        ctxTa_bf = sb("ctxTa_bf", [128, 128], BF16)
        sq_scr = sb("sq_scr", [128, 8 * 64], BF16)

        ln_a = [sb("ln_a0", [128, D], F32), sb("ln_a1", [128, D], F32)]
        ln_b = [sb("ln_b0", [128, D], F32), sb("ln_b1", [128, D], F32)]
        ybf_tile = sb("ybf_tile", [128, D], BF16)
        w1stage = [sb("w1stage0", [128, 1024], F32), sb("w1stage1", [128, 1024], F32)]
        w1bf = [sb("w1bf0", [128, 1024], BF16), sb("w1bf1", [128, 1024], BF16)]
        hwork = [sb("hwork0", [128, 512], F32), sb("hwork1", [128, 512], F32)]
        ql = sb("ql", [128, 512], F32)
        qi = sb("qi", [128, 512], I32)
        qsc = sb("qsc", [128, 512], F32)
        qinv = sb("qinv", [128, 512], F32)
        qv = sb("qv", [128, 512], F32)
        qc = sb("qc", [128, 512], F32)
        b2h = sb("b2h", [128, 8], F16)
        oq_all = sb("oq_all", [128, 2048], F16)

        psum_S = [psb("psum_S0", [128, 1024]), psb("psum_S1", [128, 1024])]
        psum_ctx = psb("psum_ctx", [128, 1024])
        psum_m = [psb("psum_m0", [128, 512]), psb("psum_m1", [128, 512])]

        CONSTS = sorted({TINY, -3 * LN2, 3 * LN2, LN2, -LN2, EPS_ATTN,
                         EPS_LN, 0.125, -CSH, -0.5, -1.0, 1.0 / 64.0, 0.5, 1.0, 0.0})
        for j, v in enumerate(CONSTS):
            nc.const_aps.register(F32, float(v), cpool[:, j : j + 1])

        sems = {}
        for s in ["ld", "lw0", "lw1", "la0", "la1", "sb0", "sb1",
                  "pe", "act", "dve", "gps"]:
            sems[s] = ctx.enter_context(nc.semaphore("sem_" + s))
        cnt = {s: 0 for s in sems}

        block = ctx.enter_context(nc.Block())
        OPS = []

        cnt_at = {}

        def op(eng, fn, waits=(), inc=None, amt=1):
            waits = [(sn, v) for (sn, v) in waits if v > 0]
            if DBG_SERIAL:
                mode = os.environ.get("MK_SERIAL", "0")
                DMAS = ("ld", "lw0", "lw1", "la0", "la1", "sb0", "sb1")
                if mode == "1":
                    waits = [(sn, v) for (sn, v) in cnt.items() if v > 0]
                elif mode == "2":
                    if eng in ("sync", "scalar") and fn.__name__ == "<lambda>" and inc in DMAS:
                        waits = waits + [(sn, cnt[sn]) for sn in ("pe", "act", "dve", "gps") if cnt[sn] > 0]
                    else:
                        waits = waits + [(sn, cnt[sn]) for sn in DMAS if cnt[sn] > 0]
                elif mode == "3":
                    if eng == "tensor":
                        waits = waits + [(sn, cnt[sn]) for sn in ("act", "dve", "gps") if cnt[sn] > 0]
                    elif eng in ("vector", "scalar", "gpsimd"):
                        waits = waits + [("pe", cnt["pe"])] if cnt["pe"] > 0 else waits
                elif mode == "4":
                    if eng == "vector" and cnt["act"] > 0:
                        waits = waits + [("act", cnt["act"])]
                    elif eng == "scalar" and inc == "act" and cnt["dve"] > 0:
                        waits = waits + [("dve", cnt["dve"])]
            OPS.append((eng, fn, waits, inc, amt))
            if inc is not None:
                cnt_at[id(fn)] = cnt[inc]
                cnt[inc] += amt
                return cnt[inc]
            return None

        def dma(queue, out_ap, in_ap, waits=(), sem="ld"):
            c = op(queue, lambda e, o=out_ap, i=in_ap: e.dma_start(out=o, in_=i),
                   waits=waits, inc=sem, amt=16)
            return (sem, c)

        W = lambda *pairs: list(pairs)

        def memset_consts(e):
            last = None
            for j, v in enumerate(CONSTS):
                last = e.memset(cpool[:, j : j + 1], float(v))
            return last
        op("vector", memset_consts, inc="dve")

        dma("sync", maskd_sb[:], maskd)
        dma("sync", wqb_sb[:], wq_b)
        dma("sync", wkb_sb[:], wk_b)
        dma("sync", b1_sb[:], b1_t)
        dma("sync", b2_sb[:], b2_t)
        dma("sync", w1stage[0][:, 0:128], identf)
        dma("sync", w1stage[1][:, 0:256], wdq)
        dma("sync", w1stage[1][:, 256:512], wuqt)
        dma("sync", ln_a[0][:, 0:512], wdkv)
        dma("sync", ln_a[1][:, 0:1024], wukvt)
        dma("sync", ln_b[0][:, 0:128], wqrt_p)
        dma("sync", ln_b[0][:, 128:256], wkrt_e)
        dma("sync", ln_b[0][:, 320:448], wkrt_o)
        ld_w = dma("sync", ln_b[0][0:64, 256:320], wot)[1]

        def cast_weights(e):
            e.tensor_copy(out=ident_bf[:], in_=w1stage[0][:, 0:128])
            e.tensor_copy(out=ident_h[:], in_=w1stage[0][:, 0:128])
            e.tensor_copy(out=wdq_bf[:], in_=w1stage[1][:, 0:256])
            e.tensor_copy(out=wuqt_bf[:], in_=w1stage[1][:, 256:512])
            e.tensor_copy(out=wdkv_bf[:], in_=ln_a[0][:, 0:512])
            e.tensor_copy(out=wukvt_bf[:], in_=ln_a[1][:, 0:1024])
            e.tensor_copy(out=wqrt_bf[:], in_=ln_b[0][:, 0:128])
            e.tensor_copy(out=wkrt_bf[0][:], in_=ln_b[0][:, 128:256])
            e.tensor_copy(out=wkrt_bf[1][:], in_=ln_b[0][:, 320:448])
            return e.tensor_copy(out=wot_bf[:], in_=ln_b[0][0:64, 256:320])
        c_cast = op("vector", cast_weights, waits=W(("ld", ld_w)), inc="dve")

        def weff(e):
            for c in range(4):
                e.matmul(out=psum_m[0][0:64, 0:64], lhsT=wdq_bf[:, c * 64 : c * 64 + 64],
                         rhs=wuqt_bf[:, c * 64 : c * 64 + 64], start=(c == 0), stop=(c == 3))
            last = None
            for c in range(8):
                last = e.matmul(out=psum_m[0][0:64, 64:192],
                                lhsT=wdkv_bf[:, c * 64 : c * 64 + 64],
                                rhs=wukvt_bf[:, c * 128 : c * 128 + 128],
                                start=(c == 0), stop=(c == 7))
            return last
        c_weff = op("tensor", weff, waits=W(("dve", c_cast)), inc="pe")

        weff_f32 = ln_b[1]
        def weff_evac(e):
            e.memset(wqeff[0][:], 0.0)
            e.memset(wqeff[1][:], 0.0)
            e.memset(wkveff[0][:], 0.0)
            e.memset(wkveff[1][:], 0.0)
            return e.tensor_copy(out=weff_f32[0:64, 0:192], in_=psum_m[0][0:64, 0:192])
        c_we1 = op("vector", weff_evac, waits=W(("pe", c_weff)), inc="dve")
        c_we2 = dma("sync", weff_scr, weff_f32[0:64, 0:192], waits=W(("dve", c_we1)))
        c_we3 = dma("sync", weff_f32[64:128, 0:192], weff_scr, waits=W(c_we2))
        def weff_cast(e):
            e.tensor_copy(out=wqeff[0][0:64, :], in_=weff_f32[0:64, 0:64])
            e.tensor_copy(out=wkveff[0][0:64, :], in_=weff_f32[0:64, 64:192])
            e.tensor_copy(out=wqeff[1][64:128, :], in_=weff_f32[64:128, 0:64])
            return e.tensor_copy(out=wkveff[1][64:128, :], in_=weff_f32[64:128, 64:192])
        c_weffe = op("vector", weff_cast, waits=W(c_we3), inc="dve")

        QSTATE = {"dve": 0, "gps": 0}

        def quant_exact(src_ap, dst_ap, n, man, clamp, signed, first_waits=()):
            qls, qis = ql[:, 0:n], qi[:, 0:n]
            qscs, qinvs = qsc[:, 0:n], qinv[:, 0:n]
            qvs, qcs = qv[:, 0:n], qc[:, 0:n]
            qes = qls
            qd = qvs
            fw = list(first_waits) + [("dve", QSTATE["dve"]), ("gps", QSTATE["gps"])]
            if signed:
                a1 = op("scalar", lambda e, sp=src_ap, o=qd: e.activation(
                    out=o, in_=sp, func=AF.Square), waits=fw, inc="act")
                a2 = op("scalar", lambda e, o=qls, i=qd: e.activation(
                    out=o, in_=i, func=AF.Ln, bias=TINY, scale=1.0),
                    waits=W(("act", a1)), inc="act")
                mulc, backc = INVLN2 / 2.0, 2.0 * LN2
            else:
                a2 = op("scalar", lambda e, sp=src_ap, o=qls: e.activation(
                    out=o, in_=sp, func=AF.Ln, bias=TINY, scale=1.0), waits=fw, inc="act")
                mulc, backc = INVLN2, LN2
            geng, gsem = ("vector", "dve") if DBG_NOGPS else ("gpsimd", "gps")
            g1 = op(geng, lambda e, o=qis, i=qls, m=mulc: e.tensor_scalar(
                out=o, in0=i, scalar1=m, scalar2=0.0, op0=ALU.mult, op1=ALU.add),
                waits=W(("act", a2), ("dve", QSTATE["dve"])), inc=gsem)
            d1 = op("vector", lambda e, o=qcs, i0=qis, i1=qls, b=backc: e.scalar_tensor_tensor(
                out=o, in0=i0, scalar=b, in1=i1, op0=ALU.mult, op1=ALU.is_gt),
                waits=W((gsem, g1)), inc="dve")
            d2 = op("vector", lambda e, o=qes, i0=qcs, i1=qis: e.scalar_tensor_tensor(
                out=o, in0=i0, scalar=-1.0, in1=i1, op0=ALU.mult, op1=ALU.add),
                inc="dve")
            if clamp is not None:
                d2 = op(geng, lambda e, o=qes, c=float(clamp): e.tensor_scalar(
                    out=o, in0=o, scalar1=c, scalar2=0.0, op0=ALU.max, op1=ALU.add),
                    waits=W(("dve", d2)), inc=gsem)
                exp_waits = W((gsem, d2))
            else:
                exp_waits = W(("dve", d2))
            bias_sc = -float(man) * LN2
            a3 = op("scalar", lambda e, o=qinvs, i=qes, b=-bias_sc: e.activation(
                out=o, in_=i, func=AF.Exp, bias=b, scale=-LN2), waits=exp_waits, inc="act")
            a4 = op("scalar", lambda e, o=qscs, i=qes, b=bias_sc: e.activation(
                out=o, in_=i, func=AF.Exp, bias=b, scale=LN2), inc="act")
            d3 = op("vector", lambda e, o=qvs, i0=src_ap, i1=qinvs: e.tensor_tensor(
                out=o, in0=i0, in1=i1, op=ALU.mult), waits=W(("act", a3)), inc="dve")
            g4 = op(geng, lambda e, o=qis, i=qvs: e.tensor_scalar(
                out=o, in0=i, scalar1=1.0, scalar2=0.0, op0=ALU.mult, op1=ALU.add),
                waits=W(("dve", d3)), inc=gsem)
            d5 = op("vector", lambda e, o=qcs, i0=qis, i1=qvs: e.scalar_tensor_tensor(
                out=o, in0=i0, scalar=0.5, in1=i1, op0=ALU.add, op1=ALU.is_le),
                waits=W((gsem, g4)), inc="dve")
            if signed:
                op("vector", lambda e, o=qes, i0=qis, i1=qvs: e.scalar_tensor_tensor(
                    out=o, in0=i0, scalar=-0.5, in1=i1, op0=ALU.add, op1=ALU.is_ge),
                    waits=W(("act", a4)), inc="dve")
                op("vector", lambda e, o=qcs, i0=qcs, i1=qes: e.tensor_tensor(
                    out=o, in0=i0, in1=i1, op=ALU.subtract), inc="dve")
            d6 = op("vector", lambda e, o=qvs, i0=qis, i1=qcs: e.tensor_tensor(
                out=o, in0=i0, in1=i1, op=ALU.add), inc="dve")
            d7 = op("vector", lambda e, o=dst_ap, i0=qvs, i1=qscs: e.tensor_tensor(
                out=o, in0=i0, in1=i1, op=ALU.mult), waits=W(("act", a4)), inc="dve")
            QSTATE["dve"] = d7
            QSTATE["gps"] = 0 if DBG_NOGPS else g4
            return d7

        last_q = c_weffe
        stage_free = [c_weffe, c_weffe]
        for m in range(32):
            buf = m % 2
            lw = dma("sync", w1stage[buf][:], w2t[m * 128 : (m + 1) * 128, :],
                     waits=W(("dve", stage_free[buf])), sem="lw%d" % buf)
            for hh in range(2):
                last_q = quant_exact(
                    w1stage[buf][:, hh * 512 : hh * 512 + 512],
                    w2qT[:, m * 1024 + hh * 512 : m * 1024 + hh * 512 + 512],
                    512, 3, -6.0, True, first_waits=W(lw))
            stage_free[buf] = last_q
        c_bh = op("vector", lambda e: e.tensor_copy(out=b2h[:], in_=b2_sb[:]),
                  waits=W(("ld", ld_w), ("dve", last_q)), inc="dve")
        b2q_done = op("vector", lambda e: e.tensor_copy(out=b2q_sb[:], in_=b2h[:]),
                      inc="dve")

        lna_free = [b2q_done, b2q_done]
        lnb_free_st = [("sb0", 0), ("sb1", 0)]
        lnb_free_d = [c_cast, c_cast]
        hwork_free = [0, 0]
        pm2_free = {}
        pS2_free = [0, 0]
        ybf_free = 0
        featbuf_pe_free = 0
        spill_cnt = {}

        for s in range(DBG_SEQ):
            last_cast = 0
            for half in range(16):
                c, hh = divmod(half, 2)
                buf = half % 2
                lx = dma("sync", ln_a[buf][:, 0:1024],
                         x_feat[s, c * 128 : c * 128 + 128, hh * 1024 : hh * 1024 + 1024],
                         waits=W(("dve", lna_free[buf])), sem="la%d" % buf)
                cc = op("vector", lambda e,
                        o=featbuf[:, c * 2048 + hh * 1024 : c * 2048 + hh * 1024 + 1024],
                        i=ln_a[buf][:, 0:1024]: e.tensor_copy(out=o, in_=i),
                        waits=W(lx, ("pe", featbuf_pe_free)), inc="dve")
                lna_free[buf] = cc
                last_cast = cc
            featbuf_ready = last_cast

            pm_free = [{"dve": cnt["dve"], "act": cnt["act"]} for _ in range(2)]
            pctx_free = {"dve": cnt["dve"], "act": cnt["act"]}
            pS_free = [{"act": cnt["act"]}, {"act": cnt["act"]}]
            attn_done = 0
            if not DBG_ATTN:
                attn_done = cnt["dve"]

            for h in range(DBG_HEADS if DBG_ATTN else 0):
                chn, bse = divmod(h, 2)
                bse *= 64
                xcol = chn * 2048

                def ones_fn(e):
                    last = None
                    for t in range(NT):
                        last = e.memset(v_aug[:, t * 66 + 64 : t * 66 + 65], 1.0)
                    return last
                ones_c = op("vector", ones_fn, waits=W(("pe", cnt["pe"])), inc="dve")

                d1_cnts, a1_cnts = [cnt["dve"]], [cnt["act"]]
                qb_free = [{"dve": 0, "act": 0}, {"dve": 0, "act": 0}]
                for g in range(DBG_G):
                    pb = g % 2
                    def p1(e, g=g, pb=pb, par=h % 2, xcol=xcol):
                        last = None
                        for t in range(4 * g, 4 * g + 4):
                            sl = featbuf[:, xcol + t * 128 : xcol + t * 128 + 128]
                            e.matmul(out=psum_m[pb][:, (t % 4) * 128 : (t % 4) * 128 + 128],
                                     lhsT=sl, rhs=wkveff[par][:], start=True, stop=True)
                            last = e.matmul(
                                out=psum_ctx[:, (g % 2) * 512 + (t % 4) * 64 : (g % 2) * 512 + (t % 4) * 64 + 64],
                                lhsT=sl, rhs=wqeff[par][:], start=True, stop=True)
                        return last
                    c_p1 = op("tensor", p1,
                              waits=W(("dve", featbuf_ready),
                                      ("dve", pm_free[pb]["dve"]), ("act", pm_free[pb]["act"]),
                                      ("dve", pctx_free["dve"]), ("act", pctx_free["act"]),
                                      ("dve", qb_free[pb]["dve"]), ("act", qb_free[pb]["act"]),
                                      ("dve", ones_c)),
                              inc="pe")
                    def d1(e, g=g, pb=pb):
                        last = None
                        for t in range(4 * g, 4 * g + 4):
                            j = t % 4
                            if DBG_D1SEL & 1:
                                last = e.tensor_copy(out=v_aug[:, t * 66 : t * 66 + 64],
                                              in_=psum_m[pb][:, j * 128 + 64 : j * 128 + 128])
                            if DBG_D1SEL & 2:
                                last = e.tensor_copy(out=kc_bf[:, t * 64 : t * 64 + 64],
                                              in_=psum_m[pb][:, j * 128 : j * 128 + 64])
                            if DBG_D1SEL & 4:
                                last = e.tensor_copy(
                                    out=qc_bf[:, t * 64 : t * 64 + 64],
                                    in_=psum_ctx[:, (g % 2) * 512 + j * 64 : (g % 2) * 512 + j * 64 + 64])
                        return last
                    if DBG_NOD1:
                        c_d1 = cnt["dve"]
                    else:
                        c_d1 = op("vector", d1, waits=W(("pe", c_p1)), inc="dve")
                    def a1(e, g=g, pb=pb):
                        last = None
                        for t in range(4 * g, 4 * g + 4):
                            j = t % 4
                            e.activation(out=sq_scr[:, (2 * j) * 64 : (2 * j) * 64 + 64],
                                         in_=psum_m[pb][:, j * 128 : j * 128 + 64],
                                         func=AF.Square, accum_out=svec[:, t : t + 1])
                            last = e.activation(
                                out=sq_scr[:, (2 * j + 1) * 64 : (2 * j + 1) * 64 + 64],
                                in_=psum_ctx[:, (g % 2) * 512 + j * 64 : (g % 2) * 512 + j * 64 + 64],
                                func=AF.Square, accum_out=svec[:, 16 + t : 17 + t])
                        return last
                    if DBG_NOA1:
                        c_a1 = cnt["act"]
                    else:
                        c_a1 = op("scalar", a1, waits=W(("pe", c_p1), ("dve", c_d1)), inc="act")
                    pm_free[pb] = {"dve": c_d1, "act": c_a1}
                    qb_free[pb] = {"dve": c_d1, "act": c_a1}
                    d1_cnts.append(c_d1)
                    a1_cnts.append(c_a1)
                pctx_free = {"dve": d1_cnts[-1], "act": a1_cnts[-1]}

                if DBG_HSTAGE < 2:
                    attn_done = max(attn_done, d1_cnts[-1])
                    continue
                def a2a(e):
                    e.activation(out=svec[:, 96:112], in_=svec[:, 0:16], func=AF.Ln,
                                 bias=EPS_ATTN, scale=1.0 / 64.0)
                    return e.activation(out=svec[:, 112:128], in_=svec[:, 16:32], func=AF.Ln,
                                        bias=EPS_ATTN, scale=1.0 / 64.0)
                op("scalar", a2a, waits=W(("dve", d1_cnts[-1])), inc="act")
                def a2b(e):
                    e.activation(out=svec[:, 32:48], in_=svec[:, 96:112], func=AF.Exp,
                                 bias=0.0, scale=-0.5)
                    return e.activation(out=svec[:, 48:64], in_=svec[:, 112:128], func=AF.Exp,
                                        bias=0.0, scale=-0.5)
                c_a2 = op("scalar", a2b, inc="act")

                def d2(e):
                    last = None
                    for t in range(16):
                        e.scalar_tensor_tensor(out=kc_bf[:, t * 64 : t * 64 + 64],
                                               in0=kc_bf[:, t * 64 : t * 64 + 64],
                                               scalar=svec[:, 32 + t : 33 + t],
                                               in1=wkb_sb[:], op0=ALU.mult, op1=ALU.mult)
                        last = e.scalar_tensor_tensor(out=qc_bf[:, t * 64 : t * 64 + 64],
                                                      in0=qc_bf[:, t * 64 : t * 64 + 64],
                                                      scalar=svec[:, 48 + t : 49 + t],
                                                      in1=wqb_sb[:], op0=ALU.mult, op1=ALU.mult)
                    return last
                c_d2 = op("vector", d2, waits=W(("act", c_a2)), inc="dve")

                if DBG_HSTAGE < 3:
                    attn_done = max(attn_done, c_d2)
                    continue
                tr_done = 0
                for src, dstT in ((kc_bf, kT_bf), (qc_bf, qT_bf)):
                    for g in range(4):
                        pb = g % 2
                        def p2(e, g=g, pb=pb, src=src):
                            last = None
                            for t in range(4 * g, 4 * g + 4):
                                last = e.matmul(
                                    out=psum_m[pb][0:64, (t % 4) * 128 : (t % 4) * 128 + 128],
                                    lhsT=src[:, t * 64 : t * 64 + 64],
                                    rhs=ident_bf[:], start=True, stop=True)
                            return last
                        c_p2 = op("tensor", p2,
                                  waits=W(("dve", c_d2), ("dve", pm_free[pb]["dve"]),
                                          ("act", pm_free[pb]["act"])),
                                  inc="pe")
                        def d3(e, g=g, pb=pb, dstT=dstT):
                            last = None
                            for t in range(4 * g, 4 * g + 4):
                                last = e.tensor_copy(
                                    out=dstT[0:64, t * 128 : t * 128 + 128],
                                    in_=psum_m[pb][0:64, (t % 4) * 128 : (t % 4) * 128 + 128])
                            return last
                        c_d3 = op("vector", d3, waits=W(("pe", c_p2)), inc="dve")
                        pm_free[pb] = {"dve": c_d3, "act": pm_free[pb]["act"]}
                        tr_done = c_d3

                if DBG_HSTAGE < 4:
                    attn_done = max(attn_done, tr_done)
                    continue
                kr_done = 0
                for j in range(4):
                    pb = j % 2
                    c_p3 = op("tensor", lambda e, j=j, pb=pb, par=h % 2, xcol=xcol: e.matmul(
                        out=psum_m[pb][:, 0:512], lhsT=wkrt_bf[par][:],
                        rhs=featbuf[:, xcol + j * 512 : xcol + j * 512 + 512],
                        start=True, stop=True),
                        waits=W(("dve", pm_free[pb]["dve"]), ("act", pm_free[pb]["act"])),
                        inc="pe")
                    c_d4 = op("vector", lambda e, j=j, pb=pb: e.tensor_copy(
                        out=kT_bf[64:128, j * 512 : j * 512 + 512],
                        in_=psum_m[pb][64:128, 0:512]),
                        waits=W(("pe", c_p3)), inc="dve")
                    pm_free[pb] = {"dve": c_d4, "act": pm_free[pb]["act"]}
                    kr_done = c_d4
                qr_done = 0
                for j in range(4):
                    pb = j % 2
                    c_p4 = op("tensor", lambda e, j=j, pb=pb: e.matmul(
                        out=psum_m[pb][:, 0:512], lhsT=wqrt_bf[0:64, :],
                        rhs=qT_bf[0:64, j * 512 : j * 512 + 512], start=True, stop=True),
                        waits=W(("dve", pm_free[pb]["dve"]), ("act", pm_free[pb]["act"]),
                                ("dve", tr_done)),
                        inc="pe")
                    c_d5 = op("vector", lambda e, j=j, pb=pb: e.tensor_copy(
                        out=qT_bf[64:128, j * 512 : j * 512 + 512],
                        in_=psum_m[pb][64:128, 0:512]),
                        waits=W(("pe", c_p4)), inc="dve")
                    pm_free[pb] = {"dve": c_d5, "act": pm_free[pb]["act"]}
                    qr_done = c_d5

                if DBG_HSTAGE < 5:
                    attn_done = max(attn_done, qr_done)
                    continue
                last_ed1 = 0
                for Hf in range(2):
                    jmax = 8 if Hf == 0 else 16
                    for j in range(jmax):
                        qlo = max(128 * j, 1024 * Hf)
                        off = qlo - 1024 * Hf
                        jb = j % 2
                        segs = []
                        p = off
                        while p < 1024:
                            q = 512 if p < 512 else 1024
                            segs.append((p, q))
                            p = q
                        def pS_fn(e, j=j, jb=jb, segs=tuple(segs), Hf=Hf):
                            last = None
                            for (p, q) in segs:
                                last = e.matmul(out=psum_S[jb][:, p:q],
                                                lhsT=kT_bf[:, j * 128 : j * 128 + 128],
                                                rhs=qT_bf[:, 1024 * Hf + p : 1024 * Hf + q],
                                                start=True, stop=True)
                            return last
                        c_ps = op("tensor", pS_fn,
                                  waits=W(("dve", kr_done), ("dve", qr_done),
                                          ("act", pS_free[jb]["act"]),
                                          ("dve", pS2_free[jb])),
                                  inc="pe")
                        has_diag = (Hf == 0) or (j >= 8)
                        c_dm = 0
                        if has_diag:
                            c_dm = op("vector", lambda e, jb=jb, off=off: e.tensor_tensor(
                                out=psum_S[jb][:, off : off + 128],
                                in0=psum_S[jb][:, off : off + 128],
                                in1=maskd_sb[:], op=ALU.add),
                                waits=W(("pe", c_ps)), inc="dve")
                        ae_w = W(("pe", c_ps))
                        if has_diag:
                            ae_w = ae_w + W(("dve", c_dm))
                        c_ae = op("scalar", lambda e, jb=jb, off=off: e.activation(
                            out=PT[jb][:, off:1024], in_=psum_S[jb][:, off:1024],
                            func=AF.Exp, bias=-CSH, scale=0.125),
                            waits=ae_w, inc="act")
                        pS_free[jb] = {"act": c_ae}
                        def pC_fn(e, j=j, jb=jb, segs=tuple(segs), jmax=jmax):
                            last = None
                            for (p, q) in segs:
                                last = e.matmul(out=psum_ctx[0:65, p:q],
                                                lhsT=v_aug[:, j * 66 : j * 66 + 65],
                                                rhs=PT[jb][:, p:q],
                                                start=(j == 0), stop=(j == jmax - 1),
                                                skip_group_check=True)
                            return last
                        waits_pc = W(("act", c_ae), ("dve", last_ed1))
                        if j == 0:
                            waits_pc += W(("dve", pctx_free["dve"]), ("act", pctx_free["act"]))
                        c_pc = op("tensor", pC_fn, waits=waits_pc, inc="pe")

                        t_sub = None
                        if Hf == 0 or j >= 8:
                            t_sub = j
                        if t_sub is not None:
                            eb = t_sub % 2
                            def ed1(e, off=off):
                                return e.tensor_copy(out=ctxTa_bf[0:65, :],
                                                     in_=psum_ctx[0:65, off : off + 128])
                            c_ed1 = op("vector", ed1, waits=W(("pe", c_pc)), inc="dve")
                            last_ed1 = c_ed1
                            def ep(e, eb=eb):
                                e.matmul(out=psum_m[eb][:, 0:1],
                                         lhsT=ctxTa_bf[0:65, :],
                                         rhs=ident_bf[0:65, 64:65], start=True, stop=True)
                                return e.matmul(out=psum_m[eb][:, 64:128],
                                                lhsT=ctxTa_bf[0:64, :], rhs=wot_bf[:],
                                                start=True, stop=True)
                            c_ep = op("tensor", ep,
                                      waits=W(("dve", c_ed1), ("dve", pm_free[eb]["dve"]),
                                              ("act", pm_free[eb]["act"])),
                                      inc="pe")
                            op("vector", lambda e, eb=eb: e.reciprocal(
                                out=svec[:, 64:65], in_=psum_m[eb][:, 0:1]),
                                waits=W(("pe", c_ep)), inc="dve")
                            c_ed2 = op("vector", lambda e, eb=eb, t_g=t_sub, h=h:
                                       e.tensor_single_scalar(
                                    out=attn_bf[:, t_g * 1024 + h * 64 : t_g * 1024 + h * 64 + 64],
                                    in_=psum_m[eb][:, 64:128], scalar=svec[:, 64:65], op=ALU.mult),
                                    inc="dve")
                            pm_free[eb] = {"dve": c_ed2, "act": pm_free[eb]["act"]}
                            attn_done = c_ed2
                    pctx_free = {"dve": cnt["dve"], "act": cnt["act"]}

            if TAP == "attn":
                for t in range(NT):
                    buf = t % 2
                    cc = op("vector", lambda e, t=t, buf=buf: e.tensor_copy(
                        out=ln_b[buf][:], in_=attn_bf[:, t * 1024 : t * 1024 + 1024]),
                        waits=W(("dve", attn_done), lnb_free_st[buf],
                                ("dve", lnb_free_d[buf])),
                        inc="dve")
                    lnb_free_d[buf] = cc
                    lnb_free_st[buf] = dma("scalar",
                                           tap_out[s, t * 128 : t * 128 + 128, :],
                                           ln_b[buf][:], waits=W(("dve", cc)),
                                           sem="sb%d" % buf)

            yT_written = 0
            for t in range(NT if DBG_LN1 else 0):
                buf = t % 2
                lx = dma("sync", ln_a[buf][:], x_tok[s, t * 128 : t * 128 + 128, :],
                         waits=W(("dve", lna_free[buf])), sem="la%d" % buf)
                c_add = op("vector", lambda e, buf=buf, t=t: e.tensor_tensor(
                    out=ln_b[buf][:], in0=ln_a[buf][:],
                    in1=attn_bf[:, t * 1024 : t * 1024 + 1024], op=ALU.add),
                    waits=W(lx, ("dve", attn_done), lnb_free_st[buf],
                            ("dve", lnb_free_d[buf])),
                    inc="dve")
                lna_free[buf] = c_add
                op("vector", lambda e, buf=buf: e.reduce_sum(
                    out=svec[:, 80:81], in_=ln_b[buf][:], axis=mybir.AxisListType.X),
                    inc="dve")
                op("vector", lambda e, buf=buf: e.scalar_tensor_tensor(
                    out=ybf_tile[:], in0=ln_b[buf][:], scalar=1.0, in1=ln_b[buf][:],
                    op0=ALU.mult, op1=ALU.mult, accum_out=svec[:, 81:82]),
                    waits=W(("pe", ybf_free)), inc="dve")
                def stats_a(e):
                    e.tensor_scalar(out=svec[:, 82:83], in0=svec[:, 80:81], scalar1=1.0 / D,
                                    scalar2=0.0, op0=ALU.mult, op1=ALU.add)
                    return e.tensor_scalar(out=svec[:, 83:84], in0=svec[:, 81:82], scalar1=1.0 / D,
                                           scalar2=0.0, op0=ALU.mult, op1=ALU.add)
                op("vector", stats_a, inc="dve")
                op("vector", lambda e: e.tensor_tensor(out=svec[:, 84:85], in0=svec[:, 82:83],
                                                       in1=svec[:, 82:83], op=ALU.mult),
                   inc="dve")
                c_stats = op("vector", lambda e: e.tensor_tensor(
                    out=svec[:, 85:86], in0=svec[:, 83:84], in1=svec[:, 84:85], op=ALU.subtract),
                    inc="dve")
                c_r1 = op("scalar", lambda e: e.activation(
                    out=svec[:, 86:87], in_=svec[:, 85:86], func=AF.Ln, bias=EPS_LN, scale=1.0),
                    waits=W(("dve", c_stats)), inc="act")
                c_rstd = op("scalar", lambda e: e.activation(
                    out=svec[:, 87:88], in_=svec[:, 86:87], func=AF.Exp, bias=0.0, scale=-0.5),
                    inc="act")
                c_apply = op("vector", lambda e, buf=buf: e.tensor_scalar(
                    out=ln_b[buf][:], in0=ln_b[buf][:], scalar1=svec[:, 82:83],
                    scalar2=svec[:, 87:88], op0=ALU.subtract, op1=ALU.mult),
                    waits=W(("act", c_rstd)), inc="dve")
                if TAP == "y":
                    c_sp = dma("scalar", tap_out[s, t * 128 : t * 128 + 128, :],
                               ln_b[buf][:], waits=W(("dve", c_apply)), sem="sb%d" % buf)
                else:
                    c_sp = dma("scalar", y_spill[t * 128 : t * 128 + 128, :],
                               ln_b[buf][:], waits=W(("dve", c_apply)), sem="sb%d" % buf)
                spill_cnt[t] = c_sp
                c_ybf = op("vector", lambda e, buf=buf: e.tensor_copy(
                    out=ybf_tile[:], in_=ln_b[buf][:]),
                    waits=W(("dve", c_apply), ("pe", ybf_free)), inc="dve")
                lnb_free_st[buf] = c_sp
                lnb_free_d[buf] = c_ybf
                for grp in range(2):
                    pb = grp
                    def ptr_fn(e, grp=grp, pb=pb):
                        last = None
                        for dc in range(4 * grp, 4 * grp + 4):
                            last = e.matmul(
                                out=psum_m[pb][:, (dc % 4) * 128 : (dc % 4) * 128 + 128],
                                lhsT=ybf_tile[:, dc * 128 : dc * 128 + 128],
                                rhs=ident_bf[:], start=True, stop=True)
                        return last
                    c_ptr = op("tensor", ptr_fn,
                               waits=W(("dve", c_ybf), ("dve", pm_free[pb]["dve"]),
                                       ("act", pm_free[pb]["act"])),
                               inc="pe")
                    def dtr(e, grp=grp, pb=pb, t=t):
                        last = None
                        for dc in range(4 * grp, 4 * grp + 4):
                            last = e.tensor_copy(
                                out=featbuf[:, dc * 2048 + t * 128 : dc * 2048 + t * 128 + 128],
                                in_=psum_m[pb][:, (dc % 4) * 128 : (dc % 4) * 128 + 128])
                        return last
                    c_dtr = op("vector", dtr, waits=W(("pe", c_ptr)), inc="dve")
                    pm_free[pb] = {"dve": c_dtr, "act": pm_free[pb]["act"]}
                    yT_written = c_dtr
                ybf_free = cnt["pe"]

            w1s_free = [cnt["dve"], cnt["dve"]]
            hq_free = 0
            for th in range(2 if DBG_FFN else 0):
                thc = th * 1024
                mm1_done = 0
                for m in range(32):
                    wbuf = m % 2
                    lw = dma("sync", w1stage[wbuf][:], w1tl[m],
                             waits=W(("dve", w1s_free[wbuf])), sem="lw%d" % wbuf)
                    c_cw = op("vector", lambda e, wbuf=wbuf: e.tensor_copy(
                        out=w1bf[wbuf][:], in_=w1stage[wbuf][:]),
                        waits=W(lw), inc="dve")
                    w1s_free[wbuf] = c_cw
                    for Tc in range(2):
                        pb = Tc
                        def pmm1(e, wbuf=wbuf, pb=pb, Tc=Tc, thc=thc):
                            last = None
                            for dcc in range(8):
                                last = e.matmul(
                                    out=psum_m[pb][:],
                                    lhsT=w1bf[wbuf][:, dcc * 128 : dcc * 128 + 128],
                                    rhs=featbuf[:, dcc * 2048 + thc + Tc * 512 :
                                                dcc * 2048 + thc + Tc * 512 + 512],
                                    start=(dcc == 0), stop=(dcc == 7))
                            return last
                        c_mm1 = op("tensor", pmm1,
                                   waits=W(("dve", c_cw), ("dve", yT_written),
                                           ("dve", pm_free[pb]["dve"]),
                                           ("act", pm_free[pb]["act"]),
                                           ("pe", hq_free)),
                                   inc="pe")
                        c_rel = op("scalar", lambda e, pb=pb, m=m: e.activation(
                            out=hwork[pb][:], in_=psum_m[pb][:], func=AF.Relu,
                            bias=b1_sb[:, m : m + 1], scale=1.0),
                            waits=W(("pe", c_mm1), ("dve", hwork_free[pb])),
                            inc="act")
                        c_hq = quant_exact(
                            hwork[pb][:],
                            hqT[:, m * 1024 + Tc * 512 : m * 1024 + Tc * 512 + 512],
                            512, 3, -6.0, False, first_waits=W(("act", c_rel)))
                        hwork_free[pb] = c_hq
                        pm_free[pb] = {"dve": c_hq, "act": c_rel}
                        mm1_done = c_hq
                for Tc2 in range(4):
                    oq_last = 0
                    for dout in range(8):
                        ob = (Tc2 * 8 + dout) % 4
                        pb = ob % 2
                        pslice = [psum_m[0][:, 0:256], psum_m[1][:, 0:256],
                                  psum_ctx[:, 0:256], psum_ctx[:, 512:768]][ob]
                        def pmm2(e, pslice=pslice, dout=dout, Tc2=Tc2):
                            last = None
                            for m in range(32):
                                last = e.matmul(
                                    out=pslice,
                                    lhsT=w2qT[:, m * 1024 + dout * 128 : m * 1024 + dout * 128 + 128],
                                    rhs=hqT[:, m * 1024 + Tc2 * 256 : m * 1024 + Tc2 * 256 + 256],
                                    start=(m == 0), stop=(m == 31))
                            return last
                        c_mm2 = op("tensor", pmm2,
                                   waits=W(("dve", mm1_done),
                                           ("dve", pm_free[pb]["dve"]),
                                           ("act", pm_free[pb]["act"]),
                                           ("dve", pm2_free.get(ob, 0))),
                                   inc="pe")
                        wbuf2 = dout % 2
                        c_ev = op("vector", lambda e, pslice=pslice, dout=dout, wbuf2=wbuf2:
                                  e.tensor_scalar(out=hwork[wbuf2][:, 0:256], in0=pslice,
                                                  scalar1=b2q_sb[:, dout : dout + 1],
                                                  scalar2=0.0, op0=ALU.add, op1=ALU.add),
                                  waits=W(("pe", c_mm2), ("dve", hwork_free[wbuf2])),
                                  inc="dve")
                        hwork_free[wbuf2] = c_ev
                        c_oq = op("vector", lambda e, wbuf2=wbuf2, dout=dout: e.tensor_copy(
                            out=oq_all[:, dout * 256 : dout * 256 + 256],
                            in_=hwork[wbuf2][:, 0:256]), inc="dve")
                        hwork_free[wbuf2] = c_oq
                        pm2_free[ob] = c_oq
                        pm_free[pb] = {"dve": c_oq, "act": pm_free[pb]["act"]}
                        oq_last = c_oq
                    for tk in range(2):
                        tb = th * 8 + Tc2 * 2 + tk
                        def ptr2(e, tk=tk):
                            last = None
                            for dout in range(8):
                                last = e.matmul(
                                    out=psum_S[tk][:, dout * 128 : dout * 128 + 128],
                                    lhsT=oq_all[:, dout * 256 + tk * 128 : dout * 256 + tk * 128 + 128],
                                    rhs=ident_h[:], start=True, stop=True)
                            return last
                        c_tr2 = op("tensor", ptr2,
                                   waits=W(("dve", oq_last), ("dve", pS2_free[tk]),
                                           ("act", pS_free[tk]["act"])),
                                   inc="pe")
                        ly = dma("scalar", ln_a[tk][:], y_spill[tb * 128 : tb * 128 + 128, :],
                                 waits=W(("dve", lna_free[tk]), spill_cnt[tb]),
                                 sem="la%d" % tk)
                        c_s2 = op("vector", lambda e, tk=tk: e.tensor_tensor(
                            out=ln_b[tk][:], in0=psum_S[tk][:, 0:1024], in1=ln_a[tk][:],
                            op=ALU.add),
                            waits=W(("pe", c_tr2), ly, lnb_free_st[tk],
                                    ("dve", lnb_free_d[tk])),
                            inc="dve")
                        lna_free[tk] = c_s2
                        op("vector", lambda e, tk=tk: e.reduce_sum(
                            out=svec[:, 88:89], in_=ln_b[tk][:], axis=mybir.AxisListType.X),
                            inc="dve")
                        c_sq2 = op("vector", lambda e, tk=tk: e.scalar_tensor_tensor(
                            out=ybf_tile[:], in0=ln_b[tk][:], scalar=1.0, in1=ln_b[tk][:],
                            op0=ALU.mult, op1=ALU.mult, accum_out=svec[:, 89:90]),
                            waits=W(("pe", ybf_free)), inc="dve")
                        def stats2_a(e):
                            e.tensor_scalar(out=svec[:, 90:91], in0=svec[:, 88:89],
                                            scalar1=1.0 / D, scalar2=0.0, op0=ALU.mult, op1=ALU.add)
                            return e.tensor_scalar(out=svec[:, 91:92], in0=svec[:, 89:90],
                                                   scalar1=1.0 / D, scalar2=0.0, op0=ALU.mult, op1=ALU.add)
                        op("vector", stats2_a, inc="dve")
                        op("vector", lambda e: e.tensor_tensor(out=svec[:, 92:93], in0=svec[:, 90:91],
                                                               in1=svec[:, 90:91], op=ALU.mult),
                           inc="dve")
                        c_sts = op("vector", lambda e: e.tensor_tensor(
                            out=svec[:, 93:94], in0=svec[:, 91:92], in1=svec[:, 92:93],
                            op=ALU.subtract), inc="dve")
                        op("scalar", lambda e: e.activation(
                            out=svec[:, 94:95], in_=svec[:, 93:94], func=AF.Ln,
                            bias=EPS_LN, scale=1.0), waits=W(("dve", c_sts)), inc="act")
                        c_rs2 = op("scalar", lambda e: e.activation(
                            out=svec[:, 95:96], in_=svec[:, 94:95], func=AF.Exp,
                            bias=0.0, scale=-0.5), inc="act")
                        c_ap2 = op("vector", lambda e, tk=tk: e.tensor_scalar(
                            out=ln_b[tk][:], in0=ln_b[tk][:], scalar1=svec[:, 90:91],
                            scalar2=svec[:, 95:96], op0=ALU.subtract, op1=ALU.mult),
                            waits=W(("act", c_rs2)), inc="dve")
                        pS2_free[tk] = c_ap2
                        lnb_free_d[tk] = c_ap2
                        lnb_free_st[tk] = dma("scalar",
                                              out_tok[s, tb * 128 : tb * 128 + 128, :],
                                              ln_b[tk][:], waits=W(("dve", c_ap2)),
                                              sem="sb%d" % tk)
                hq_free = cnt["pe"]
            featbuf_pe_free = cnt["pe"]

        op("sync", lambda e: e.nop(),
           waits=W(("sb0", cnt["sb0"]), ("sb1", cnt["sb1"])))

        ENGS = ["sync", "scalar", "vector", "tensor", "gpsimd"]
        emitted_waits = {e: {} for e in ENGS}

        OWN_SEM = {"vector": "dve", "scalar": "act", "gpsimd": "gps"}

        def replay(eng_name):
            ew = emitted_waits[eng_name]
            own = OWN_SEM.get(eng_name)
            def body(eng):
                for (en, fn, waits, inc, amt) in OPS:
                    if en != eng_name:
                        continue
                    allw = list(waits)
                    if own is not None and inc == own:
                        allw.append((own, cnt_at[id(fn)]))
                    for (sname, val) in allw:
                        if val > ew.get(sname, 0):
                            eng.wait_ge(sems[sname], val)
                            ew[sname] = val
                    instr = fn(eng)
                    if inc is not None:
                        assert instr is not None, f"op on {eng_name} returned no instruction"
                        instr.then_inc(sems[inc], amt, skip_validation=(amt == 16))
            return body

        for en in ENGS:
            if any(o[0] == en for o in OPS):
                getattr(block, en)(replay(en))

    return nc


def _host_inputs(inputs, core):
    x = np.asarray(inputs["L0_attn_x"])
    xs = np.ascontiguousarray(x[2 * core : 2 * core + 2])
    xf = np.ascontiguousarray(xs.transpose(0, 2, 1))
    W_DQ = np.asarray(inputs["L0_attn_W_DQ"]); W_UQ = np.asarray(inputs["L0_attn_W_UQ"])
    W_DKV = np.asarray(inputs["L0_attn_W_DKV"]); W_UKV = np.asarray(inputs["L0_attn_W_UKV"])
    W_QR = np.asarray(inputs["L0_attn_W_QR"]); W_KR = np.asarray(inputs["L0_attn_W_KR"])
    W_O = np.asarray(inputs["L0_attn_W_O"])
    qnw = np.asarray(inputs["L0_attn_q_norm_w"]); knw = np.asarray(inputs["L0_attn_k_norm_w"])
    w1 = np.asarray(inputs["L0_ffn_w1"]); b1 = np.asarray(inputs["L0_ffn_b1"])
    w2 = np.asarray(inputs["L0_ffn_w2"]); b2 = np.asarray(inputs["L0_ffn_b2"])

    tile_pc = lambda a, P=128: np.ascontiguousarray(
        a.reshape(a.shape[0] // P, P, a.shape[1]).transpose(1, 0, 2).reshape(P, -1))
    def pad_rt(wt):
        half = np.concatenate([np.zeros((64, 64), np.float32),
                               np.ascontiguousarray(wt.T)], axis=1)
        return np.concatenate([half, half], axis=0)

    def par_rt(wt, par):
        out = np.zeros((128, 128), np.float32)
        out[par * 64 : par * 64 + 64, 64:128] = np.ascontiguousarray(wt.T)
        return out
    d = {
        "x_tok": xs,
        "x_feat": xf,
        "wdq": tile_pc(W_DQ),
        "wuqt": tile_pc(np.ascontiguousarray(W_UQ.T)),
        "wdkv": tile_pc(W_DKV),
        "wukvt": tile_pc(np.ascontiguousarray(W_UKV.T)),
        "wqrt_p": pad_rt(W_QR),
        "wkrt_e": par_rt(W_KR, 0),
        "wkrt_o": par_rt(W_KR, 1),
        "wot": np.ascontiguousarray(W_O.T),
        "wq_b": np.ascontiguousarray(np.broadcast_to(qnw, (128, 64))),
        "wk_b": np.ascontiguousarray(np.broadcast_to(knw, (128, 64))),
        "w1tl": np.ascontiguousarray(
            w1.T.reshape(8, 128, 32, 128).transpose(2, 1, 0, 3).reshape(32, 128, 1024)),
        "w2t": np.ascontiguousarray(w2.T),
        "b1_t": np.ascontiguousarray(b1.reshape(32, 128).T),
        "b2_t": np.ascontiguousarray(b2.reshape(8, 128).T),
        "identf": np.eye(128, dtype=np.float32),
    }
    d = {k: np.ascontiguousarray(np.asarray(v, dtype=np.float32)) for k, v in d.items()}
    import ml_dtypes
    d["maskd"] = np.ascontiguousarray(
        np.where(np.arange(128)[None, :] >= np.arange(128)[:, None],
                 np.float32(0.0), np.float32(-1e30)).astype(ml_dtypes.bfloat16))
    return d


def kernel(**inputs):
    nc = build_program()
    in_maps = [_host_inputs(inputs, c) for c in range(8)]
    res = run_bass_kernel_spmd(nc, in_maps, core_ids=list(range(8)))
    out = np.concatenate([r["out_tok"] for r in res.results], axis=0)
    if TAP is not None:
        kernel.tap = np.concatenate([r["tap"] for r in res.results], axis=0)
    return out


# version 03
# speedup vs baseline: 1.1052x; 1.1052x over previous
import contextlib
import numpy as np
import concourse.bass as bass
import concourse.mybir as mybir
from concourse.bass_utils import run_bass_kernel_spmd

F32 = mybir.dt.float32
BF16 = mybir.dt.bfloat16
F16 = mybir.dt.float16
F8 = mybir.dt.float8e4
I32 = mybir.dt.int32
AF = mybir.ActivationFunctionType
ALU = mybir.AluOpType

LN2 = 0.6931471805599453
INVLN2 = 1.4426950408889634
EPS_ATTN = 1.1920929e-07
EPS_LN = 1e-5
TINY = 1e-38
CSH = 20.0
T = 2048
D = 1024
NT = 16
NSEQ = 2

TAP = None
import os
DBG_HEADS = int(os.environ.get("MK_HEADS", "16"))
DBG_SEQ = int(os.environ.get("MK_SEQ", str(NSEQ)))
DBG_FFN = os.environ.get("MK_FFN", "1") == "1"
DBG_LN1 = os.environ.get("MK_LN1", "1") == "1"
DBG_ATTN = os.environ.get("MK_ATTN", "1") == "1"
DBG_HSTAGE = int(os.environ.get("MK_HSTAGE", "9"))
DBG_G = int(os.environ.get("MK_G", "4"))
DBG_NOA1 = os.environ.get("MK_NOA1", "0") == "1"
DBG_NOD1 = os.environ.get("MK_NOD1", "0") == "1"
DBG_D1SEL = int(os.environ.get("MK_D1SEL", "7"))
DBG_NOGPS = os.environ.get("MK_NOGPS", "0") == "1"
DBG_SERIAL = os.environ.get("MK_SERIAL", "0") != "0"


def build_program():
    nc = bass.Bass("TRN2", target_bir_lowering=False, debug=False)

    x_tok = nc.dram_tensor("x_tok", [NSEQ, T, D], F32, kind="ExternalInput").ap()
    x_feat = nc.dram_tensor("x_feat", [NSEQ, D, T], F32, kind="ExternalInput").ap()
    wdq = nc.dram_tensor("wdq", [128, 256], F32, kind="ExternalInput").ap()
    wuqt = nc.dram_tensor("wuqt", [128, 256], F32, kind="ExternalInput").ap()
    wdkv = nc.dram_tensor("wdkv", [128, 512], F32, kind="ExternalInput").ap()
    wukvt = nc.dram_tensor("wukvt", [128, 1024], F32, kind="ExternalInput").ap()
    wqrt_p = nc.dram_tensor("wqrt_p", [128, 128], F32, kind="ExternalInput").ap()
    wkrt_e = nc.dram_tensor("wkrt_e", [128, 128], F32, kind="ExternalInput").ap()
    wkrt_o = nc.dram_tensor("wkrt_o", [128, 128], F32, kind="ExternalInput").ap()
    wot = nc.dram_tensor("wot", [64, 64], F32, kind="ExternalInput").ap()
    wq_b = nc.dram_tensor("wq_b", [128, 64], F32, kind="ExternalInput").ap()
    wk_b = nc.dram_tensor("wk_b", [128, 64], F32, kind="ExternalInput").ap()
    w1tl = nc.dram_tensor("w1tl", [32, 128, 1024], F32, kind="ExternalInput").ap()
    w2t = nc.dram_tensor("w2t", [4096, D], F32, kind="ExternalInput").ap()
    b1_t = nc.dram_tensor("b1_t", [128, 32], F32, kind="ExternalInput").ap()
    b2_t = nc.dram_tensor("b2_t", [128, 8], F32, kind="ExternalInput").ap()
    identf = nc.dram_tensor("identf", [128, 128], F32, kind="ExternalInput").ap()
    maskd = nc.dram_tensor("maskd", [128, 128], BF16, kind="ExternalInput").ap()
    maskdT = nc.dram_tensor("maskdT", [128, 128], BF16, kind="ExternalInput").ap()
    out_tok = nc.dram_tensor("out_tok", [NSEQ, T, D], F32, kind="ExternalOutput").ap()
    y_spill = nc.dram_tensor("y_spill", [T, D], F32, kind="Internal").ap()
    weff_scr = nc.dram_tensor("weff_scr", [64, 192], F32, kind="Internal").ap()
    tap_out = None
    if TAP is not None:
        tap_out = nc.dram_tensor("tap", [NSEQ, T, D], F32, kind="ExternalOutput").ap()

    ctx = contextlib.ExitStack()
    with ctx:
        sb = lambda name, shape, dt: ctx.enter_context(nc.sbuf_tensor(name, shape, dt))
        psb = lambda name, shape: ctx.enter_context(nc.psum_tensor(name, shape, F32))

        cpool = sb("cpool", [128, 16], F32)
        ident_bf = sb("ident_bf", [128, 128], BF16)
        ident_h = sb("ident_h", [128, 128], F16)
        maskd_sb = sb("maskd_sb", [128, 128], BF16)
        maskdT_sb = sb("maskdT_sb", [128, 128], BF16)
        wqb_sb = sb("wqb_sb", [128, 64], F32)
        wkb_sb = sb("wkb_sb", [128, 64], F32)
        wdq_bf = sb("wdq_bf", [128, 256], BF16)
        wuqt_bf = sb("wuqt_bf", [128, 256], BF16)
        wdkv_bf = sb("wdkv_bf", [128, 512], BF16)
        wukvt_bf = sb("wukvt_bf", [128, 1024], BF16)
        wqrt_bf = sb("wqrt_bf", [128, 128], BF16)
        wkrt_bf = [sb("wkrt_bf0", [128, 128], BF16), sb("wkrt_bf1", [128, 128], BF16)]
        wot_bf = sb("wot_bf", [64, 64], BF16)
        wqkv = [sb("wqkv0", [128, 192], BF16), sb("wqkv1", [128, 192], BF16)]
        b1_sb = sb("b1_sb", [128, 32], F32)
        b2_sb = sb("b2_sb", [128, 8], F32)
        b2q_sb = sb("b2q_sb", [128, 8], F32)

        featbuf = sb("featbuf", [128, 8 * T], BF16)
        attn_bf = sb("attn_bf", [128, NT * D], BF16)
        w2qT = sb("w2qT", [128, 32 * D], F8)
        hqT = sb("hqT", [128, 32 * 512], F8)

        kT_bf = sb("kT_bf", [128, T], BF16)
        qT_bf = sb("qT_bf", [128, T], BF16)
        v_aug = sb("v_aug", [128, NT * 66], BF16)
        kc_bf = sb("kc_bf", [128, NT * 64], BF16)
        qc_bf = sb("qc_bf", [128, NT * 64], BF16)
        svec = sb("svec", [128, 128], F32)
        PT = [sb("PT0", [128, 1024], BF16), sb("PT1", [128, 1024], BF16)]
        ctxTa_bf = sb("ctxTa_bf", [128, 128], BF16)
        qsq = sb("qsq", [128, 1024], BF16)

        ln_a = [sb("ln_a0", [128, D], F32), sb("ln_a1", [128, D], F32)]
        ln_b = [sb("ln_b0", [128, D], F32), sb("ln_b1", [128, D], F32)]
        ybf_tile = sb("ybf_tile", [128, D], BF16)
        w1stage = [sb("w1stage0", [128, 1024], F32), sb("w1stage1", [128, 1024], F32)]
        w1bf = [sb("w1bf0", [128, 1024], BF16), sb("w1bf1", [128, 1024], BF16)]
        hwork = [sb("hwork0", [128, 512], F32), sb("hwork1", [128, 512], F32)]
        ql = [sb("ql0", [128, 512], F32), sb("ql1", [128, 512], F32)]
        qi = [sb("qi0", [128, 512], I32), sb("qi1", [128, 512], I32)]
        qsc = [sb("qsc0", [128, 512], F32), sb("qsc1", [128, 512], F32)]
        qinv = [sb("qinv0", [128, 512], F32), sb("qinv1", [128, 512], F32)]
        qv = [sb("qv0", [128, 512], F32), sb("qv1", [128, 512], F32)]
        qc = [sb("qc0", [128, 512], F32), sb("qc1", [128, 512], F32)]
        b2h = sb("b2h", [128, 8], F16)
        oq_all = sb("oq_all", [128, 2048], F16)

        psum_S = [psb("psum_S0", [128, 1024]), psb("psum_S1", [128, 1024])]
        psum_ctx = psb("psum_ctx", [128, 1024])
        psum_m = [psb("psum_m0", [128, 512]), psb("psum_m1", [128, 512])]

        CONSTS = sorted({TINY, -3 * LN2, 3 * LN2, LN2, -LN2, EPS_ATTN,
                         EPS_LN, 0.125, -CSH, -0.5, -1.0, 1.0 / 64.0, 0.5, 1.0, 0.0})
        for j, v in enumerate(CONSTS):
            nc.const_aps.register(F32, float(v), cpool[:, j : j + 1])

        sems = {}
        for s in ["ld", "lw0", "lw1", "la0", "la1", "sb0", "sb1",
                  "pe", "act", "dve", "gps"]:
            sems[s] = ctx.enter_context(nc.semaphore("sem_" + s))
        cnt = {s: 0 for s in sems}

        block = ctx.enter_context(nc.Block())
        OPS = []

        cnt_at = {}

        def op(eng, fn, waits=(), inc=None, amt=1):
            waits = [(sn, v) for (sn, v) in waits if v > 0]
            if DBG_SERIAL:
                mode = os.environ.get("MK_SERIAL", "0")
                DMAS = ("ld", "lw0", "lw1", "la0", "la1", "sb0", "sb1")
                if mode == "1":
                    waits = [(sn, v) for (sn, v) in cnt.items() if v > 0]
                elif mode == "2":
                    if eng in ("sync", "scalar") and fn.__name__ == "<lambda>" and inc in DMAS:
                        waits = waits + [(sn, cnt[sn]) for sn in ("pe", "act", "dve", "gps") if cnt[sn] > 0]
                    else:
                        waits = waits + [(sn, cnt[sn]) for sn in DMAS if cnt[sn] > 0]
                elif mode == "3":
                    if eng == "tensor":
                        waits = waits + [(sn, cnt[sn]) for sn in ("act", "dve", "gps") if cnt[sn] > 0]
                    elif eng in ("vector", "scalar", "gpsimd"):
                        waits = waits + [("pe", cnt["pe"])] if cnt["pe"] > 0 else waits
                elif mode == "4":
                    if eng == "vector" and cnt["act"] > 0:
                        waits = waits + [("act", cnt["act"])]
                    elif eng == "scalar" and inc == "act" and cnt["dve"] > 0:
                        waits = waits + [("dve", cnt["dve"])]
            OPS.append((eng, fn, waits, inc, amt))
            if inc is not None:
                cnt_at[id(fn)] = cnt[inc]
                cnt[inc] += amt
                return cnt[inc]
            return None

        def dma(queue, out_ap, in_ap, waits=(), sem="ld"):
            c = op(queue, lambda e, o=out_ap, i=in_ap: e.dma_start(out=o, in_=i),
                   waits=waits, inc=sem, amt=16)
            return (sem, c)

        W = lambda *pairs: list(pairs)

        def memset_consts(e):
            last = None
            for j, v in enumerate(CONSTS):
                last = e.memset(cpool[:, j : j + 1], float(v))
            return last
        op("vector", memset_consts, inc="dve")

        dma("sync", maskd_sb[:], maskd)
        dma("sync", maskdT_sb[:], maskdT)
        dma("sync", wqb_sb[:], wq_b)
        dma("sync", wkb_sb[:], wk_b)
        dma("sync", b1_sb[:], b1_t)
        dma("sync", b2_sb[:], b2_t)
        dma("sync", w1stage[0][:, 0:128], identf)
        dma("sync", w1stage[1][:, 0:256], wdq)
        dma("sync", w1stage[1][:, 256:512], wuqt)
        dma("sync", ln_a[0][:, 0:512], wdkv)
        dma("sync", ln_a[1][:, 0:1024], wukvt)
        dma("sync", ln_b[0][:, 0:128], wqrt_p)
        dma("sync", ln_b[0][:, 128:256], wkrt_e)
        dma("sync", ln_b[0][:, 320:448], wkrt_o)
        ld_w = dma("sync", ln_b[0][0:64, 256:320], wot)[1]

        def cast_weights(e):
            e.tensor_copy(out=ident_bf[:], in_=w1stage[0][:, 0:128])
            e.tensor_copy(out=ident_h[:], in_=w1stage[0][:, 0:128])
            e.tensor_copy(out=wdq_bf[:], in_=w1stage[1][:, 0:256])
            e.tensor_copy(out=wuqt_bf[:], in_=w1stage[1][:, 256:512])
            e.tensor_copy(out=wdkv_bf[:], in_=ln_a[0][:, 0:512])
            e.tensor_copy(out=wukvt_bf[:], in_=ln_a[1][:, 0:1024])
            e.tensor_copy(out=wqrt_bf[:], in_=ln_b[0][:, 0:128])
            e.tensor_copy(out=wkrt_bf[0][:], in_=ln_b[0][:, 128:256])
            e.tensor_copy(out=wkrt_bf[1][:], in_=ln_b[0][:, 320:448])
            return e.tensor_copy(out=wot_bf[:], in_=ln_b[0][0:64, 256:320])
        c_cast = op("vector", cast_weights, waits=W(("ld", ld_w)), inc="dve")

        def weff(e):
            for c in range(4):
                e.matmul(out=psum_m[0][0:64, 0:64], lhsT=wdq_bf[:, c * 64 : c * 64 + 64],
                         rhs=wuqt_bf[:, c * 64 : c * 64 + 64], start=(c == 0), stop=(c == 3))
            last = None
            for c in range(8):
                last = e.matmul(out=psum_m[0][0:64, 64:192],
                                lhsT=wdkv_bf[:, c * 64 : c * 64 + 64],
                                rhs=wukvt_bf[:, c * 128 : c * 128 + 128],
                                start=(c == 0), stop=(c == 7))
            return last
        c_weff = op("tensor", weff, waits=W(("dve", c_cast)), inc="pe")

        weff_f32 = ln_b[1]
        def weff_evac(e):
            e.memset(wqkv[0][:], 0.0)
            e.memset(wqkv[1][:], 0.0)
            return e.tensor_copy(out=weff_f32[0:64, 0:192], in_=psum_m[0][0:64, 0:192])
        c_we1 = op("vector", weff_evac, waits=W(("pe", c_weff)), inc="dve")
        c_we2 = dma("sync", weff_scr, weff_f32[0:64, 0:192], waits=W(("dve", c_we1)))
        c_we3 = dma("sync", weff_f32[64:128, 0:192], weff_scr, waits=W(c_we2))
        def weff_cast(e):
            e.tensor_copy(out=wqkv[0][0:64, 128:192], in_=weff_f32[0:64, 0:64])
            e.tensor_copy(out=wqkv[0][0:64, 0:128], in_=weff_f32[0:64, 64:192])
            e.tensor_copy(out=wqkv[1][64:128, 128:192], in_=weff_f32[64:128, 0:64])
            return e.tensor_copy(out=wqkv[1][64:128, 0:128], in_=weff_f32[64:128, 64:192])
        c_weffe = op("vector", weff_cast, waits=W(c_we3), inc="dve")

        QSTATE = [{"dve": 0, "gps": 0, "act": 0}, {"dve": 0, "gps": 0, "act": 0}]
        QPAR = [0]

        def quant_exact(src_ap, dst_ap, n, man, clamp, signed, first_waits=()):
            par = QPAR[0]
            QPAR[0] ^= 1
            QS = QSTATE[par]
            qls, qis = ql[par][:, 0:n], qi[par][:, 0:n]
            qscs, qinvs = qsc[par][:, 0:n], qinv[par][:, 0:n]
            qvs, qcs = qv[par][:, 0:n], qc[par][:, 0:n]
            qes = qls
            qd = qvs
            fw = list(first_waits) + [("dve", QS["dve"]), ("gps", QS["gps"]),
                                      ("act", QS["act"])]
            if signed:
                a1 = op("scalar", lambda e, sp=src_ap, o=qd: e.activation(
                    out=o, in_=sp, func=AF.Square), waits=fw, inc="act")
                a2 = op("scalar", lambda e, o=qls, i=qd: e.activation(
                    out=o, in_=i, func=AF.Ln, bias=TINY, scale=1.0),
                    waits=W(("act", a1)), inc="act")
                mulc, backc = INVLN2 / 2.0, 2.0 * LN2
            else:
                a2 = op("scalar", lambda e, sp=src_ap, o=qls: e.activation(
                    out=o, in_=sp, func=AF.Ln, bias=TINY, scale=1.0), waits=fw, inc="act")
                mulc, backc = INVLN2, LN2
            geng, gsem = ("vector", "dve") if DBG_NOGPS else ("gpsimd", "gps")
            g1 = op(geng, lambda e, o=qis, i=qls, m=mulc: e.tensor_scalar(
                out=o, in0=i, scalar1=m, scalar2=0.0, op0=ALU.mult, op1=ALU.add),
                waits=W(("act", a2), ("dve", QS["dve"])), inc=gsem)
            d1 = op("vector", lambda e, o=qcs, i0=qis, i1=qls, b=backc: e.scalar_tensor_tensor(
                out=o, in0=i0, scalar=b, in1=i1, op0=ALU.mult, op1=ALU.is_gt),
                waits=W((gsem, g1)), inc="dve")
            d2 = op("vector", lambda e, o=qes, i0=qcs, i1=qis: e.scalar_tensor_tensor(
                out=o, in0=i0, scalar=-1.0, in1=i1, op0=ALU.mult, op1=ALU.add),
                inc="dve")
            if clamp is not None:
                d2 = op(geng, lambda e, o=qes, c=float(clamp): e.tensor_scalar(
                    out=o, in0=o, scalar1=c, scalar2=0.0, op0=ALU.max, op1=ALU.add),
                    waits=W(("dve", d2)), inc=gsem)
                exp_waits = W((gsem, d2))
            else:
                exp_waits = W(("dve", d2))
            bias_sc = -float(man) * LN2
            a3 = op("scalar", lambda e, o=qinvs, i=qes, b=-bias_sc: e.activation(
                out=o, in_=i, func=AF.Exp, bias=b, scale=-LN2), waits=exp_waits, inc="act")
            a4 = op("scalar", lambda e, o=qscs, i=qes, b=bias_sc: e.activation(
                out=o, in_=i, func=AF.Exp, bias=b, scale=LN2), inc="act")
            d3 = op("vector", lambda e, o=qvs, i0=src_ap, i1=qinvs: e.tensor_tensor(
                out=o, in0=i0, in1=i1, op=ALU.mult), waits=W(("act", a3)), inc="dve")
            g4 = op(geng, lambda e, o=qis, i=qvs: e.tensor_scalar(
                out=o, in0=i, scalar1=1.0, scalar2=0.0, op0=ALU.mult, op1=ALU.add),
                waits=W(("dve", d3)), inc=gsem)
            d5 = op("vector", lambda e, o=qcs, i0=qis, i1=qvs: e.scalar_tensor_tensor(
                out=o, in0=i0, scalar=0.5, in1=i1, op0=ALU.add, op1=ALU.is_le),
                waits=W((gsem, g4)), inc="dve")
            if signed:
                op("vector", lambda e, o=qes, i0=qis, i1=qvs: e.scalar_tensor_tensor(
                    out=o, in0=i0, scalar=-0.5, in1=i1, op0=ALU.add, op1=ALU.is_ge),
                    waits=W(("act", a4)), inc="dve")
                op("vector", lambda e, o=qcs, i0=qcs, i1=qes: e.tensor_tensor(
                    out=o, in0=i0, in1=i1, op=ALU.subtract), inc="dve")
            d6 = op("vector", lambda e, o=qvs, i0=qis, i1=qcs: e.tensor_tensor(
                out=o, in0=i0, in1=i1, op=ALU.add), inc="dve")
            d7 = op("vector", lambda e, o=dst_ap, i0=qvs, i1=qscs: e.tensor_tensor(
                out=o, in0=i0, in1=i1, op=ALU.mult), waits=W(("act", a4)), inc="dve")
            QS["dve"] = d7
            QS["gps"] = 0 if DBG_NOGPS else g4
            QS["act"] = a4
            return d7

        stage_free = [c_weffe, c_weffe]
        w2_lw = {}
        c_bh = op("vector", lambda e: e.tensor_copy(out=b2h[:], in_=b2_sb[:]),
                  waits=W(("ld", ld_w)), inc="dve")
        b2q_done = op("vector", lambda e: e.tensor_copy(out=b2q_sb[:], in_=b2h[:]),
                      inc="dve")

        lna_free = [c_cast, c_cast]
        lnb_free_st = [("sb0", 0), ("sb1", 0)]
        lnb_free_d = [c_cast, c_cast]
        hwork_free = [0, 0]
        pm2_free = {}
        pS2_free = [0, 0]
        ybf_free = 0
        featbuf_pe_free = 0
        spill_cnt = {}

        for s in range(DBG_SEQ):
            last_cast = 0
            for half in range(16):
                c, hh = divmod(half, 2)
                buf = half % 2
                lx = dma("sync", ln_a[buf][:, 0:1024],
                         x_feat[s, c * 128 : c * 128 + 128, hh * 1024 : hh * 1024 + 1024],
                         waits=W(("dve", lna_free[buf])), sem="la%d" % buf)
                cc = op("vector", lambda e,
                        o=featbuf[:, c * 2048 + hh * 1024 : c * 2048 + hh * 1024 + 1024],
                        i=ln_a[buf][:, 0:1024]: e.tensor_copy(out=o, in_=i),
                        waits=W(lx, ("pe", featbuf_pe_free)), inc="dve")
                lna_free[buf] = cc
                last_cast = cc
            featbuf_ready = last_cast

            pm_free = [{"dve": cnt["dve"], "act": cnt["act"]} for _ in range(2)]
            pctx_free = {"dve": cnt["dve"], "act": cnt["act"]}
            pS_free = [{"act": cnt["act"]}, {"act": cnt["act"]}]
            attn_done = 0
            if not DBG_ATTN:
                attn_done = cnt["dve"]

            def make_ctx_extract(h, pm_free, pctx_free_ref):
                def emit_ctx_extract(Hf, j, jb, segs, c_ae, jmax, last_ed1):
                    def pC_fn(e, j=j, jb=jb, segs=segs, jmax=jmax):
                        last = None
                        for (p, q) in segs:
                            last = e.matmul(out=psum_ctx[0:65, p:q],
                                            lhsT=v_aug[:, j * 66 : j * 66 + 65],
                                            rhs=PT[jb][:, p:q],
                                            start=(j == 0), stop=(j == jmax - 1),
                                            skip_group_check=True)
                        return last
                    waits_pc = W(("act", c_ae), ("dve", last_ed1))
                    if j == 0:
                        waits_pc += W(("dve", pctx_free_ref[0]["dve"]),
                                      ("act", pctx_free_ref[0]["act"]))
                    c_pc = op("tensor", pC_fn, waits=waits_pc, inc="pe")
                    t_sub = j if (Hf == 0 or j >= 8) else None
                    if t_sub is None:
                        return last_ed1
                    eb = t_sub % 2
                    off = 128 * t_sub - 1024 * Hf
                    def ed1(e, off=off):
                        return e.tensor_copy(out=ctxTa_bf[0:65, :],
                                             in_=psum_ctx[0:65, off : off + 128])
                    c_ed1 = op("vector", ed1, waits=W(("pe", c_pc)), inc="dve")
                    def ep(e, eb=eb):
                        e.matmul(out=psum_m[eb][:, 0:1],
                                 lhsT=ctxTa_bf[0:65, :],
                                 rhs=ident_bf[0:65, 64:65], start=True, stop=True)
                        return e.matmul(out=psum_m[eb][:, 64:128],
                                        lhsT=ctxTa_bf[0:64, :], rhs=wot_bf[:],
                                        start=True, stop=True)
                    c_ep = op("tensor", ep,
                              waits=W(("dve", c_ed1), ("dve", pm_free[eb]["dve"]),
                                      ("act", pm_free[eb]["act"])),
                              inc="pe")
                    op("vector", lambda e, eb=eb: e.reciprocal(
                        out=svec[:, 64:65], in_=psum_m[eb][:, 0:1]),
                        waits=W(("pe", c_ep)), inc="dve")
                    c_ed2 = op("vector", lambda e, eb=eb, t_g=t_sub, h=h:
                               e.tensor_single_scalar(
                        out=attn_bf[:, t_g * 1024 + h * 64 : t_g * 1024 + h * 64 + 64],
                        in_=psum_m[eb][:, 64:128], scalar=svec[:, 64:65], op=ALU.mult),
                        inc="dve")
                    pm_free[eb] = {"dve": c_ed2, "act": pm_free[eb]["act"]}
                    nonlocal_state["attn_done"] = c_ed2
                    return c_ed1
                return emit_ctx_extract
            nonlocal_state = {"attn_done": 0}

            for h in range(DBG_HEADS if DBG_ATTN else 0):
                if s == 0:
                    if h >= 1:
                        for m in (2 * h - 2, 2 * h - 1):
                            buf = m % 2
                            stage_free[buf] = quant_exact(
                                w1stage[buf][:, 0:512],
                                w2qT[:, m * 1024 : m * 1024 + 512],
                                512, 3, -6.0, True, first_waits=W(w2_lw[m]))
                            stage_free[buf] = quant_exact(
                                w1stage[buf][:, 512:1024],
                                w2qT[:, m * 1024 + 512 : m * 1024 + 1024],
                                512, 3, -6.0, True, first_waits=W(w2_lw[m]))
                    for m in (2 * h, 2 * h + 1):
                        buf = m % 2
                        w2_lw[m] = dma("sync", w1stage[buf][:],
                                       w2t[m * 128 : (m + 1) * 128, :],
                                       waits=W(("dve", stage_free[buf])),
                                       sem="lw%d" % buf)
                chn, bse = divmod(h, 2)
                bse *= 64
                xcol = chn * 2048

                def ones_fn(e):
                    last = None
                    for t in range(NT):
                        last = e.memset(v_aug[:, t * 66 + 64 : t * 66 + 65], 1.0)
                    return last
                ones_c = op("vector", ones_fn, waits=W(("pe", cnt["pe"])), inc="dve")

                def pslot(g, j4):
                    if g % 2 == 0:
                        return psum_ctx[:, j4 * 256 : j4 * 256 + 192]
                    return psum_m[j4 // 2][:, (j4 % 2) * 256 : (j4 % 2) * 256 + 192]
                slot_free = [pctx_free["dve"], max(pm_free[0]["dve"], pm_free[1]["dve"])]
                slot_free_act = [pctx_free["act"], max(pm_free[0]["act"], pm_free[1]["act"])]
                d1_cnts = [cnt["dve"]]
                for g in range(DBG_G):
                    def p1(e, g=g, par=h % 2, xcol=xcol):
                        last = None
                        for t in range(4 * g, 4 * g + 4):
                            last = e.matmul(out=pslot(g, t % 4),
                                            lhsT=featbuf[:, xcol + t * 128 : xcol + t * 128 + 128],
                                            rhs=wqkv[par][:], start=True, stop=True)
                        return last
                    c_p1 = op("tensor", p1,
                              waits=W(("dve", featbuf_ready),
                                      ("dve", slot_free[g % 2]),
                                      ("act", slot_free_act[g % 2]),
                                      ("dve", ones_c)),
                              inc="pe")
                    def d1(e, g=g):
                        base = pslot(g, 0)
                        t0c = 4 * g
                        st = base.tensor
                        if g % 2 == 0:
                            src3 = lambda off: psum_ctx[:, 0:1024].rearrange(
                                "p (s c) -> p s c", s=4)[:, :, off : off + 64]
                        else:
                            src3 = None
                        if g % 2 == 0:
                            e.tensor_copy(out=kc_bf[:, t0c * 64 : t0c * 64 + 256],
                                          in_=src3(0))
                            e.tensor_copy(
                                out=v_aug[:].rearrange("p (s c) -> p s c", c=66)[
                                    :, t0c : t0c + 4, 0:64],
                                in_=src3(64))
                            return e.tensor_copy(out=qc_bf[:, t0c * 64 : t0c * 64 + 256],
                                                 in_=src3(128))
                        else:
                            last = None
                            for half in range(2):
                                pm = psum_m[half][:, 0:512].rearrange(
                                    "p (s c) -> p s c", s=2)
                                tb = t0c + 2 * half
                                e.tensor_copy(out=kc_bf[:, tb * 64 : tb * 64 + 128],
                                              in_=pm[:, :, 0:64])
                                e.tensor_copy(
                                    out=v_aug[:].rearrange("p (s c) -> p s c", c=66)[
                                        :, tb : tb + 2, 0:64],
                                    in_=pm[:, :, 64:128])
                                last = e.tensor_copy(out=qc_bf[:, tb * 64 : tb * 64 + 128],
                                                     in_=pm[:, :, 128:192])
                            return last
                    c_d1 = op("vector", d1, waits=W(("pe", c_p1)), inc="dve")
                    slot_free[g % 2] = c_d1
                    d1_cnts.append(c_d1)
                pctx_free = {"dve": slot_free[0], "act": cnt["act"]}
                for pb in range(2):
                    pm_free[pb] = {"dve": slot_free[1], "act": pm_free[pb]["act"]}

                c_sqk = op("scalar", lambda e: e.activation(
                    out=qsq[:], in_=kc_bf[:], func=AF.Square),
                    waits=W(("dve", d1_cnts[-1])), inc="act")
                c_rdk = op("vector", lambda e: e.tensor_reduce(
                    out=svec[:, 0:16],
                    in_=qsq[:].rearrange("p (s c) -> p s c", c=64),
                    axis=mybir.AxisListType.X, op=ALU.add),
                    waits=W(("act", c_sqk)), inc="dve")
                c_sqq = op("scalar", lambda e: e.activation(
                    out=qsq[:], in_=qc_bf[:], func=AF.Square),
                    waits=W(("dve", c_rdk)), inc="act")
                c_rdq = op("vector", lambda e: e.tensor_reduce(
                    out=svec[:, 16:32],
                    in_=qsq[:].rearrange("p (s c) -> p s c", c=64),
                    axis=mybir.AxisListType.X, op=ALU.add),
                    waits=W(("act", c_sqq)), inc="dve")
                a1_cnts = [cnt["act"]]
                d1_cnts.append(c_rdq)

                def a2a(e):
                    e.activation(out=svec[:, 96:112], in_=svec[:, 0:16], func=AF.Ln,
                                 bias=EPS_ATTN, scale=1.0 / 64.0)
                    return e.activation(out=svec[:, 112:128], in_=svec[:, 16:32], func=AF.Ln,
                                        bias=EPS_ATTN, scale=1.0 / 64.0)
                op("scalar", a2a, waits=W(("dve", d1_cnts[-1])), inc="act")
                def a2b(e):
                    e.activation(out=svec[:, 32:48], in_=svec[:, 96:112], func=AF.Exp,
                                 bias=0.0, scale=-0.5)
                    return e.activation(out=svec[:, 48:64], in_=svec[:, 112:128], func=AF.Exp,
                                        bias=0.0, scale=-0.5)
                c_a2 = op("scalar", a2b, inc="act")

                def d2(e):
                    last = None
                    for t in range(16):
                        e.scalar_tensor_tensor(out=kc_bf[:, t * 64 : t * 64 + 64],
                                               in0=kc_bf[:, t * 64 : t * 64 + 64],
                                               scalar=svec[:, 32 + t : 33 + t],
                                               in1=wkb_sb[:], op0=ALU.mult, op1=ALU.mult)
                        last = e.scalar_tensor_tensor(out=qc_bf[:, t * 64 : t * 64 + 64],
                                                      in0=qc_bf[:, t * 64 : t * 64 + 64],
                                                      scalar=svec[:, 48 + t : 49 + t],
                                                      in1=wqb_sb[:], op0=ALU.mult, op1=ALU.mult)
                    return last
                c_d2 = op("vector", d2, waits=W(("act", c_a2)), inc="dve")

                if DBG_HSTAGE < 3:
                    attn_done = max(attn_done, c_d2)
                    continue
                tr_done = 0
                for src, dstT in ((kc_bf, kT_bf), (qc_bf, qT_bf)):
                    for g in range(4):
                        pb = g % 2
                        def p2(e, g=g, pb=pb, src=src):
                            last = None
                            for t in range(4 * g, 4 * g + 4):
                                last = e.matmul(
                                    out=psum_m[pb][0:64, (t % 4) * 128 : (t % 4) * 128 + 128],
                                    lhsT=src[:, t * 64 : t * 64 + 64],
                                    rhs=ident_bf[:], start=True, stop=True)
                            return last
                        c_p2 = op("tensor", p2,
                                  waits=W(("dve", c_d2), ("dve", pm_free[pb]["dve"]),
                                          ("act", pm_free[pb]["act"])),
                                  inc="pe")
                        def d3(e, g=g, pb=pb, dstT=dstT):
                            return e.tensor_copy(
                                out=dstT[0:64, 4 * g * 128 : 4 * g * 128 + 512],
                                in_=psum_m[pb][0:64, 0:512])
                        c_d3 = op("vector", d3, waits=W(("pe", c_p2)), inc="dve")
                        pm_free[pb] = {"dve": c_d3, "act": pm_free[pb]["act"]}
                        tr_done = c_d3

                if DBG_HSTAGE < 4:
                    attn_done = max(attn_done, tr_done)
                    continue
                kr_done = 0
                for j in range(4):
                    pb = j % 2
                    c_p3 = op("tensor", lambda e, j=j, pb=pb, par=h % 2, xcol=xcol: e.matmul(
                        out=psum_m[pb][:, 0:512], lhsT=wkrt_bf[par][:],
                        rhs=featbuf[:, xcol + j * 512 : xcol + j * 512 + 512],
                        start=True, stop=True),
                        waits=W(("dve", pm_free[pb]["dve"]), ("act", pm_free[pb]["act"])),
                        inc="pe")
                    c_d4 = op("vector", lambda e, j=j, pb=pb: e.tensor_copy(
                        out=kT_bf[64:128, j * 512 : j * 512 + 512],
                        in_=psum_m[pb][64:128, 0:512]),
                        waits=W(("pe", c_p3)), inc="dve")
                    pm_free[pb] = {"dve": c_d4, "act": pm_free[pb]["act"]}
                    kr_done = c_d4
                qr_done = 0
                for j in range(4):
                    pb = j % 2
                    c_p4 = op("tensor", lambda e, j=j, pb=pb: e.matmul(
                        out=psum_m[pb][:, 0:512], lhsT=wqrt_bf[0:64, :],
                        rhs=qT_bf[0:64, j * 512 : j * 512 + 512], start=True, stop=True),
                        waits=W(("dve", pm_free[pb]["dve"]), ("act", pm_free[pb]["act"]),
                                ("dve", tr_done)),
                        inc="pe")
                    c_d5 = op("vector", lambda e, j=j, pb=pb: e.tensor_copy(
                        out=qT_bf[64:128, j * 512 : j * 512 + 512],
                        in_=psum_m[pb][64:128, 0:512]),
                        waits=W(("pe", c_p4)), inc="dve")
                    pm_free[pb] = {"dve": c_d5, "act": pm_free[pb]["act"]}
                    qr_done = c_d5

                if DBG_HSTAGE < 5:
                    attn_done = max(attn_done, qr_done)
                    continue
                pctx_free_ref = [pctx_free]
                emit_ctx_extract = make_ctx_extract(h, pm_free, pctx_free_ref)
                last_ed1 = 0
                for Hf in range(2):
                    jmax = 8 if Hf == 0 else 16
                    pend = None
                    for j in range(jmax):
                        qlo = max(128 * j, 1024 * Hf)
                        off = qlo - 1024 * Hf
                        jb = j % 2
                        segs = []
                        p = off
                        while p < 1024:
                            q = 512 if p < 512 else 1024
                            segs.append((p, q))
                            p = q
                        has_diag = (Hf == 0) or (j >= 8)
                        def pS_fn(e, j=j, jb=jb, segs=tuple(segs), Hf=Hf, has_diag=has_diag, off=off):
                            last = None
                            for (p, q) in segs:
                                last = e.matmul(out=psum_S[jb][:, p:q],
                                                lhsT=kT_bf[:, j * 128 : j * 128 + 128],
                                                rhs=qT_bf[:, 1024 * Hf + p : 1024 * Hf + q],
                                                start=True, stop=True)
                            if has_diag:
                                last = e.matmul(out=psum_S[jb][:, off : off + 128],
                                                lhsT=maskdT_sb[:], rhs=ident_bf[:],
                                                start=False, stop=True,
                                                skip_group_check=True)
                            return last
                        c_ps = op("tensor", pS_fn,
                                  waits=W(("dve", kr_done), ("dve", qr_done),
                                          ("act", pS_free[jb]["act"]),
                                          ("dve", pS2_free[jb])),
                                  inc="pe")
                        c_ae = op("scalar", lambda e, jb=jb, off=off: e.activation(
                            out=PT[jb][:, off:1024], in_=psum_S[jb][:, off:1024],
                            func=AF.Exp, bias=-CSH, scale=0.125),
                            waits=W(("pe", c_ps)), inc="act")
                        pS_free[jb] = {"act": c_ae}

                        if pend is not None:
                            last_ed1 = emit_ctx_extract(*pend, last_ed1)
                        pend = (Hf, j, jb, tuple(segs), c_ae, jmax)
                    last_ed1 = emit_ctx_extract(*pend, last_ed1)
                    pctx_free = {"dve": cnt["dve"], "act": cnt["act"]}

            if DBG_ATTN:
                attn_done = max(attn_done, nonlocal_state["attn_done"])
            if s == 0 and DBG_ATTN and DBG_HEADS == 16 and DBG_FFN:
                for m in (30, 31):
                    buf = m % 2
                    stage_free[buf] = quant_exact(
                        w1stage[buf][:, 0:512], w2qT[:, m * 1024 : m * 1024 + 512],
                        512, 3, -6.0, True, first_waits=W(w2_lw[m]))
                    stage_free[buf] = quant_exact(
                        w1stage[buf][:, 512:1024],
                        w2qT[:, m * 1024 + 512 : m * 1024 + 1024],
                        512, 3, -6.0, True, first_waits=W(w2_lw[m]))

            if TAP == "attn":
                for t in range(NT):
                    buf = t % 2
                    cc = op("vector", lambda e, t=t, buf=buf: e.tensor_copy(
                        out=ln_b[buf][:], in_=attn_bf[:, t * 1024 : t * 1024 + 1024]),
                        waits=W(("dve", attn_done), lnb_free_st[buf],
                                ("dve", lnb_free_d[buf])),
                        inc="dve")
                    lnb_free_d[buf] = cc
                    lnb_free_st[buf] = dma("scalar",
                                           tap_out[s, t * 128 : t * 128 + 128, :],
                                           ln_b[buf][:], waits=W(("dve", cc)),
                                           sem="sb%d" % buf)

            yT_written = 0
            for t in range(NT if DBG_LN1 else 0):
                buf = t % 2
                lx = dma("sync", ln_a[buf][:], x_tok[s, t * 128 : t * 128 + 128, :],
                         waits=W(("dve", lna_free[buf])), sem="la%d" % buf)
                c_add = op("vector", lambda e, buf=buf, t=t: e.tensor_tensor(
                    out=ln_b[buf][:], in0=ln_a[buf][:],
                    in1=attn_bf[:, t * 1024 : t * 1024 + 1024], op=ALU.add),
                    waits=W(lx, ("dve", attn_done), lnb_free_st[buf],
                            ("dve", lnb_free_d[buf])),
                    inc="dve")
                lna_free[buf] = c_add
                op("vector", lambda e, buf=buf: e.reduce_sum(
                    out=svec[:, 80:81], in_=ln_b[buf][:], axis=mybir.AxisListType.X),
                    inc="dve")
                op("vector", lambda e, buf=buf: e.scalar_tensor_tensor(
                    out=ybf_tile[:], in0=ln_b[buf][:], scalar=1.0, in1=ln_b[buf][:],
                    op0=ALU.mult, op1=ALU.mult, accum_out=svec[:, 81:82]),
                    waits=W(("pe", ybf_free)), inc="dve")
                def stats_a(e):
                    e.tensor_scalar(out=svec[:, 82:83], in0=svec[:, 80:81], scalar1=1.0 / D,
                                    scalar2=0.0, op0=ALU.mult, op1=ALU.add)
                    return e.tensor_scalar(out=svec[:, 83:84], in0=svec[:, 81:82], scalar1=1.0 / D,
                                           scalar2=0.0, op0=ALU.mult, op1=ALU.add)
                op("vector", stats_a, inc="dve")
                op("vector", lambda e: e.tensor_tensor(out=svec[:, 84:85], in0=svec[:, 82:83],
                                                       in1=svec[:, 82:83], op=ALU.mult),
                   inc="dve")
                c_stats = op("vector", lambda e: e.tensor_tensor(
                    out=svec[:, 85:86], in0=svec[:, 83:84], in1=svec[:, 84:85], op=ALU.subtract),
                    inc="dve")
                c_r1 = op("scalar", lambda e: e.activation(
                    out=svec[:, 86:87], in_=svec[:, 85:86], func=AF.Ln, bias=EPS_LN, scale=1.0),
                    waits=W(("dve", c_stats)), inc="act")
                c_rstd = op("scalar", lambda e: e.activation(
                    out=svec[:, 87:88], in_=svec[:, 86:87], func=AF.Exp, bias=0.0, scale=-0.5),
                    inc="act")
                c_apply = op("vector", lambda e, buf=buf: e.tensor_scalar(
                    out=ln_b[buf][:], in0=ln_b[buf][:], scalar1=svec[:, 82:83],
                    scalar2=svec[:, 87:88], op0=ALU.subtract, op1=ALU.mult),
                    waits=W(("act", c_rstd)), inc="dve")
                if TAP == "y":
                    c_sp = dma("scalar", tap_out[s, t * 128 : t * 128 + 128, :],
                               ln_b[buf][:], waits=W(("dve", c_apply)), sem="sb%d" % buf)
                else:
                    c_sp = dma("scalar", y_spill[t * 128 : t * 128 + 128, :],
                               ln_b[buf][:], waits=W(("dve", c_apply)), sem="sb%d" % buf)
                spill_cnt[t] = c_sp
                c_ybf = op("vector", lambda e, buf=buf: e.tensor_copy(
                    out=ybf_tile[:], in_=ln_b[buf][:]),
                    waits=W(("dve", c_apply), ("pe", ybf_free)), inc="dve")
                lnb_free_st[buf] = c_sp
                lnb_free_d[buf] = c_ybf
                for grp in range(2):
                    pb = grp
                    def ptr_fn(e, grp=grp, pb=pb):
                        last = None
                        for dc in range(4 * grp, 4 * grp + 4):
                            last = e.matmul(
                                out=psum_m[pb][:, (dc % 4) * 128 : (dc % 4) * 128 + 128],
                                lhsT=ybf_tile[:, dc * 128 : dc * 128 + 128],
                                rhs=ident_bf[:], start=True, stop=True)
                        return last
                    c_ptr = op("tensor", ptr_fn,
                               waits=W(("dve", c_ybf), ("dve", pm_free[pb]["dve"]),
                                       ("act", pm_free[pb]["act"])),
                               inc="pe")
                    def dtr(e, grp=grp, pb=pb, t=t):
                        return e.tensor_copy(
                            out=featbuf[:].rearrange("p (c n) -> p c n", n=2048)[
                                :, 4 * grp : 4 * grp + 4, t * 128 : t * 128 + 128],
                            in_=psum_m[pb][:, 0:512])
                    c_dtr = op("vector", dtr, waits=W(("pe", c_ptr)), inc="dve")
                    pm_free[pb] = {"dve": c_dtr, "act": pm_free[pb]["act"]}
                    yT_written = c_dtr
                ybf_free = cnt["pe"]

            w1s_free = [cnt["dve"], cnt["dve"]]
            hq_free = 0
            for th in range(4 if DBG_FFN else 0):
                thc = th * 512
                mm1_done = 0
                for m in range(32):
                    wbuf = m % 2
                    pb = m % 2
                    lw = dma("sync", w1stage[wbuf][:], w1tl[m],
                             waits=W(("dve", w1s_free[wbuf])), sem="lw%d" % wbuf)
                    c_cw = op("vector", lambda e, wbuf=wbuf: e.tensor_copy(
                        out=w1bf[wbuf][:], in_=w1stage[wbuf][:]),
                        waits=W(lw), inc="dve")
                    w1s_free[wbuf] = c_cw
                    def pmm1(e, wbuf=wbuf, pb=pb, thc=thc):
                        last = None
                        for dcc in range(8):
                            last = e.matmul(
                                out=psum_m[pb][:],
                                lhsT=w1bf[wbuf][:, dcc * 128 : dcc * 128 + 128],
                                rhs=featbuf[:, dcc * 2048 + thc : dcc * 2048 + thc + 512],
                                start=(dcc == 0), stop=(dcc == 7))
                        return last
                    c_mm1 = op("tensor", pmm1,
                               waits=W(("dve", c_cw), ("dve", yT_written),
                                       ("dve", pm_free[pb]["dve"]),
                                       ("act", pm_free[pb]["act"]),
                                       ("pe", hq_free)),
                               inc="pe")
                    c_rel = op("scalar", lambda e, pb=pb, m=m: e.activation(
                        out=hwork[pb][:], in_=psum_m[pb][:], func=AF.Relu,
                        bias=b1_sb[:, m : m + 1], scale=1.0),
                        waits=W(("pe", c_mm1), ("dve", hwork_free[pb])),
                        inc="act")
                    c_hq = quant_exact(
                        hwork[pb][:],
                        hqT[:, m * 512 : m * 512 + 512],
                        512, 3, -6.0, False, first_waits=W(("act", c_rel)))
                    hwork_free[pb] = c_hq
                    pm_free[pb] = {"dve": c_hq, "act": c_rel}
                    mm1_done = c_hq
                for Tc2 in range(2):
                    oq_last = 0
                    for dout in range(8):
                        ob = (Tc2 * 8 + dout) % 4
                        pb = ob % 2
                        pslice = [psum_m[0][:, 0:256], psum_m[1][:, 0:256],
                                  psum_ctx[:, 0:256], psum_ctx[:, 512:768]][ob]
                        def pmm2(e, pslice=pslice, dout=dout, Tc2=Tc2):
                            last = None
                            for m in range(32):
                                last = e.matmul(
                                    out=pslice,
                                    lhsT=w2qT[:, m * 1024 + dout * 128 : m * 1024 + dout * 128 + 128],
                                    rhs=hqT[:, m * 512 + Tc2 * 256 : m * 512 + Tc2 * 256 + 256],
                                    start=(m == 0), stop=(m == 31))
                            return last
                        c_mm2 = op("tensor", pmm2,
                                   waits=W(("dve", mm1_done),
                                           ("dve", pm_free[pb]["dve"]),
                                           ("act", pm_free[pb]["act"]),
                                           ("dve", pm2_free.get(ob, 0))),
                                   inc="pe")
                        wbuf2 = dout % 2
                        c_ev = op("vector", lambda e, pslice=pslice, dout=dout, wbuf2=wbuf2:
                                  e.tensor_scalar(out=hwork[wbuf2][:, 0:256], in0=pslice,
                                                  scalar1=b2q_sb[:, dout : dout + 1],
                                                  scalar2=0.0, op0=ALU.add, op1=ALU.add),
                                  waits=W(("pe", c_mm2), ("dve", hwork_free[wbuf2])),
                                  inc="dve")
                        hwork_free[wbuf2] = c_ev
                        c_oq = op("vector", lambda e, wbuf2=wbuf2, dout=dout: e.tensor_copy(
                            out=oq_all[:, dout * 256 : dout * 256 + 256],
                            in_=hwork[wbuf2][:, 0:256]), inc="dve")
                        hwork_free[wbuf2] = c_oq
                        pm2_free[ob] = c_oq
                        pm_free[pb] = {"dve": c_oq, "act": pm_free[pb]["act"]}
                        oq_last = c_oq
                    for tk in range(2):
                        tb = th * 4 + Tc2 * 2 + tk
                        def ptr2(e, tk=tk):
                            last = None
                            for dout in range(8):
                                last = e.matmul(
                                    out=psum_S[tk][:, dout * 128 : dout * 128 + 128],
                                    lhsT=oq_all[:, dout * 256 + tk * 128 : dout * 256 + tk * 128 + 128],
                                    rhs=ident_h[:], start=True, stop=True)
                            return last
                        c_tr2 = op("tensor", ptr2,
                                   waits=W(("dve", oq_last), ("dve", pS2_free[tk]),
                                           ("act", pS_free[tk]["act"])),
                                   inc="pe")
                        ly = dma("scalar", ln_a[tk][:], y_spill[tb * 128 : tb * 128 + 128, :],
                                 waits=W(("dve", lna_free[tk]), spill_cnt[tb]),
                                 sem="la%d" % tk)
                        c_s2 = op("vector", lambda e, tk=tk: e.tensor_tensor(
                            out=ln_b[tk][:], in0=psum_S[tk][:, 0:1024], in1=ln_a[tk][:],
                            op=ALU.add),
                            waits=W(("pe", c_tr2), ly, lnb_free_st[tk],
                                    ("dve", lnb_free_d[tk])),
                            inc="dve")
                        lna_free[tk] = c_s2
                        op("vector", lambda e, tk=tk: e.reduce_sum(
                            out=svec[:, 88:89], in_=ln_b[tk][:], axis=mybir.AxisListType.X),
                            inc="dve")
                        c_sq2 = op("vector", lambda e, tk=tk: e.scalar_tensor_tensor(
                            out=ybf_tile[:], in0=ln_b[tk][:], scalar=1.0, in1=ln_b[tk][:],
                            op0=ALU.mult, op1=ALU.mult, accum_out=svec[:, 89:90]),
                            waits=W(("pe", ybf_free)), inc="dve")
                        def stats2_a(e):
                            e.tensor_scalar(out=svec[:, 90:91], in0=svec[:, 88:89],
                                            scalar1=1.0 / D, scalar2=0.0, op0=ALU.mult, op1=ALU.add)
                            return e.tensor_scalar(out=svec[:, 91:92], in0=svec[:, 89:90],
                                                   scalar1=1.0 / D, scalar2=0.0, op0=ALU.mult, op1=ALU.add)
                        op("vector", stats2_a, inc="dve")
                        op("vector", lambda e: e.tensor_tensor(out=svec[:, 92:93], in0=svec[:, 90:91],
                                                               in1=svec[:, 90:91], op=ALU.mult),
                           inc="dve")
                        c_sts = op("vector", lambda e: e.tensor_tensor(
                            out=svec[:, 93:94], in0=svec[:, 91:92], in1=svec[:, 92:93],
                            op=ALU.subtract), inc="dve")
                        op("scalar", lambda e: e.activation(
                            out=svec[:, 94:95], in_=svec[:, 93:94], func=AF.Ln,
                            bias=EPS_LN, scale=1.0), waits=W(("dve", c_sts)), inc="act")
                        c_rs2 = op("scalar", lambda e: e.activation(
                            out=svec[:, 95:96], in_=svec[:, 94:95], func=AF.Exp,
                            bias=0.0, scale=-0.5), inc="act")
                        c_ap2 = op("vector", lambda e, tk=tk: e.tensor_scalar(
                            out=ln_b[tk][:], in0=ln_b[tk][:], scalar1=svec[:, 90:91],
                            scalar2=svec[:, 95:96], op0=ALU.subtract, op1=ALU.mult),
                            waits=W(("act", c_rs2)), inc="dve")
                        pS2_free[tk] = c_ap2
                        lnb_free_d[tk] = c_ap2
                        lnb_free_st[tk] = dma("scalar",
                                              out_tok[s, tb * 128 : tb * 128 + 128, :],
                                              ln_b[tk][:], waits=W(("dve", c_ap2)),
                                              sem="sb%d" % tk)
                hq_free = cnt["pe"]
            featbuf_pe_free = cnt["pe"]

        op("sync", lambda e: e.nop(),
           waits=W(("sb0", cnt["sb0"]), ("sb1", cnt["sb1"])))

        ENGS = ["sync", "scalar", "vector", "tensor", "gpsimd"]
        emitted_waits = {e: {} for e in ENGS}

        OWN_SEM = {"vector": "dve", "scalar": "act", "gpsimd": "gps"}

        def replay(eng_name):
            ew = emitted_waits[eng_name]
            own = OWN_SEM.get(eng_name)
            def body(eng):
                for (en, fn, waits, inc, amt) in OPS:
                    if en != eng_name:
                        continue
                    allw = list(waits)
                    if own is not None and inc == own:
                        allw.append((own, cnt_at[id(fn)]))
                    for (sname, val) in allw:
                        if val > ew.get(sname, 0):
                            eng.wait_ge(sems[sname], val)
                            ew[sname] = val
                    instr = fn(eng)
                    if inc is not None:
                        assert instr is not None, f"op on {eng_name} returned no instruction"
                        instr.then_inc(sems[inc], amt, skip_validation=(amt == 16))
            return body

        for en in ENGS:
            if any(o[0] == en for o in OPS):
                getattr(block, en)(replay(en))

    return nc


def _host_inputs(inputs, core):
    x = np.asarray(inputs["L0_attn_x"])
    xs = np.ascontiguousarray(x[2 * core : 2 * core + 2])
    xf = np.ascontiguousarray(xs.transpose(0, 2, 1))
    W_DQ = np.asarray(inputs["L0_attn_W_DQ"]); W_UQ = np.asarray(inputs["L0_attn_W_UQ"])
    W_DKV = np.asarray(inputs["L0_attn_W_DKV"]); W_UKV = np.asarray(inputs["L0_attn_W_UKV"])
    W_QR = np.asarray(inputs["L0_attn_W_QR"]); W_KR = np.asarray(inputs["L0_attn_W_KR"])
    W_O = np.asarray(inputs["L0_attn_W_O"])
    qnw = np.asarray(inputs["L0_attn_q_norm_w"]); knw = np.asarray(inputs["L0_attn_k_norm_w"])
    w1 = np.asarray(inputs["L0_ffn_w1"]); b1 = np.asarray(inputs["L0_ffn_b1"])
    w2 = np.asarray(inputs["L0_ffn_w2"]); b2 = np.asarray(inputs["L0_ffn_b2"])

    tile_pc = lambda a, P=128: np.ascontiguousarray(
        a.reshape(a.shape[0] // P, P, a.shape[1]).transpose(1, 0, 2).reshape(P, -1))
    def pad_rt(wt):
        half = np.concatenate([np.zeros((64, 64), np.float32),
                               np.ascontiguousarray(wt.T)], axis=1)
        return np.concatenate([half, half], axis=0)

    def par_rt(wt, par):
        out = np.zeros((128, 128), np.float32)
        out[par * 64 : par * 64 + 64, 64:128] = np.ascontiguousarray(wt.T)
        return out
    d = {
        "x_tok": xs,
        "x_feat": xf,
        "wdq": tile_pc(W_DQ),
        "wuqt": tile_pc(np.ascontiguousarray(W_UQ.T)),
        "wdkv": tile_pc(W_DKV),
        "wukvt": tile_pc(np.ascontiguousarray(W_UKV.T)),
        "wqrt_p": pad_rt(W_QR),
        "wkrt_e": par_rt(W_KR, 0),
        "wkrt_o": par_rt(W_KR, 1),
        "wot": np.ascontiguousarray(W_O.T),
        "wq_b": np.ascontiguousarray(np.broadcast_to(qnw, (128, 64))),
        "wk_b": np.ascontiguousarray(np.broadcast_to(knw, (128, 64))),
        "w1tl": np.ascontiguousarray(
            w1.T.reshape(8, 128, 32, 128).transpose(2, 1, 0, 3).reshape(32, 128, 1024)),
        "w2t": np.ascontiguousarray(w2.T),
        "b1_t": np.ascontiguousarray(b1.reshape(32, 128).T),
        "b2_t": np.ascontiguousarray(b2.reshape(8, 128).T),
        "identf": np.eye(128, dtype=np.float32),
    }
    d = {k: np.ascontiguousarray(np.asarray(v, dtype=np.float32)) for k, v in d.items()}
    import ml_dtypes
    mk = np.where(np.arange(128)[None, :] >= np.arange(128)[:, None],
                  np.float32(0.0), np.float32(-1e30)).astype(ml_dtypes.bfloat16)
    d["maskd"] = np.ascontiguousarray(mk)
    d["maskdT"] = np.ascontiguousarray(mk.T)
    return d


def kernel(**inputs):
    nc = build_program()
    in_maps = [_host_inputs(inputs, c) for c in range(8)]
    res = run_bass_kernel_spmd(nc, in_maps, core_ids=list(range(8)))
    out = np.concatenate([r["out_tok"] for r in res.results], axis=0)
    if TAP is not None:
        kernel.tap = np.concatenate([r["tap"] for r in res.results], axis=0)
    return out


# version 04
# speedup vs baseline: 1.3354x; 1.2084x over previous
import contextlib
import numpy as np
import concourse.bass as bass
import concourse.mybir as mybir
from concourse.bass_utils import run_bass_kernel_spmd

F32 = mybir.dt.float32
BF16 = mybir.dt.bfloat16
F16 = mybir.dt.float16
F8 = mybir.dt.float8e4
I32 = mybir.dt.int32
AF = mybir.ActivationFunctionType
ALU = mybir.AluOpType

LN2 = 0.6931471805599453
INVLN2 = 1.4426950408889634
EPS_ATTN = 1.1920929e-07
EPS_LN = 1e-5
TINY = 1e-38
CSH = 20.0
T = 2048
D = 1024
NT = 16
NSEQ = 2

TAP = None
import os
DBG_HEADS = int(os.environ.get("MK_HEADS", "16"))
DBG_SEQ = int(os.environ.get("MK_SEQ", str(NSEQ)))
DBG_FFN = os.environ.get("MK_FFN", "1") == "1"
DBG_LN1 = os.environ.get("MK_LN1", "1") == "1"
DBG_ATTN = os.environ.get("MK_ATTN", "1") == "1"
DBG_HSTAGE = int(os.environ.get("MK_HSTAGE", "9"))
DBG_G = int(os.environ.get("MK_G", "4"))
DBG_NOA1 = os.environ.get("MK_NOA1", "0") == "1"
DBG_NOD1 = os.environ.get("MK_NOD1", "0") == "1"
DBG_D1SEL = int(os.environ.get("MK_D1SEL", "7"))
DBG_NOGPS = os.environ.get("MK_NOGPS", "0") == "1"
DBG_SERIAL = os.environ.get("MK_SERIAL", "0") != "0"


def build_program():
    nc = bass.Bass("TRN2", target_bir_lowering=False, debug=False)

    x_tok = nc.dram_tensor("x_tok", [NSEQ, T, D], F32, kind="ExternalInput").ap()
    x_feat = nc.dram_tensor("x_feat", [NSEQ, D, T], F32, kind="ExternalInput").ap()
    wdq = nc.dram_tensor("wdq", [128, 256], F32, kind="ExternalInput").ap()
    wuqt = nc.dram_tensor("wuqt", [128, 256], F32, kind="ExternalInput").ap()
    wdkv = nc.dram_tensor("wdkv", [128, 512], F32, kind="ExternalInput").ap()
    wukvt = nc.dram_tensor("wukvt", [128, 1024], F32, kind="ExternalInput").ap()
    wqrt_p = nc.dram_tensor("wqrt_p", [128, 128], F32, kind="ExternalInput").ap()
    wkrt_e = nc.dram_tensor("wkrt_e", [128, 128], F32, kind="ExternalInput").ap()
    wkrt_o = nc.dram_tensor("wkrt_o", [128, 128], F32, kind="ExternalInput").ap()
    wot = nc.dram_tensor("wot", [64, 64], F32, kind="ExternalInput").ap()
    wq_b = nc.dram_tensor("wq_b", [128, 64], F32, kind="ExternalInput").ap()
    wk_b = nc.dram_tensor("wk_b", [128, 64], F32, kind="ExternalInput").ap()
    w1tl = nc.dram_tensor("w1tl", [32, 128, 1024], F32, kind="ExternalInput").ap()
    w2t = nc.dram_tensor("w2t", [4096, D], F32, kind="ExternalInput").ap()
    b1_t = nc.dram_tensor("b1_t", [128, 32], F32, kind="ExternalInput").ap()
    b2_t = nc.dram_tensor("b2_t", [128, 8], F32, kind="ExternalInput").ap()
    identf = nc.dram_tensor("identf", [128, 128], F32, kind="ExternalInput").ap()
    maskd = nc.dram_tensor("maskd", [128, 128], BF16, kind="ExternalInput").ap()
    maskdT = nc.dram_tensor("maskdT", [128, 128], BF16, kind="ExternalInput").ap()
    out_tok = nc.dram_tensor("out_tok", [NSEQ, T, D], F32, kind="ExternalOutput").ap()
    y_spill = nc.dram_tensor("y_spill", [T, D], F32, kind="Internal").ap()
    weff_scr = nc.dram_tensor("weff_scr", [64, 192], F32, kind="Internal").ap()
    w1bf_dram = nc.dram_tensor("w1bf_dram", [32, 128, 1024], BF16, kind="Internal").ap()
    tap_out = None
    if TAP is not None:
        tap_out = nc.dram_tensor("tap", [NSEQ, T, D], F32, kind="ExternalOutput").ap()

    ctx = contextlib.ExitStack()
    with ctx:
        sb = lambda name, shape, dt: ctx.enter_context(nc.sbuf_tensor(name, shape, dt))
        psb = lambda name, shape: ctx.enter_context(nc.psum_tensor(name, shape, F32))

        cpool = sb("cpool", [128, 16], F32)
        ident_bf = sb("ident_bf", [128, 128], BF16)
        ident_h = sb("ident_h", [128, 128], F16)
        maskd_sb = sb("maskd_sb", [128, 128], BF16)
        maskdT_sb = sb("maskdT_sb", [128, 128], BF16)
        wqb_sb = sb("wqb_sb", [128, 64], F32)
        wkb_sb = sb("wkb_sb", [128, 64], F32)
        wdq_bf = sb("wdq_bf", [128, 256], BF16)
        wuqt_bf = sb("wuqt_bf", [128, 256], BF16)
        wdkv_bf = sb("wdkv_bf", [128, 512], BF16)
        wukvt_bf = sb("wukvt_bf", [128, 1024], BF16)
        wqrt_bf = sb("wqrt_bf", [128, 128], BF16)
        wkrt_bf = [sb("wkrt_bf0", [128, 128], BF16), sb("wkrt_bf1", [128, 128], BF16)]
        wot_bf = sb("wot_bf", [64, 64], BF16)
        wqkv = [sb("wqkv0", [128, 192], BF16), sb("wqkv1", [128, 192], BF16)]
        b1_sb = sb("b1_sb", [128, 32], F32)
        b2_sb = sb("b2_sb", [128, 8], F32)
        b2q_sb = sb("b2q_sb", [128, 8], F32)

        featbuf = sb("featbuf", [128, 8 * T], BF16)
        attn_bf = sb("attn_bf", [128, NT * D], BF16)
        w2qT = sb("w2qT", [128, 32 * D], F8)
        hqT = [sb("hqT0", [128, 32 * 256], F8), sb("hqT1", [128, 32 * 256], F8)]

        kT_bf = sb("kT_bf", [128, T], BF16)
        qT_bf = sb("qT_bf", [128, T], BF16)
        v_aug = sb("v_aug", [128, NT * 66], BF16)
        kc_bf = sb("kc_bf", [128, NT * 64], BF16)
        qc_bf = sb("qc_bf", [128, NT * 64], BF16)
        svec = sb("svec", [128, 128], F32)
        PT = [sb("PT0", [128, 1024], BF16), sb("PT1", [128, 1024], BF16)]
        ctxTa_bf = sb("ctxTa_bf", [128, 128], BF16)
        qsq = sb("qsq", [128, 1024], BF16)

        ln_a = [sb("ln_a0", [128, D], F32), sb("ln_a1", [128, D], F32)]
        ln_b = [sb("ln_b0", [128, D], F32), sb("ln_b1", [128, D], F32)]
        ybf_tile = sb("ybf_tile", [128, D], BF16)
        w1stage = [sb("w1stage0", [128, 1024], F32), sb("w1stage1", [128, 1024], F32)]
        w1bf = [sb("w1bf0", [128, 1024], BF16), sb("w1bf1", [128, 1024], BF16)]
        hwork = sb("hwork", [128, 1024], F32)
        ql = [sb("ql0", [128, 512], F32), sb("ql1", [128, 512], F32)]
        qi = [sb("qi0", [128, 512], I32), sb("qi1", [128, 512], I32)]
        qsc = [sb("qsc0", [128, 512], F32), sb("qsc1", [128, 512], F32)]
        qinv = [sb("qinv0", [128, 512], F32), sb("qinv1", [128, 512], F32)]
        qv = [sb("qv0", [128, 512], F32), sb("qv1", [128, 512], F32)]
        qc = [sb("qc0", [128, 512], F32), sb("qc1", [128, 512], F32)]
        b2h = sb("b2h", [128, 8], F16)
        oq_all = sb("oq_all", [128, 2048], F16)
        owork = sb("owork", [128, 512], F32)

        psum_S = [psb("psum_S0", [128, 1024]), psb("psum_S1", [128, 1024])]
        psum_ctx = psb("psum_ctx", [128, 1024])
        psum_m = [psb("psum_m0", [128, 512]), psb("psum_m1", [128, 512])]

        CONSTS = sorted({TINY, -3 * LN2, 3 * LN2, LN2, -LN2, EPS_ATTN,
                         EPS_LN, 0.125, -CSH, -0.5, -1.0, 1.0 / 64.0, 0.5, 1.0, 0.0})
        for j, v in enumerate(CONSTS):
            nc.const_aps.register(F32, float(v), cpool[:, j : j + 1])

        sems = {}
        for s in ["ld", "lw0", "lw1", "la0", "la1", "sb0", "sb1", "wc0", "wc1",
                  "pe", "act", "dve", "gps"]:
            sems[s] = ctx.enter_context(nc.semaphore("sem_" + s))
        cnt = {s: 0 for s in sems}

        block = ctx.enter_context(nc.Block())
        OPS = []

        cnt_at = {}

        def op(eng, fn, waits=(), inc=None, amt=1):
            waits = [(sn, v) for (sn, v) in waits if v > 0]
            if DBG_SERIAL:
                mode = os.environ.get("MK_SERIAL", "0")
                DMAS = ("ld", "lw0", "lw1", "la0", "la1", "sb0", "sb1")
                if mode == "1":
                    waits = [(sn, v) for (sn, v) in cnt.items() if v > 0]
                elif mode == "2":
                    if eng in ("sync", "scalar") and fn.__name__ == "<lambda>" and inc in DMAS:
                        waits = waits + [(sn, cnt[sn]) for sn in ("pe", "act", "dve", "gps") if cnt[sn] > 0]
                    else:
                        waits = waits + [(sn, cnt[sn]) for sn in DMAS if cnt[sn] > 0]
                elif mode == "3":
                    if eng == "tensor":
                        waits = waits + [(sn, cnt[sn]) for sn in ("act", "dve", "gps") if cnt[sn] > 0]
                    elif eng in ("vector", "scalar", "gpsimd"):
                        waits = waits + [("pe", cnt["pe"])] if cnt["pe"] > 0 else waits
                elif mode == "4":
                    if eng == "vector" and cnt["act"] > 0:
                        waits = waits + [("act", cnt["act"])]
                    elif eng == "scalar" and inc == "act" and cnt["dve"] > 0:
                        waits = waits + [("dve", cnt["dve"])]
            OPS.append((eng, fn, waits, inc, amt))
            if inc is not None:
                cnt_at[id(fn)] = cnt[inc]
                cnt[inc] += amt
                return cnt[inc]
            return None

        def dma(queue, out_ap, in_ap, waits=(), sem="ld"):
            c = op(queue, lambda e, o=out_ap, i=in_ap: e.dma_start(out=o, in_=i),
                   waits=waits, inc=sem, amt=16)
            return (sem, c)

        W = lambda *pairs: list(pairs)

        def memset_consts(e):
            last = None
            for j, v in enumerate(CONSTS):
                last = e.memset(cpool[:, j : j + 1], float(v))
            return last
        op("vector", memset_consts, inc="dve")

        dma("sync", maskd_sb[:], maskd)
        dma("sync", maskdT_sb[:], maskdT)
        dma("sync", wqb_sb[:], wq_b)
        dma("sync", wkb_sb[:], wk_b)
        dma("sync", b1_sb[:], b1_t)
        dma("sync", b2_sb[:], b2_t)
        dma("sync", w1stage[0][:, 0:128], identf)
        dma("sync", w1stage[1][:, 0:256], wdq)
        dma("sync", w1stage[1][:, 256:512], wuqt)
        dma("sync", ln_a[0][:, 0:512], wdkv)
        dma("sync", ln_a[1][:, 0:1024], wukvt)
        dma("sync", ln_b[0][:, 0:128], wqrt_p)
        dma("sync", ln_b[0][:, 128:256], wkrt_e)
        dma("sync", ln_b[0][:, 320:448], wkrt_o)
        ld_w = dma("sync", ln_b[0][0:64, 256:320], wot)[1]

        def cast_weights(e):
            e.tensor_copy(out=ident_bf[:], in_=w1stage[0][:, 0:128])
            e.tensor_copy(out=ident_h[:], in_=w1stage[0][:, 0:128])
            e.tensor_copy(out=wdq_bf[:], in_=w1stage[1][:, 0:256])
            e.tensor_copy(out=wuqt_bf[:], in_=w1stage[1][:, 256:512])
            e.tensor_copy(out=wdkv_bf[:], in_=ln_a[0][:, 0:512])
            e.tensor_copy(out=wukvt_bf[:], in_=ln_a[1][:, 0:1024])
            e.tensor_copy(out=wqrt_bf[:], in_=ln_b[0][:, 0:128])
            e.tensor_copy(out=wkrt_bf[0][:], in_=ln_b[0][:, 128:256])
            e.tensor_copy(out=wkrt_bf[1][:], in_=ln_b[0][:, 320:448])
            return e.tensor_copy(out=wot_bf[:], in_=ln_b[0][0:64, 256:320])
        c_cast = op("vector", cast_weights, waits=W(("ld", ld_w)), inc="dve")

        def weff(e):
            for c in range(4):
                e.matmul(out=psum_m[0][0:64, 0:64], lhsT=wdq_bf[:, c * 64 : c * 64 + 64],
                         rhs=wuqt_bf[:, c * 64 : c * 64 + 64], start=(c == 0), stop=(c == 3))
            last = None
            for c in range(8):
                last = e.matmul(out=psum_m[0][0:64, 64:192],
                                lhsT=wdkv_bf[:, c * 64 : c * 64 + 64],
                                rhs=wukvt_bf[:, c * 128 : c * 128 + 128],
                                start=(c == 0), stop=(c == 7))
            return last
        c_weff = op("tensor", weff, waits=W(("dve", c_cast)), inc="pe")

        weff_f32 = ln_b[1]
        def weff_evac(e):
            e.memset(wqkv[0][:], 0.0)
            e.memset(wqkv[1][:], 0.0)
            return e.tensor_copy(out=weff_f32[0:64, 0:192], in_=psum_m[0][0:64, 0:192])
        c_we1 = op("vector", weff_evac, waits=W(("pe", c_weff)), inc="dve")
        c_we2 = dma("sync", weff_scr, weff_f32[0:64, 0:192], waits=W(("dve", c_we1)))
        c_we3 = dma("sync", weff_f32[64:128, 0:192], weff_scr, waits=W(c_we2))
        def weff_cast(e):
            e.tensor_copy(out=wqkv[0][0:64, 128:192], in_=weff_f32[0:64, 0:64])
            e.tensor_copy(out=wqkv[0][0:64, 0:128], in_=weff_f32[0:64, 64:192])
            e.tensor_copy(out=wqkv[1][64:128, 128:192], in_=weff_f32[64:128, 0:64])
            return e.tensor_copy(out=wqkv[1][64:128, 0:128], in_=weff_f32[64:128, 64:192])
        c_weffe = op("vector", weff_cast, waits=W(c_we3), inc="dve")

        QSTATE = [{"dve": 0, "gps": 0, "act": 0}, {"dve": 0, "gps": 0, "act": 0}]
        QPAR = [0]

        def quant_exact(src_ap, dst_ap, n, man, clamp, signed, first_waits=()):
            par = QPAR[0]
            QPAR[0] ^= 1
            QS = QSTATE[par]
            qls, qis = ql[par][:, 0:n], qi[par][:, 0:n]
            qscs, qinvs = qsc[par][:, 0:n], qinv[par][:, 0:n]
            qvs, qcs = qv[par][:, 0:n], qc[par][:, 0:n]
            qes = qls
            qd = qvs
            fw = list(first_waits) + [("dve", QS["dve"]), ("gps", QS["gps"]),
                                      ("act", QS["act"])]
            if signed:
                a1 = op("scalar", lambda e, sp=src_ap, o=qd: e.activation(
                    out=o, in_=sp, func=AF.Square), waits=fw, inc="act")
                a2 = op("scalar", lambda e, o=qls, i=qd: e.activation(
                    out=o, in_=i, func=AF.Ln, bias=TINY, scale=1.0),
                    waits=W(("act", a1)), inc="act")
                mulc, backc = INVLN2 / 2.0, 2.0 * LN2
            else:
                a2 = op("scalar", lambda e, sp=src_ap, o=qls: e.activation(
                    out=o, in_=sp, func=AF.Ln, bias=TINY, scale=1.0), waits=fw, inc="act")
                mulc, backc = INVLN2, LN2
            geng, gsem = ("vector", "dve") if DBG_NOGPS else ("gpsimd", "gps")
            g1 = op(geng, lambda e, o=qis, i=qls, m=mulc: e.tensor_scalar(
                out=o, in0=i, scalar1=m, scalar2=0.0, op0=ALU.mult, op1=ALU.add),
                waits=W(("act", a2), ("dve", QS["dve"])), inc=gsem)
            d1 = op("vector", lambda e, o=qcs, i0=qis, i1=qls, b=backc: e.scalar_tensor_tensor(
                out=o, in0=i0, scalar=b, in1=i1, op0=ALU.mult, op1=ALU.is_gt),
                waits=W((gsem, g1)), inc="dve")
            d2 = op("vector", lambda e, o=qes, i0=qcs, i1=qis: e.scalar_tensor_tensor(
                out=o, in0=i0, scalar=-1.0, in1=i1, op0=ALU.mult, op1=ALU.add),
                inc="dve")
            if clamp is not None:
                d2 = op(geng, lambda e, o=qes, c=float(clamp): e.tensor_scalar(
                    out=o, in0=o, scalar1=c, scalar2=0.0, op0=ALU.max, op1=ALU.add),
                    waits=W(("dve", d2)), inc=gsem)
                exp_waits = W((gsem, d2))
            else:
                exp_waits = W(("dve", d2))
            bias_sc = -float(man) * LN2
            a3 = op("scalar", lambda e, o=qinvs, i=qes, b=-bias_sc: e.activation(
                out=o, in_=i, func=AF.Exp, bias=b, scale=-LN2), waits=exp_waits, inc="act")
            a4 = op("scalar", lambda e, o=qscs, i=qes, b=bias_sc: e.activation(
                out=o, in_=i, func=AF.Exp, bias=b, scale=LN2), inc="act")
            d3 = op("vector", lambda e, o=qvs, i0=src_ap, i1=qinvs: e.tensor_tensor(
                out=o, in0=i0, in1=i1, op=ALU.mult), waits=W(("act", a3)), inc="dve")
            g4 = op(geng, lambda e, o=qis, i=qvs: e.tensor_scalar(
                out=o, in0=i, scalar1=1.0, scalar2=0.0, op0=ALU.mult, op1=ALU.add),
                waits=W(("dve", d3)), inc=gsem)
            d5 = op("vector", lambda e, o=qcs, i0=qis, i1=qvs: e.scalar_tensor_tensor(
                out=o, in0=i0, scalar=0.5, in1=i1, op0=ALU.add, op1=ALU.is_le),
                waits=W((gsem, g4)), inc="dve")
            if signed:
                op("vector", lambda e, o=qes, i0=qis, i1=qvs: e.scalar_tensor_tensor(
                    out=o, in0=i0, scalar=-0.5, in1=i1, op0=ALU.add, op1=ALU.is_ge),
                    waits=W(("act", a4)), inc="dve")
                op("vector", lambda e, o=qcs, i0=qcs, i1=qes: e.tensor_tensor(
                    out=o, in0=i0, in1=i1, op=ALU.subtract), inc="dve")
            d6 = op("vector", lambda e, o=qvs, i0=qis, i1=qcs: e.tensor_tensor(
                out=o, in0=i0, in1=i1, op=ALU.add), inc="dve")
            d7 = op("vector", lambda e, o=dst_ap, i0=qvs, i1=qscs: e.tensor_tensor(
                out=o, in0=i0, in1=i1, op=ALU.mult), waits=W(("act", a4)), inc="dve")
            QS["dve"] = d7
            QS["gps"] = 0 if DBG_NOGPS else g4
            QS["act"] = a4
            return d7

        stage_free = [c_weffe, c_weffe]
        w2_lw = {}
        c_bh = op("vector", lambda e: e.tensor_copy(out=b2h[:], in_=b2_sb[:]),
                  waits=W(("ld", ld_w)), inc="dve")
        b2q_done = op("vector", lambda e: e.tensor_copy(out=b2q_sb[:], in_=b2h[:]),
                      inc="dve")

        lna_free = [c_cast, c_cast]
        lnb_free_st = [("sb0", 0), ("sb1", 0)]
        lnb_free_d = [c_cast, c_cast]
        hwork_free = [0, 0]
        pm2_free = {}
        pS2_free = [0, 0]
        ybf_free = 0
        featbuf_pe_free = 0
        spill_cnt = {}
        w1c_done = {}

        for s in range(DBG_SEQ):
            last_cast = 0
            for half in range(16):
                c, hh = divmod(half, 2)
                buf = half % 2
                lx = dma("sync", ln_a[buf][:, 0:1024],
                         x_feat[s, c * 128 : c * 128 + 128, hh * 1024 : hh * 1024 + 1024],
                         waits=W(("dve", lna_free[buf])), sem="la%d" % buf)
                cc = op("vector", lambda e,
                        o=featbuf[:, c * 2048 + hh * 1024 : c * 2048 + hh * 1024 + 1024],
                        i=ln_a[buf][:, 0:1024]: e.tensor_copy(out=o, in_=i),
                        waits=W(lx, ("pe", featbuf_pe_free)), inc="dve")
                lna_free[buf] = cc
                last_cast = cc
            featbuf_ready = last_cast

            pm_free = [{"dve": cnt["dve"], "act": cnt["act"]} for _ in range(2)]
            pctx_free = {"dve": cnt["dve"], "act": cnt["act"]}
            pS_free = [{"act": cnt["act"]}, {"act": cnt["act"]}]
            attn_done = 0
            if not DBG_ATTN:
                attn_done = cnt["dve"]

            def make_ctx_extract(h, pm_free, pctx_free_ref):
                def emit_ctx_extract(Hf, j, jb, segs, c_ae, jmax, last_ed1):
                    def pC_fn(e, j=j, jb=jb, segs=segs, jmax=jmax):
                        last = None
                        for (p, q) in segs:
                            last = e.matmul(out=psum_ctx[0:65, p:q],
                                            lhsT=v_aug[:, j * 66 : j * 66 + 65],
                                            rhs=PT[jb][:, p:q],
                                            start=(j == 0), stop=(j == jmax - 1),
                                            skip_group_check=True)
                        return last
                    waits_pc = W(("act", c_ae), ("dve", last_ed1))
                    if j == 0:
                        waits_pc += W(("dve", pctx_free_ref[0]["dve"]),
                                      ("act", pctx_free_ref[0]["act"]))
                    c_pc = op("tensor", pC_fn, waits=waits_pc, inc="pe")
                    t_sub = j if (Hf == 0 or j >= 8) else None
                    if t_sub is None:
                        return last_ed1
                    eb = t_sub % 2
                    off = 128 * t_sub - 1024 * Hf
                    def ed1(e, off=off):
                        return e.tensor_copy(out=ctxTa_bf[0:65, :],
                                             in_=psum_ctx[0:65, off : off + 128])
                    c_ed1 = op("vector", ed1, waits=W(("pe", c_pc)), inc="dve")
                    def ep(e, eb=eb):
                        e.matmul(out=psum_m[eb][:, 0:1],
                                 lhsT=ctxTa_bf[0:65, :],
                                 rhs=ident_bf[0:65, 64:65], start=True, stop=True)
                        return e.matmul(out=psum_m[eb][:, 64:128],
                                        lhsT=ctxTa_bf[0:64, :], rhs=wot_bf[:],
                                        start=True, stop=True)
                    c_ep = op("tensor", ep,
                              waits=W(("dve", c_ed1), ("dve", pm_free[eb]["dve"]),
                                      ("act", pm_free[eb]["act"])),
                              inc="pe")
                    op("vector", lambda e, eb=eb: e.reciprocal(
                        out=svec[:, 64:65], in_=psum_m[eb][:, 0:1]),
                        waits=W(("pe", c_ep)), inc="dve")
                    c_ed2 = op("vector", lambda e, eb=eb, t_g=t_sub, h=h:
                               e.tensor_single_scalar(
                        out=attn_bf[:, t_g * 1024 + h * 64 : t_g * 1024 + h * 64 + 64],
                        in_=psum_m[eb][:, 64:128], scalar=svec[:, 64:65], op=ALU.mult),
                        inc="dve")
                    pm_free[eb] = {"dve": c_ed2, "act": pm_free[eb]["act"]}
                    nonlocal_state["attn_done"] = c_ed2
                    return c_ed1
                return emit_ctx_extract
            nonlocal_state = {"attn_done": 0}

            for h in range(DBG_HEADS if DBG_ATTN else 0):
                if s == 0:
                    if h >= 1:
                        for m in (2 * h - 2, 2 * h - 1):
                            buf = m % 2
                            stage_free[buf] = quant_exact(
                                w1stage[buf][:, 0:512],
                                w2qT[:, m * 1024 : m * 1024 + 512],
                                512, 3, -6.0, True, first_waits=W(w2_lw[m]))
                            stage_free[buf] = quant_exact(
                                w1stage[buf][:, 512:1024],
                                w2qT[:, m * 1024 + 512 : m * 1024 + 1024],
                                512, 3, -6.0, True, first_waits=W(w2_lw[m]))
                    for m in (2 * h, 2 * h + 1):
                        buf = m % 2
                        w2_lw[m] = dma("sync", w1stage[buf][:],
                                       w2t[m * 128 : (m + 1) * 128, :],
                                       waits=W(("dve", stage_free[buf])),
                                       sem="lw%d" % buf)
                chn, bse = divmod(h, 2)
                bse *= 64
                xcol = chn * 2048

                def ones_fn(e):
                    last = None
                    for t in range(NT):
                        last = e.memset(v_aug[:, t * 66 + 64 : t * 66 + 65], 1.0)
                    return last
                ones_c = op("vector", ones_fn, waits=W(("pe", cnt["pe"])), inc="dve")

                def pslot(g, j4):
                    if g % 2 == 0:
                        return psum_ctx[:, j4 * 256 : j4 * 256 + 192]
                    return psum_m[j4 // 2][:, (j4 % 2) * 256 : (j4 % 2) * 256 + 192]
                slot_free = [pctx_free["dve"], max(pm_free[0]["dve"], pm_free[1]["dve"])]
                slot_free_act = [pctx_free["act"], max(pm_free[0]["act"], pm_free[1]["act"])]
                d1_cnts = [cnt["dve"]]
                for g in range(DBG_G):
                    def p1(e, g=g, par=h % 2, xcol=xcol):
                        last = None
                        for t in range(4 * g, 4 * g + 4):
                            last = e.matmul(out=pslot(g, t % 4),
                                            lhsT=featbuf[:, xcol + t * 128 : xcol + t * 128 + 128],
                                            rhs=wqkv[par][:], start=True, stop=True)
                        return last
                    c_p1 = op("tensor", p1,
                              waits=W(("dve", featbuf_ready),
                                      ("dve", slot_free[g % 2]),
                                      ("act", slot_free_act[g % 2]),
                                      ("dve", ones_c)),
                              inc="pe")
                    def d1(e, g=g):
                        base = pslot(g, 0)
                        t0c = 4 * g
                        st = base.tensor
                        if g % 2 == 0:
                            src3 = lambda off: psum_ctx[:, 0:1024].rearrange(
                                "p (s c) -> p s c", s=4)[:, :, off : off + 64]
                        else:
                            src3 = None
                        if g % 2 == 0:
                            e.tensor_copy(out=kc_bf[:, t0c * 64 : t0c * 64 + 256],
                                          in_=src3(0))
                            e.tensor_copy(
                                out=v_aug[:].rearrange("p (s c) -> p s c", c=66)[
                                    :, t0c : t0c + 4, 0:64],
                                in_=src3(64))
                            return e.tensor_copy(out=qc_bf[:, t0c * 64 : t0c * 64 + 256],
                                                 in_=src3(128))
                        else:
                            last = None
                            for half in range(2):
                                pm = psum_m[half][:, 0:512].rearrange(
                                    "p (s c) -> p s c", s=2)
                                tb = t0c + 2 * half
                                e.tensor_copy(out=kc_bf[:, tb * 64 : tb * 64 + 128],
                                              in_=pm[:, :, 0:64])
                                e.tensor_copy(
                                    out=v_aug[:].rearrange("p (s c) -> p s c", c=66)[
                                        :, tb : tb + 2, 0:64],
                                    in_=pm[:, :, 64:128])
                                last = e.tensor_copy(out=qc_bf[:, tb * 64 : tb * 64 + 128],
                                                     in_=pm[:, :, 128:192])
                            return last
                    c_d1 = op("vector", d1, waits=W(("pe", c_p1)), inc="dve")
                    slot_free[g % 2] = c_d1
                    d1_cnts.append(c_d1)
                pctx_free = {"dve": slot_free[0], "act": cnt["act"]}
                for pb in range(2):
                    pm_free[pb] = {"dve": slot_free[1], "act": pm_free[pb]["act"]}

                c_sqk = op("scalar", lambda e: e.activation(
                    out=qsq[:], in_=kc_bf[:], func=AF.Square),
                    waits=W(("dve", d1_cnts[-1])), inc="act")
                c_rdk = op("vector", lambda e: e.tensor_reduce(
                    out=svec[:, 0:16],
                    in_=qsq[:].rearrange("p (s c) -> p s c", c=64),
                    axis=mybir.AxisListType.X, op=ALU.add),
                    waits=W(("act", c_sqk)), inc="dve")
                c_sqq = op("scalar", lambda e: e.activation(
                    out=qsq[:], in_=qc_bf[:], func=AF.Square),
                    waits=W(("dve", c_rdk)), inc="act")
                c_rdq = op("vector", lambda e: e.tensor_reduce(
                    out=svec[:, 16:32],
                    in_=qsq[:].rearrange("p (s c) -> p s c", c=64),
                    axis=mybir.AxisListType.X, op=ALU.add),
                    waits=W(("act", c_sqq)), inc="dve")
                a1_cnts = [cnt["act"]]
                d1_cnts.append(c_rdq)

                def a2a(e):
                    e.activation(out=svec[:, 96:112], in_=svec[:, 0:16], func=AF.Ln,
                                 bias=EPS_ATTN, scale=1.0 / 64.0)
                    return e.activation(out=svec[:, 112:128], in_=svec[:, 16:32], func=AF.Ln,
                                        bias=EPS_ATTN, scale=1.0 / 64.0)
                op("scalar", a2a, waits=W(("dve", d1_cnts[-1])), inc="act")
                def a2b(e):
                    e.activation(out=svec[:, 32:48], in_=svec[:, 96:112], func=AF.Exp,
                                 bias=0.0, scale=-0.5)
                    return e.activation(out=svec[:, 48:64], in_=svec[:, 112:128], func=AF.Exp,
                                        bias=0.0, scale=-0.5)
                c_a2 = op("scalar", a2b, inc="act")

                def d2(e):
                    last = None
                    for t in range(16):
                        e.scalar_tensor_tensor(out=kc_bf[:, t * 64 : t * 64 + 64],
                                               in0=kc_bf[:, t * 64 : t * 64 + 64],
                                               scalar=svec[:, 32 + t : 33 + t],
                                               in1=wkb_sb[:], op0=ALU.mult, op1=ALU.mult)
                        last = e.scalar_tensor_tensor(out=qc_bf[:, t * 64 : t * 64 + 64],
                                                      in0=qc_bf[:, t * 64 : t * 64 + 64],
                                                      scalar=svec[:, 48 + t : 49 + t],
                                                      in1=wqb_sb[:], op0=ALU.mult, op1=ALU.mult)
                    return last
                c_d2 = op("vector", d2, waits=W(("act", c_a2)), inc="dve")

                if DBG_HSTAGE < 3:
                    attn_done = max(attn_done, c_d2)
                    continue
                tr_done = 0
                for src, dstT in ((kc_bf, kT_bf), (qc_bf, qT_bf)):
                    for g in range(4):
                        pb = g % 2
                        def p2(e, g=g, pb=pb, src=src):
                            last = None
                            for t in range(4 * g, 4 * g + 4):
                                last = e.matmul(
                                    out=psum_m[pb][0:64, (t % 4) * 128 : (t % 4) * 128 + 128],
                                    lhsT=src[:, t * 64 : t * 64 + 64],
                                    rhs=ident_bf[:], start=True, stop=True)
                            return last
                        c_p2 = op("tensor", p2,
                                  waits=W(("dve", c_d2), ("dve", pm_free[pb]["dve"]),
                                          ("act", pm_free[pb]["act"])),
                                  inc="pe")
                        def d3(e, g=g, pb=pb, dstT=dstT):
                            return e.tensor_copy(
                                out=dstT[0:64, 4 * g * 128 : 4 * g * 128 + 512],
                                in_=psum_m[pb][0:64, 0:512])
                        c_d3 = op("vector", d3, waits=W(("pe", c_p2)), inc="dve")
                        pm_free[pb] = {"dve": c_d3, "act": pm_free[pb]["act"]}
                        tr_done = c_d3

                if DBG_HSTAGE < 4:
                    attn_done = max(attn_done, tr_done)
                    continue
                kr_done = 0
                for j in range(4):
                    pb = j % 2
                    c_p3 = op("tensor", lambda e, j=j, pb=pb, par=h % 2, xcol=xcol: e.matmul(
                        out=psum_m[pb][:, 0:512], lhsT=wkrt_bf[par][:],
                        rhs=featbuf[:, xcol + j * 512 : xcol + j * 512 + 512],
                        start=True, stop=True),
                        waits=W(("dve", pm_free[pb]["dve"]), ("act", pm_free[pb]["act"])),
                        inc="pe")
                    c_d4 = op("vector", lambda e, j=j, pb=pb: e.tensor_copy(
                        out=kT_bf[64:128, j * 512 : j * 512 + 512],
                        in_=psum_m[pb][64:128, 0:512]),
                        waits=W(("pe", c_p3)), inc="dve")
                    pm_free[pb] = {"dve": c_d4, "act": pm_free[pb]["act"]}
                    kr_done = c_d4
                qr_done = 0
                for j in range(4):
                    pb = j % 2
                    c_p4 = op("tensor", lambda e, j=j, pb=pb: e.matmul(
                        out=psum_m[pb][:, 0:512], lhsT=wqrt_bf[0:64, :],
                        rhs=qT_bf[0:64, j * 512 : j * 512 + 512], start=True, stop=True),
                        waits=W(("dve", pm_free[pb]["dve"]), ("act", pm_free[pb]["act"]),
                                ("dve", tr_done)),
                        inc="pe")
                    c_d5 = op("vector", lambda e, j=j, pb=pb: e.tensor_copy(
                        out=qT_bf[64:128, j * 512 : j * 512 + 512],
                        in_=psum_m[pb][64:128, 0:512]),
                        waits=W(("pe", c_p4)), inc="dve")
                    pm_free[pb] = {"dve": c_d5, "act": pm_free[pb]["act"]}
                    qr_done = c_d5

                if DBG_HSTAGE < 5:
                    attn_done = max(attn_done, qr_done)
                    continue
                pctx_free_ref = [pctx_free]
                emit_ctx_extract = make_ctx_extract(h, pm_free, pctx_free_ref)
                last_ed1 = 0
                for Hf in range(2):
                    jmax = 8 if Hf == 0 else 16
                    pend = None
                    for j in range(jmax):
                        qlo = max(128 * j, 1024 * Hf)
                        off = qlo - 1024 * Hf
                        jb = j % 2
                        segs = []
                        p = off
                        while p < 1024:
                            q = 512 if p < 512 else 1024
                            segs.append((p, q))
                            p = q
                        has_diag = (Hf == 0) or (j >= 8)
                        def pS_fn(e, j=j, jb=jb, segs=tuple(segs), Hf=Hf, has_diag=has_diag, off=off):
                            last = None
                            for (p, q) in segs:
                                last = e.matmul(out=psum_S[jb][:, p:q],
                                                lhsT=kT_bf[:, j * 128 : j * 128 + 128],
                                                rhs=qT_bf[:, 1024 * Hf + p : 1024 * Hf + q],
                                                start=True, stop=True)
                            if has_diag:
                                last = e.matmul(out=psum_S[jb][:, off : off + 128],
                                                lhsT=maskdT_sb[:], rhs=ident_bf[:],
                                                start=False, stop=True,
                                                skip_group_check=True)
                            return last
                        c_ps = op("tensor", pS_fn,
                                  waits=W(("dve", kr_done), ("dve", qr_done),
                                          ("act", pS_free[jb]["act"]),
                                          ("dve", pS2_free[jb])),
                                  inc="pe")
                        c_ae = op("scalar", lambda e, jb=jb, off=off: e.activation(
                            out=PT[jb][:, off:1024], in_=psum_S[jb][:, off:1024],
                            func=AF.Exp, bias=-CSH, scale=0.125),
                            waits=W(("pe", c_ps)), inc="act")
                        pS_free[jb] = {"act": c_ae}

                        if pend is not None:
                            last_ed1 = emit_ctx_extract(*pend, last_ed1)
                        pend = (Hf, j, jb, tuple(segs), c_ae, jmax)
                    last_ed1 = emit_ctx_extract(*pend, last_ed1)
                    pctx_free = {"dve": cnt["dve"], "act": cnt["act"]}

            if DBG_ATTN:
                attn_done = max(attn_done, nonlocal_state["attn_done"])
            if s == 0 and DBG_ATTN and DBG_HEADS == 16 and DBG_FFN:
                for m in (30, 31):
                    buf = m % 2
                    stage_free[buf] = quant_exact(
                        w1stage[buf][:, 0:512], w2qT[:, m * 1024 : m * 1024 + 512],
                        512, 3, -6.0, True, first_waits=W(w2_lw[m]))
                    stage_free[buf] = quant_exact(
                        w1stage[buf][:, 512:1024],
                        w2qT[:, m * 1024 + 512 : m * 1024 + 1024],
                        512, 3, -6.0, True, first_waits=W(w2_lw[m]))

            if TAP == "attn":
                for t in range(NT):
                    buf = t % 2
                    cc = op("vector", lambda e, t=t, buf=buf: e.tensor_copy(
                        out=ln_b[buf][:], in_=attn_bf[:, t * 1024 : t * 1024 + 1024]),
                        waits=W(("dve", attn_done), lnb_free_st[buf],
                                ("dve", lnb_free_d[buf])),
                        inc="dve")
                    lnb_free_d[buf] = cc
                    lnb_free_st[buf] = dma("scalar",
                                           tap_out[s, t * 128 : t * 128 + 128, :],
                                           ln_b[buf][:], waits=W(("dve", cc)),
                                           sem="sb%d" % buf)

            yT_written = 0
            for t in range(NT if DBG_LN1 else 0):
                buf = t % 2
                lx = dma("sync", ln_a[buf][:], x_tok[s, t * 128 : t * 128 + 128, :],
                         waits=W(("dve", lna_free[buf])), sem="la%d" % buf)
                c_add = op("vector", lambda e, buf=buf, t=t: e.tensor_tensor(
                    out=ln_b[buf][:], in0=ln_a[buf][:],
                    in1=attn_bf[:, t * 1024 : t * 1024 + 1024], op=ALU.add),
                    waits=W(lx, ("dve", attn_done), lnb_free_st[buf],
                            ("dve", lnb_free_d[buf])),
                    inc="dve")
                lna_free[buf] = c_add
                op("vector", lambda e, buf=buf: e.reduce_sum(
                    out=svec[:, 80:81], in_=ln_b[buf][:], axis=mybir.AxisListType.X),
                    inc="dve")
                op("vector", lambda e, buf=buf: e.scalar_tensor_tensor(
                    out=ybf_tile[:], in0=ln_b[buf][:], scalar=1.0, in1=ln_b[buf][:],
                    op0=ALU.mult, op1=ALU.mult, accum_out=svec[:, 81:82]),
                    waits=W(("pe", ybf_free)), inc="dve")
                def stats_a(e):
                    e.tensor_scalar(out=svec[:, 82:83], in0=svec[:, 80:81], scalar1=1.0 / D,
                                    scalar2=0.0, op0=ALU.mult, op1=ALU.add)
                    return e.tensor_scalar(out=svec[:, 83:84], in0=svec[:, 81:82], scalar1=1.0 / D,
                                           scalar2=0.0, op0=ALU.mult, op1=ALU.add)
                op("vector", stats_a, inc="dve")
                op("vector", lambda e: e.tensor_tensor(out=svec[:, 84:85], in0=svec[:, 82:83],
                                                       in1=svec[:, 82:83], op=ALU.mult),
                   inc="dve")
                c_stats = op("vector", lambda e: e.tensor_tensor(
                    out=svec[:, 85:86], in0=svec[:, 83:84], in1=svec[:, 84:85], op=ALU.subtract),
                    inc="dve")
                c_r1 = op("scalar", lambda e: e.activation(
                    out=svec[:, 86:87], in_=svec[:, 85:86], func=AF.Ln, bias=EPS_LN, scale=1.0),
                    waits=W(("dve", c_stats)), inc="act")
                c_rstd = op("scalar", lambda e: e.activation(
                    out=svec[:, 87:88], in_=svec[:, 86:87], func=AF.Exp, bias=0.0, scale=-0.5),
                    inc="act")
                c_apply = op("vector", lambda e, buf=buf: e.tensor_scalar(
                    out=ln_b[buf][:], in0=ln_b[buf][:], scalar1=svec[:, 82:83],
                    scalar2=svec[:, 87:88], op0=ALU.subtract, op1=ALU.mult),
                    waits=W(("act", c_rstd)), inc="dve")
                if TAP == "y":
                    c_sp = dma("scalar", tap_out[s, t * 128 : t * 128 + 128, :],
                               ln_b[buf][:], waits=W(("dve", c_apply)), sem="sb%d" % buf)
                else:
                    c_sp = dma("scalar", y_spill[t * 128 : t * 128 + 128, :],
                               ln_b[buf][:], waits=W(("dve", c_apply)), sem="sb%d" % buf)
                spill_cnt[t] = c_sp
                c_ybf = op("vector", lambda e, buf=buf: e.tensor_copy(
                    out=ybf_tile[:], in_=ln_b[buf][:]),
                    waits=W(("dve", c_apply), ("pe", ybf_free)), inc="dve")
                lnb_free_st[buf] = c_sp
                lnb_free_d[buf] = c_ybf
                for grp in range(2):
                    pb = grp
                    def ptr_fn(e, grp=grp, pb=pb):
                        last = None
                        for dc in range(4 * grp, 4 * grp + 4):
                            last = e.matmul(
                                out=psum_m[pb][:, (dc % 4) * 128 : (dc % 4) * 128 + 128],
                                lhsT=ybf_tile[:, dc * 128 : dc * 128 + 128],
                                rhs=ident_bf[:], start=True, stop=True)
                        return last
                    c_ptr = op("tensor", ptr_fn,
                               waits=W(("dve", c_ybf), ("dve", pm_free[pb]["dve"]),
                                       ("act", pm_free[pb]["act"])),
                               inc="pe")
                    def dtr(e, grp=grp, pb=pb, t=t):
                        return e.tensor_copy(
                            out=featbuf[:].rearrange("p (c n) -> p c n", n=2048)[
                                :, 4 * grp : 4 * grp + 4, t * 128 : t * 128 + 128],
                            in_=psum_m[pb][:, 0:512])
                    c_dtr = op("vector", dtr, waits=W(("pe", c_ptr)), inc="dve")
                    pm_free[pb] = {"dve": c_dtr, "act": pm_free[pb]["act"]}
                    yT_written = c_dtr
                ybf_free = cnt["pe"]

            def emit_f2(th, par):
                nonlocal lnb_free_st, lnb_free_d, lna_free, ybf_free
                thc = th * 256
                last_mm2 = 0
                oq_last = 0
                for dout in range(8):
                    ob = dout % 4
                    pslice = [psum_ctx[:, 0:256], psum_ctx[:, 512:768],
                              psum_S[0][:, 0:256], psum_S[0][:, 512:768]][ob]
                    def pmm2(e, pslice=pslice, dout=dout, par=par):
                        last = None
                        for m in range(32):
                            last = e.matmul(
                                out=pslice,
                                lhsT=w2qT[:, m * 1024 + dout * 128 : m * 1024 + dout * 128 + 128],
                                rhs=hqT[par][:, m * 256 : m * 256 + 256],
                                start=(m == 0), stop=(m == 31))
                        return last
                    c_mm2 = op("tensor", pmm2,
                               waits=W(("dve", hq_done[par]),
                                       ("dve", pm2_free.get(ob, 0))),
                               inc="pe")
                    last_mm2 = c_mm2
                    wslot = dout % 2
                    c_ev = op("vector", lambda e, pslice=pslice, dout=dout, wslot=wslot:
                              e.tensor_scalar(out=owork[:, wslot * 256 : wslot * 256 + 256],
                                              in0=pslice,
                                              scalar1=b2q_sb[:, dout : dout + 1],
                                              scalar2=0.0, op0=ALU.add, op1=ALU.add),
                              waits=W(("pe", c_mm2), ("dve", oq_free[wslot])), inc="dve")
                    c_oq = op("vector", lambda e, wslot=wslot, dout=dout: e.tensor_copy(
                        out=oq_all[:, dout * 256 : dout * 256 + 256],
                        in_=owork[:, wslot * 256 : wslot * 256 + 256]), inc="dve")
                    oq_free[wslot] = c_oq
                    pm2_free[ob] = c_oq
                    oq_last = c_oq
                hq_rd[par] = last_mm2
                pS2_free[0] = max(pS2_free[0], oq_last)
                for tk in range(2):
                    tb = th * 2 + tk
                    def ptr2(e, tk=tk):
                        last = None
                        for dout in range(8):
                            last = e.matmul(
                                out=psum_m[dout // 4][:, (dout % 4) * 128 : (dout % 4) * 128 + 128],
                                lhsT=oq_all[:, dout * 256 + tk * 128 : dout * 256 + tk * 128 + 128],
                                rhs=ident_h[:], start=True, stop=True)
                        return last
                    c_tr2 = op("tensor", ptr2,
                               waits=W(("dve", oq_last),
                                       ("act", relu_cnt[0]), ("act", relu_cnt[1]),
                                       ("act", relu_cnt[2]), ("act", relu_cnt[3]),
                                       ("dve", ps2_tr_free[0])),
                               inc="pe")
                    ly = dma("scalar", ln_a[tk][:], y_spill[tb * 128 : tb * 128 + 128, :],
                             waits=W(("dve", lna_free[tk]), spill_cnt[tb]),
                             sem="la%d" % tk)
                    c_s2 = op("vector", lambda e, tk=tk: e.scalar_tensor_tensor(
                        out=ln_b[tk][:, 0:512], in0=psum_m[0][:, 0:512], scalar=1.0,
                        in1=ln_a[tk][:, 0:512], op0=ALU.mult, op1=ALU.add),
                        waits=W(("pe", c_tr2), ly, lnb_free_st[tk],
                                ("dve", lnb_free_d[tk])),
                        inc="dve")
                    c_s2b = op("vector", lambda e, tk=tk: e.scalar_tensor_tensor(
                        out=ln_b[tk][:, 512:1024], in0=psum_m[1][:, 0:512], scalar=1.0,
                        in1=ln_a[tk][:, 512:1024], op0=ALU.mult, op1=ALU.add),
                        inc="dve")
                    ps2_tr_free[0] = c_s2b
                    lna_free[tk] = c_s2b
                    op("vector", lambda e, tk=tk: e.reduce_sum(
                        out=svec[:, 88:89], in_=ln_b[tk][:], axis=mybir.AxisListType.X),
                        inc="dve")
                    c_sq2 = op("vector", lambda e, tk=tk: e.scalar_tensor_tensor(
                        out=ybf_tile[:], in0=ln_b[tk][:], scalar=1.0, in1=ln_b[tk][:],
                        op0=ALU.mult, op1=ALU.mult, accum_out=svec[:, 89:90]),
                        waits=W(("pe", ybf_free)), inc="dve")
                    def stats2_a(e):
                        e.tensor_scalar(out=svec[:, 90:91], in0=svec[:, 88:89],
                                        scalar1=1.0 / D, scalar2=0.0, op0=ALU.mult, op1=ALU.add)
                        return e.tensor_scalar(out=svec[:, 91:92], in0=svec[:, 89:90],
                                               scalar1=1.0 / D, scalar2=0.0, op0=ALU.mult, op1=ALU.add)
                    op("vector", stats2_a, inc="dve")
                    op("vector", lambda e: e.tensor_tensor(out=svec[:, 92:93], in0=svec[:, 90:91],
                                                           in1=svec[:, 90:91], op=ALU.mult),
                       inc="dve")
                    c_sts = op("vector", lambda e: e.tensor_tensor(
                        out=svec[:, 93:94], in0=svec[:, 91:92], in1=svec[:, 92:93],
                        op=ALU.subtract), inc="dve")
                    op("scalar", lambda e: e.activation(
                        out=svec[:, 94:95], in_=svec[:, 93:94], func=AF.Ln,
                        bias=EPS_LN, scale=1.0), waits=W(("dve", c_sts)), inc="act")
                    c_rs2 = op("scalar", lambda e: e.activation(
                        out=svec[:, 95:96], in_=svec[:, 94:95], func=AF.Exp,
                        bias=0.0, scale=-0.5), inc="act")
                    c_ap2 = op("vector", lambda e, tk=tk: e.tensor_scalar(
                        out=ln_b[tk][:], in0=ln_b[tk][:], scalar1=svec[:, 90:91],
                        scalar2=svec[:, 95:96], op0=ALU.subtract, op1=ALU.mult),
                        waits=W(("act", c_rs2)), inc="dve")
                    lnb_free_d[tk] = c_ap2
                    lnb_free_st[tk] = dma("scalar",
                                          out_tok[s, tb * 128 : tb * 128 + 128, :],
                                          ln_b[tk][:], waits=W(("dve", c_ap2)),
                                          sem="sb%d" % tk)
            oq_free = [0, 0]
            ps2_tr_free = [0]

            w1s_free = [cnt["dve"], cnt["dve"]]
            w1st_ld = {0: None, 1: None}
            hwork_free = [0, 0, 0, 0]
            relu_cnt = [0, 0, 0, 0]
            hq_done = [0, 0]
            hq_rd = [0, 0]
            pend_mm2 = None
            for th in range(8 if DBG_FFN else 0):
                thc = th * 256
                par = th % 2
                last_hq = 0
                for m in range(32):
                    wbuf = m % 2
                    if s == 0 and th == 0:
                        lw = dma("sync", w1stage[wbuf][:], w1tl[m],
                                 waits=W(("dve", w1s_free[wbuf])), sem="lw%d" % wbuf)
                        c_cw = op("vector", lambda e, wbuf=wbuf: e.tensor_copy(
                            out=w1bf[wbuf][:], in_=w1stage[wbuf][:]),
                            waits=W(lw, w1c_done.get(m - 2, ("wc0", 0)),
                                    ("pe", w1st_ld[wbuf] or 0)), inc="dve")
                        w1c_done[m] = dma("scalar", w1bf_dram[m], w1bf[wbuf][:],
                                          waits=W(("dve", c_cw)), sem="wc%d" % wbuf)
                        w1s_free[wbuf] = c_cw
                    else:
                        lw = dma("sync", w1bf[wbuf][:], w1bf_dram[m],
                                 waits=W(("pe", w1st_ld[wbuf] or 0),
                                         ("wc%d" % wbuf, cnt["wc%d" % wbuf])),
                                 sem="lw%d" % wbuf)
                        c_cw = None
                    slot = m % 4
                    def mm1_slot(slot):
                        return [psum_m[0][:, 0:256], psum_m[1][:, 0:256],
                                psum_S[1][:, 0:256], psum_S[1][:, 512:768]][slot]
                    def pmm1(e, wbuf=wbuf, slot=slot, thc=thc):
                        last = None
                        for dcc in range(8):
                            last = e.matmul(
                                out=mm1_slot(slot),
                                lhsT=w1bf[wbuf][:, dcc * 128 : dcc * 128 + 128],
                                rhs=featbuf[:, dcc * 2048 + thc : dcc * 2048 + thc + 256],
                                start=(dcc == 0), stop=(dcc == 7))
                        return last
                    mm1_waits = W(("dve", yT_written), ("act", relu_cnt[slot]),
                                  ("pe", hq_rd[par]), ("dve", ps2_tr_free[0]))
                    if c_cw is not None:
                        mm1_waits += W(("dve", c_cw))
                    else:
                        mm1_waits += W(lw)
                    c_mm1 = op("tensor", pmm1, waits=mm1_waits, inc="pe")
                    w1st_ld[wbuf] = c_mm1
                    c_rel = op("scalar", lambda e, slot=slot, m=m: e.activation(
                        out=hwork[:, slot * 256 : slot * 256 + 256],
                        in_=mm1_slot(slot),
                        func=AF.Relu, bias=b1_sb[:, m : m + 1], scale=1.0),
                        waits=W(("pe", c_mm1), ("dve", hwork_free[slot])),
                        inc="act")
                    relu_cnt[slot] = c_rel
                    if m % 2 == 1:
                        p0 = (m - 1) % 4
                        c_hq = quant_exact(
                            hwork[:, p0 * 256 : p0 * 256 + 512],
                            hqT[par][:, (m - 1) * 256 : (m - 1) * 256 + 512],
                            512, 3, -6.0, False, first_waits=W(("act", c_rel)))
                        hwork_free[p0] = c_hq
                        hwork_free[p0 + 1] = c_hq
                        last_hq = c_hq
                hq_done[par] = last_hq

                if pend_mm2 is not None:
                    emit_f2(*pend_mm2)
                pend_mm2 = (th, par)
            if DBG_FFN:
                emit_f2(*pend_mm2)
            featbuf_pe_free = cnt["pe"]

        op("sync", lambda e: e.nop(),
           waits=W(("sb0", cnt["sb0"]), ("sb1", cnt["sb1"])))

        ENGS = ["sync", "scalar", "vector", "tensor", "gpsimd"]
        emitted_waits = {e: {} for e in ENGS}

        OWN_SEM = {"vector": "dve", "scalar": "act", "gpsimd": "gps"}

        def replay(eng_name):
            ew = emitted_waits[eng_name]
            own = OWN_SEM.get(eng_name)
            def body(eng):
                for (en, fn, waits, inc, amt) in OPS:
                    if en != eng_name:
                        continue
                    allw = list(waits)
                    if own is not None and inc == own:
                        allw.append((own, cnt_at[id(fn)]))
                    for (sname, val) in allw:
                        if val > ew.get(sname, 0):
                            eng.wait_ge(sems[sname], val)
                            ew[sname] = val
                    instr = fn(eng)
                    if inc is not None:
                        assert instr is not None, f"op on {eng_name} returned no instruction"
                        instr.then_inc(sems[inc], amt, skip_validation=(amt == 16))
            return body

        for en in ENGS:
            if any(o[0] == en for o in OPS):
                getattr(block, en)(replay(en))

    return nc


def _host_inputs(inputs, core):
    x = np.asarray(inputs["L0_attn_x"])
    xs = np.ascontiguousarray(x[2 * core : 2 * core + 2])
    xf = np.ascontiguousarray(xs.transpose(0, 2, 1))
    W_DQ = np.asarray(inputs["L0_attn_W_DQ"]); W_UQ = np.asarray(inputs["L0_attn_W_UQ"])
    W_DKV = np.asarray(inputs["L0_attn_W_DKV"]); W_UKV = np.asarray(inputs["L0_attn_W_UKV"])
    W_QR = np.asarray(inputs["L0_attn_W_QR"]); W_KR = np.asarray(inputs["L0_attn_W_KR"])
    W_O = np.asarray(inputs["L0_attn_W_O"])
    qnw = np.asarray(inputs["L0_attn_q_norm_w"]); knw = np.asarray(inputs["L0_attn_k_norm_w"])
    w1 = np.asarray(inputs["L0_ffn_w1"]); b1 = np.asarray(inputs["L0_ffn_b1"])
    w2 = np.asarray(inputs["L0_ffn_w2"]); b2 = np.asarray(inputs["L0_ffn_b2"])

    tile_pc = lambda a, P=128: np.ascontiguousarray(
        a.reshape(a.shape[0] // P, P, a.shape[1]).transpose(1, 0, 2).reshape(P, -1))
    def pad_rt(wt):
        half = np.concatenate([np.zeros((64, 64), np.float32),
                               np.ascontiguousarray(wt.T)], axis=1)
        return np.concatenate([half, half], axis=0)

    def par_rt(wt, par):
        out = np.zeros((128, 128), np.float32)
        out[par * 64 : par * 64 + 64, 64:128] = np.ascontiguousarray(wt.T)
        return out
    d = {
        "x_tok": xs,
        "x_feat": xf,
        "wdq": tile_pc(W_DQ),
        "wuqt": tile_pc(np.ascontiguousarray(W_UQ.T)),
        "wdkv": tile_pc(W_DKV),
        "wukvt": tile_pc(np.ascontiguousarray(W_UKV.T)),
        "wqrt_p": pad_rt(W_QR),
        "wkrt_e": par_rt(W_KR, 0),
        "wkrt_o": par_rt(W_KR, 1),
        "wot": np.ascontiguousarray(W_O.T),
        "wq_b": np.ascontiguousarray(np.broadcast_to(qnw, (128, 64))),
        "wk_b": np.ascontiguousarray(np.broadcast_to(knw, (128, 64))),
        "w1tl": np.ascontiguousarray(
            w1.T.reshape(8, 128, 32, 128).transpose(2, 1, 0, 3).reshape(32, 128, 1024)),
        "w2t": np.ascontiguousarray(w2.T),
        "b1_t": np.ascontiguousarray(b1.reshape(32, 128).T),
        "b2_t": np.ascontiguousarray(b2.reshape(8, 128).T),
        "identf": np.eye(128, dtype=np.float32),
    }
    d = {k: np.ascontiguousarray(np.asarray(v, dtype=np.float32)) for k, v in d.items()}
    import ml_dtypes
    mk = np.where(np.arange(128)[None, :] >= np.arange(128)[:, None],
                  np.float32(0.0), np.float32(-1e30)).astype(ml_dtypes.bfloat16)
    d["maskd"] = np.ascontiguousarray(mk)
    d["maskdT"] = np.ascontiguousarray(mk.T)
    return d


def kernel(**inputs):
    nc = build_program()
    in_maps = [_host_inputs(inputs, c) for c in range(8)]
    res = run_bass_kernel_spmd(nc, in_maps, core_ids=list(range(8)))
    out = np.concatenate([r["out_tok"] for r in res.results], axis=0)
    if TAP is not None:
        kernel.tap = np.concatenate([r["tap"] for r in res.results], axis=0)
    return out


# version 05
# speedup vs baseline: 1.3639x; 1.0213x over previous
import contextlib
import numpy as np
import concourse.bass as bass
import concourse.mybir as mybir
from concourse.bass_utils import run_bass_kernel_spmd

F32 = mybir.dt.float32
BF16 = mybir.dt.bfloat16
F16 = mybir.dt.float16
F8 = mybir.dt.float8e4
I32 = mybir.dt.int32
AF = mybir.ActivationFunctionType
ALU = mybir.AluOpType

LN2 = 0.6931471805599453
INVLN2 = 1.4426950408889634
EPS_ATTN = 1.1920929e-07
EPS_LN = 1e-5
TINY = 1e-38
CSH = 20.0
T = 2048
D = 1024
NT = 16
NSEQ = 2

TAP = None
import os
DBG_HEADS = int(os.environ.get("MK_HEADS", "16"))
DBG_SEQ = int(os.environ.get("MK_SEQ", str(NSEQ)))
DBG_FFN = os.environ.get("MK_FFN", "1") == "1"
DBG_LN1 = os.environ.get("MK_LN1", "1") == "1"
DBG_ATTN = os.environ.get("MK_ATTN", "1") == "1"
DBG_HSTAGE = int(os.environ.get("MK_HSTAGE", "9"))
DBG_G = int(os.environ.get("MK_G", "4"))
DBG_NOA1 = os.environ.get("MK_NOA1", "0") == "1"
DBG_NOD1 = os.environ.get("MK_NOD1", "0") == "1"
DBG_D1SEL = int(os.environ.get("MK_D1SEL", "7"))
DBG_NOGPS = os.environ.get("MK_NOGPS", "0") == "1"
DBG_SERIAL = os.environ.get("MK_SERIAL", "0") != "0"


def build_program():
    nc = bass.Bass("TRN2", target_bir_lowering=False, debug=False)

    x_tok = nc.dram_tensor("x_tok", [NSEQ, T, D], F32, kind="ExternalInput").ap()
    x_feat = nc.dram_tensor("x_feat", [NSEQ, D, T], F32, kind="ExternalInput").ap()
    wdq = nc.dram_tensor("wdq", [128, 256], F32, kind="ExternalInput").ap()
    wuqt = nc.dram_tensor("wuqt", [128, 256], F32, kind="ExternalInput").ap()
    wdkv = nc.dram_tensor("wdkv", [128, 512], F32, kind="ExternalInput").ap()
    wukvt = nc.dram_tensor("wukvt", [128, 1024], F32, kind="ExternalInput").ap()
    wqrt_p = nc.dram_tensor("wqrt_p", [128, 128], F32, kind="ExternalInput").ap()
    wkrt_e = nc.dram_tensor("wkrt_e", [128, 128], F32, kind="ExternalInput").ap()
    wkrt_o = nc.dram_tensor("wkrt_o", [128, 128], F32, kind="ExternalInput").ap()
    wot = nc.dram_tensor("wot", [64, 64], F32, kind="ExternalInput").ap()
    wq_b = nc.dram_tensor("wq_b", [128, 64], F32, kind="ExternalInput").ap()
    wk_b = nc.dram_tensor("wk_b", [128, 64], F32, kind="ExternalInput").ap()
    w1tl = nc.dram_tensor("w1tl", [32, 128, 1024], F32, kind="ExternalInput").ap()
    w2t = nc.dram_tensor("w2t", [4096, D], F32, kind="ExternalInput").ap()
    b1_t = nc.dram_tensor("b1_t", [128, 32], F32, kind="ExternalInput").ap()
    b2_t = nc.dram_tensor("b2_t", [128, 8], F32, kind="ExternalInput").ap()
    identf = nc.dram_tensor("identf", [128, 128], F32, kind="ExternalInput").ap()
    maskd = nc.dram_tensor("maskd", [128, 128], BF16, kind="ExternalInput").ap()
    maskdT = nc.dram_tensor("maskdT", [128, 128], BF16, kind="ExternalInput").ap()
    out_tok = nc.dram_tensor("out_tok", [NSEQ, T, D], F32, kind="ExternalOutput").ap()
    y_spill = nc.dram_tensor("y_spill", [T, D], F32, kind="Internal").ap()
    weff_scr = nc.dram_tensor("weff_scr", [64, 192], F32, kind="Internal").ap()
    w1bf_dram = nc.dram_tensor("w1bf_dram", [32, 128, 1024], BF16, kind="Internal").ap()
    tap_out = None
    if TAP is not None:
        tap_out = nc.dram_tensor("tap", [NSEQ, T, D], F32, kind="ExternalOutput").ap()

    ctx = contextlib.ExitStack()
    with ctx:
        sb = lambda name, shape, dt: ctx.enter_context(nc.sbuf_tensor(name, shape, dt))
        psb = lambda name, shape: ctx.enter_context(nc.psum_tensor(name, shape, F32))

        cpool = sb("cpool", [128, 16], F32)
        ident_bf = sb("ident_bf", [128, 128], BF16)
        ident_h = sb("ident_h", [128, 128], F16)
        maskd_sb = sb("maskd_sb", [128, 128], BF16)
        maskdT_sb = sb("maskdT_sb", [128, 128], BF16)
        wqb_sb = sb("wqb_sb", [128, 64], F32)
        wkb_sb = sb("wkb_sb", [128, 64], F32)
        wdq_bf = sb("wdq_bf", [128, 256], BF16)
        wuqt_bf = sb("wuqt_bf", [128, 256], BF16)
        wdkv_bf = sb("wdkv_bf", [128, 512], BF16)
        wukvt_bf = sb("wukvt_bf", [128, 1024], BF16)
        wqrt_bf = sb("wqrt_bf", [128, 128], BF16)
        wkrt_bf = [sb("wkrt_bf0", [128, 128], BF16), sb("wkrt_bf1", [128, 128], BF16)]
        wot_bf = sb("wot_bf", [64, 64], BF16)
        wqkv = [sb("wqkv0", [128, 192], BF16), sb("wqkv1", [128, 192], BF16)]
        b1_sb = sb("b1_sb", [128, 32], F32)
        b2_sb = sb("b2_sb", [128, 8], F32)
        b2q_sb = sb("b2q_sb", [128, 8], F32)

        featbuf = sb("featbuf", [128, 8 * T], BF16)
        attn_bf = sb("attn_bf", [128, NT * D], BF16)
        w2qT = sb("w2qT", [128, 32 * D], F8)
        hqT = [sb("hqT0", [128, 32 * 256], F8), sb("hqT1", [128, 32 * 256], F8)]

        kT_bf = sb("kT_bf", [128, T], BF16)
        qT_bf = sb("qT_bf", [128, T], BF16)
        v_aug = [sb("v_aug0", [128, NT * 66], BF16), sb("v_aug1", [128, NT * 66], BF16)]
        kc_bf = sb("kc_bf", [128, NT * 64], BF16)
        qc_bf = sb("qc_bf", [128, NT * 64], BF16)
        svec = sb("svec", [128, 128], F32)
        PT = [sb("PT0", [128, 1024], BF16), sb("PT1", [128, 1024], BF16)]
        ctxTa_bf = sb("ctxTa_bf", [128, 128], BF16)
        qsq = sb("qsq", [128, 1024], BF16)

        ln_a = [sb("ln_a0", [128, D], F32), sb("ln_a1", [128, D], F32)]
        ln_b = [sb("ln_b0", [128, D], F32), sb("ln_b1", [128, D], F32)]
        ybf_tile = sb("ybf_tile", [128, D], BF16)
        w1stage = [sb("w1stage0", [128, 1024], F32), sb("w1stage1", [128, 1024], F32)]
        w1bf = [sb("w1bf0", [128, 1024], BF16), sb("w1bf1", [128, 1024], BF16)]
        hwork = sb("hwork", [128, 1024], F32)
        ql = [sb("ql0", [128, 512], F32), sb("ql1", [128, 512], F32)]
        qi = [sb("qi0", [128, 512], I32), sb("qi1", [128, 512], I32)]
        qsc = [sb("qsc0", [128, 512], F32), sb("qsc1", [128, 512], F32)]
        qinv = [sb("qinv0", [128, 512], F32), sb("qinv1", [128, 512], F32)]
        qv = [sb("qv0", [128, 512], F32), sb("qv1", [128, 512], F32)]
        qc = [sb("qc0", [128, 512], F32), sb("qc1", [128, 512], F32)]
        b2h = sb("b2h", [128, 8], F16)
        oq_all = sb("oq_all", [128, 2048], F16)
        owork = sb("owork", [128, 512], F32)

        psum_S = [psb("psum_S0", [128, 1024]), psb("psum_S1", [128, 1024])]
        psum_ctx = psb("psum_ctx", [128, 1024])
        psum_m = [psb("psum_m0", [128, 512]), psb("psum_m1", [128, 512])]

        CONSTS = sorted({TINY, -3 * LN2, 3 * LN2, LN2, -LN2, EPS_ATTN,
                         EPS_LN, 0.125, -CSH, -0.5, -1.0, 1.0 / 64.0, 0.5, 1.0, 0.0})
        for j, v in enumerate(CONSTS):
            nc.const_aps.register(F32, float(v), cpool[:, j : j + 1])

        sems = {}
        for s in ["ld", "lw0", "lw1", "la0", "la1", "sb0", "sb1", "wc0", "wc1",
                  "pe", "act", "dve", "gps"]:
            sems[s] = ctx.enter_context(nc.semaphore("sem_" + s))
        cnt = {s: 0 for s in sems}

        block = ctx.enter_context(nc.Block())
        OPS = []

        cnt_at = {}

        def op(eng, fn, waits=(), inc=None, amt=1):
            waits = [(sn, v) for (sn, v) in waits if v > 0]
            if DBG_SERIAL:
                mode = os.environ.get("MK_SERIAL", "0")
                DMAS = ("ld", "lw0", "lw1", "la0", "la1", "sb0", "sb1")
                if mode == "1":
                    waits = [(sn, v) for (sn, v) in cnt.items() if v > 0]
                elif mode == "2":
                    if eng in ("sync", "scalar") and fn.__name__ == "<lambda>" and inc in DMAS:
                        waits = waits + [(sn, cnt[sn]) for sn in ("pe", "act", "dve", "gps") if cnt[sn] > 0]
                    else:
                        waits = waits + [(sn, cnt[sn]) for sn in DMAS if cnt[sn] > 0]
                elif mode == "3":
                    if eng == "tensor":
                        waits = waits + [(sn, cnt[sn]) for sn in ("act", "dve", "gps") if cnt[sn] > 0]
                    elif eng in ("vector", "scalar", "gpsimd"):
                        waits = waits + [("pe", cnt["pe"])] if cnt["pe"] > 0 else waits
                elif mode == "4":
                    if eng == "vector" and cnt["act"] > 0:
                        waits = waits + [("act", cnt["act"])]
                    elif eng == "scalar" and inc == "act" and cnt["dve"] > 0:
                        waits = waits + [("dve", cnt["dve"])]
            OPS.append((eng, fn, waits, inc, amt))
            if inc is not None:
                cnt_at[id(fn)] = cnt[inc]
                cnt[inc] += amt
                return cnt[inc]
            return None

        def dma(queue, out_ap, in_ap, waits=(), sem="ld"):
            c = op(queue, lambda e, o=out_ap, i=in_ap: e.dma_start(out=o, in_=i),
                   waits=waits, inc=sem, amt=16)
            return (sem, c)

        W = lambda *pairs: list(pairs)

        def memset_consts(e):
            last = None
            for j, v in enumerate(CONSTS):
                last = e.memset(cpool[:, j : j + 1], float(v))
            return last
        op("vector", memset_consts, inc="dve")

        dma("sync", maskd_sb[:], maskd)
        dma("sync", maskdT_sb[:], maskdT)
        dma("sync", wqb_sb[:], wq_b)
        dma("sync", wkb_sb[:], wk_b)
        dma("sync", b1_sb[:], b1_t)
        dma("sync", b2_sb[:], b2_t)
        dma("sync", w1stage[0][:, 0:128], identf)
        dma("sync", w1stage[1][:, 0:256], wdq)
        dma("sync", w1stage[1][:, 256:512], wuqt)
        dma("sync", ln_a[0][:, 0:512], wdkv)
        dma("sync", ln_a[1][:, 0:1024], wukvt)
        dma("sync", ln_b[0][:, 0:128], wqrt_p)
        dma("sync", ln_b[0][:, 128:256], wkrt_e)
        dma("sync", ln_b[0][:, 320:448], wkrt_o)
        ld_w = dma("sync", ln_b[0][0:64, 256:320], wot)[1]

        def cast_weights(e):
            e.tensor_copy(out=ident_bf[:], in_=w1stage[0][:, 0:128])
            e.tensor_copy(out=ident_h[:], in_=w1stage[0][:, 0:128])
            e.tensor_copy(out=wdq_bf[:], in_=w1stage[1][:, 0:256])
            e.tensor_copy(out=wuqt_bf[:], in_=w1stage[1][:, 256:512])
            e.tensor_copy(out=wdkv_bf[:], in_=ln_a[0][:, 0:512])
            e.tensor_copy(out=wukvt_bf[:], in_=ln_a[1][:, 0:1024])
            e.tensor_copy(out=wqrt_bf[:], in_=ln_b[0][:, 0:128])
            e.tensor_copy(out=wkrt_bf[0][:], in_=ln_b[0][:, 128:256])
            e.tensor_copy(out=wkrt_bf[1][:], in_=ln_b[0][:, 320:448])
            return e.tensor_copy(out=wot_bf[:], in_=ln_b[0][0:64, 256:320])
        c_cast = op("vector", cast_weights, waits=W(("ld", ld_w)), inc="dve")

        def weff(e):
            for c in range(4):
                e.matmul(out=psum_m[0][0:64, 0:64], lhsT=wdq_bf[:, c * 64 : c * 64 + 64],
                         rhs=wuqt_bf[:, c * 64 : c * 64 + 64], start=(c == 0), stop=(c == 3))
            last = None
            for c in range(8):
                last = e.matmul(out=psum_m[0][0:64, 64:192],
                                lhsT=wdkv_bf[:, c * 64 : c * 64 + 64],
                                rhs=wukvt_bf[:, c * 128 : c * 128 + 128],
                                start=(c == 0), stop=(c == 7))
            return last
        c_weff = op("tensor", weff, waits=W(("dve", c_cast)), inc="pe")

        weff_f32 = ln_b[1]
        def weff_evac(e):
            e.memset(wqkv[0][:], 0.0)
            e.memset(wqkv[1][:], 0.0)
            return e.tensor_copy(out=weff_f32[0:64, 0:192], in_=psum_m[0][0:64, 0:192])
        c_we1 = op("vector", weff_evac, waits=W(("pe", c_weff)), inc="dve")
        c_we2 = dma("sync", weff_scr, weff_f32[0:64, 0:192], waits=W(("dve", c_we1)))
        c_we3 = dma("sync", weff_f32[64:128, 0:192], weff_scr, waits=W(c_we2))
        def weff_cast(e):
            e.tensor_copy(out=wqkv[0][0:64, 128:192], in_=weff_f32[0:64, 0:64])
            e.tensor_copy(out=wqkv[0][0:64, 0:128], in_=weff_f32[0:64, 64:192])
            e.tensor_copy(out=wqkv[1][64:128, 128:192], in_=weff_f32[64:128, 0:64])
            return e.tensor_copy(out=wqkv[1][64:128, 0:128], in_=weff_f32[64:128, 64:192])
        c_weffe = op("vector", weff_cast, waits=W(c_we3), inc="dve")

        QSTATE = [{"dve": 0, "gps": 0, "act": 0}, {"dve": 0, "gps": 0, "act": 0}]
        QPAR = [0]

        def quant_exact(src_ap, dst_ap, n, man, clamp, signed, first_waits=()):
            par = QPAR[0]
            QPAR[0] ^= 1
            QS = QSTATE[par]
            qls, qis = ql[par][:, 0:n], qi[par][:, 0:n]
            qscs, qinvs = qsc[par][:, 0:n], qinv[par][:, 0:n]
            qvs, qcs = qv[par][:, 0:n], qc[par][:, 0:n]
            qes = qls
            qd = qvs
            fw = list(first_waits) + [("dve", QS["dve"]), ("gps", QS["gps"]),
                                      ("act", QS["act"])]
            if signed:
                a1 = op("scalar", lambda e, sp=src_ap, o=qd: e.activation(
                    out=o, in_=sp, func=AF.Square), waits=fw, inc="act")
                a2 = op("scalar", lambda e, o=qls, i=qd: e.activation(
                    out=o, in_=i, func=AF.Ln, bias=TINY, scale=1.0),
                    waits=W(("act", a1)), inc="act")
                mulc, backc = INVLN2 / 2.0, 2.0 * LN2
            else:
                a2 = op("scalar", lambda e, sp=src_ap, o=qls: e.activation(
                    out=o, in_=sp, func=AF.Ln, bias=TINY, scale=1.0), waits=fw, inc="act")
                mulc, backc = INVLN2, LN2
            geng, gsem = ("vector", "dve") if DBG_NOGPS else ("gpsimd", "gps")
            g1 = op(geng, lambda e, o=qis, i=qls, m=mulc: e.tensor_scalar(
                out=o, in0=i, scalar1=m, scalar2=0.0, op0=ALU.mult, op1=ALU.add),
                waits=W(("act", a2), ("dve", QS["dve"])), inc=gsem)
            d1 = op("vector", lambda e, o=qcs, i0=qis, i1=qls, b=backc: e.scalar_tensor_tensor(
                out=o, in0=i0, scalar=b, in1=i1, op0=ALU.mult, op1=ALU.is_gt),
                waits=W((gsem, g1)), inc="dve")
            d2 = op("vector", lambda e, o=qes, i0=qcs, i1=qis: e.scalar_tensor_tensor(
                out=o, in0=i0, scalar=-1.0, in1=i1, op0=ALU.mult, op1=ALU.add),
                inc="dve")
            if clamp is not None:
                d2 = op(geng, lambda e, o=qes, c=float(clamp): e.tensor_scalar(
                    out=o, in0=o, scalar1=c, scalar2=0.0, op0=ALU.max, op1=ALU.add),
                    waits=W(("dve", d2)), inc=gsem)
                exp_waits = W((gsem, d2))
            else:
                exp_waits = W(("dve", d2))
            bias_sc = -float(man) * LN2
            a3 = op("scalar", lambda e, o=qinvs, i=qes, b=-bias_sc: e.activation(
                out=o, in_=i, func=AF.Exp, bias=b, scale=-LN2), waits=exp_waits, inc="act")
            a4 = op("scalar", lambda e, o=qscs, i=qes, b=bias_sc: e.activation(
                out=o, in_=i, func=AF.Exp, bias=b, scale=LN2), inc="act")
            d3 = op("vector", lambda e, o=qvs, i0=src_ap, i1=qinvs: e.tensor_tensor(
                out=o, in0=i0, in1=i1, op=ALU.mult), waits=W(("act", a3)), inc="dve")
            g4 = op(geng, lambda e, o=qis, i=qvs: e.tensor_scalar(
                out=o, in0=i, scalar1=1.0, scalar2=0.0, op0=ALU.mult, op1=ALU.add),
                waits=W(("dve", d3)), inc=gsem)
            d5 = op("vector", lambda e, o=qcs, i0=qis, i1=qvs: e.scalar_tensor_tensor(
                out=o, in0=i0, scalar=0.5, in1=i1, op0=ALU.add, op1=ALU.is_le),
                waits=W((gsem, g4)), inc="dve")
            if signed:
                op("vector", lambda e, o=qes, i0=qis, i1=qvs: e.scalar_tensor_tensor(
                    out=o, in0=i0, scalar=-0.5, in1=i1, op0=ALU.add, op1=ALU.is_ge),
                    waits=W(("act", a4)), inc="dve")
                op("vector", lambda e, o=qcs, i0=qcs, i1=qes: e.tensor_tensor(
                    out=o, in0=i0, in1=i1, op=ALU.subtract), inc="dve")
            d6 = op("vector", lambda e, o=qvs, i0=qis, i1=qcs: e.tensor_tensor(
                out=o, in0=i0, in1=i1, op=ALU.add), inc="dve")
            d7 = op("vector", lambda e, o=dst_ap, i0=qvs, i1=qscs: e.tensor_tensor(
                out=o, in0=i0, in1=i1, op=ALU.mult), waits=W(("act", a4)), inc="dve")
            QS["dve"] = d7
            QS["gps"] = 0 if DBG_NOGPS else g4
            QS["act"] = a4
            return d7

        stage_free = [c_weffe, c_weffe]
        w2_lw = {}
        c_bh = op("vector", lambda e: e.tensor_copy(out=b2h[:], in_=b2_sb[:]),
                  waits=W(("ld", ld_w)), inc="dve")
        b2q_done = op("vector", lambda e: e.tensor_copy(out=b2q_sb[:], in_=b2h[:]),
                      inc="dve")

        lna_free = [c_cast, c_cast]
        lnb_free_st = [("sb0", 0), ("sb1", 0)]
        lnb_free_d = [c_cast, c_cast]
        hwork_free = [0, 0]
        pm2_free = {}
        pS2_free = [0, 0]
        ybf_free = 0
        featbuf_pe_free = 0
        spill_cnt = {}
        w1c_done = {}

        for s in range(DBG_SEQ):
            last_cast = 0
            for half in range(16):
                c, hh = divmod(half, 2)
                buf = half % 2
                lx = dma("sync", ln_a[buf][:, 0:1024],
                         x_feat[s, c * 128 : c * 128 + 128, hh * 1024 : hh * 1024 + 1024],
                         waits=W(("dve", lna_free[buf])), sem="la%d" % buf)
                cc = op("vector", lambda e,
                        o=featbuf[:, c * 2048 + hh * 1024 : c * 2048 + hh * 1024 + 1024],
                        i=ln_a[buf][:, 0:1024]: e.tensor_copy(out=o, in_=i),
                        waits=W(lx, ("pe", featbuf_pe_free)), inc="dve")
                lna_free[buf] = cc
                last_cast = cc
            featbuf_ready = last_cast

            pm_free = [{"dve": cnt["dve"], "act": cnt["act"]} for _ in range(2)]
            pctx_free = {"dve": cnt["dve"], "act": cnt["act"]}
            pS_free = [{"act": cnt["act"]}, {"act": cnt["act"]}]
            attn_done = 0
            if not DBG_ATTN:
                attn_done = cnt["dve"]

            def make_ctx_extract(h, pm_free, pctx_free_ref, hpar):
                def emit_ctx_extract(Hf, j, jb, segs, c_ae, jmax, last_ed1):
                    def pC_fn(e, j=j, jb=jb, segs=segs, jmax=jmax):
                        last = None
                        for (p, q) in segs:
                            last = e.matmul(out=psum_ctx[0:65, p:q],
                                            lhsT=v_aug[hpar][:, j * 66 : j * 66 + 65],
                                            rhs=PT[jb][:, p:q],
                                            start=(j == 0), stop=(j == jmax - 1),
                                            skip_group_check=True)
                        return last
                    waits_pc = W(("act", c_ae), ("dve", last_ed1))
                    if j == 0:
                        waits_pc += W(("dve", pctx_free_ref[0]["dve"]),
                                      ("act", pctx_free_ref[0]["act"]))
                    c_pc = op("tensor", pC_fn, waits=waits_pc, inc="pe")
                    t_sub = j if (Hf == 0 or j >= 8) else None
                    if t_sub is None:
                        return last_ed1
                    eb = t_sub % 2
                    off = 128 * t_sub - 1024 * Hf
                    def ed1(e, off=off):
                        return e.tensor_copy(out=ctxTa_bf[0:65, :],
                                             in_=psum_ctx[0:65, off : off + 128])
                    c_ed1 = op("vector", ed1, waits=W(("pe", c_pc)), inc="dve")
                    def ep(e, eb=eb):
                        e.matmul(out=psum_m[eb][:, 0:1],
                                 lhsT=ctxTa_bf[0:65, :],
                                 rhs=ident_bf[0:65, 64:65], start=True, stop=True)
                        return e.matmul(out=psum_m[eb][:, 64:128],
                                        lhsT=ctxTa_bf[0:64, :], rhs=wot_bf[:],
                                        start=True, stop=True)
                    c_ep = op("tensor", ep,
                              waits=W(("dve", c_ed1), ("dve", pm_free[eb]["dve"]),
                                      ("act", pm_free[eb]["act"])),
                              inc="pe")
                    op("vector", lambda e, eb=eb: e.reciprocal(
                        out=svec[:, 64:65], in_=psum_m[eb][:, 0:1]),
                        waits=W(("pe", c_ep)), inc="dve")
                    c_ed2 = op("vector", lambda e, eb=eb, t_g=t_sub, h=h:
                               e.tensor_single_scalar(
                        out=attn_bf[:, t_g * 1024 + h * 64 : t_g * 1024 + h * 64 + 64],
                        in_=psum_m[eb][:, 64:128], scalar=svec[:, 64:65], op=ALU.mult),
                        inc="dve")
                    pm_free[eb] = {"dve": c_ed2, "act": pm_free[eb]["act"]}
                    nonlocal_state["attn_done"] = c_ed2
                    return c_ed1
                return emit_ctx_extract
            nonlocal_state = {"attn_done": 0}

            NHEADS = DBG_HEADS if DBG_ATTN else 0
            slotB_free = [cnt["dve"], cnt["dve"]]
            slotB_act = [cnt["act"], cnt["act"]]
            head_end_pe = [0, 0]
            prolog_state = {}

            def emit_w2(h):
                if s != 0:
                    return
                if h >= 1:
                    for m in (2 * h - 2, 2 * h - 1):
                        buf = m % 2
                        stage_free[buf] = quant_exact(
                            w1stage[buf][:, 0:512],
                            w2qT[:, m * 1024 : m * 1024 + 512],
                            512, 3, -6.0, True, first_waits=W(w2_lw[m]))
                        stage_free[buf] = quant_exact(
                            w1stage[buf][:, 512:1024],
                            w2qT[:, m * 1024 + 512 : m * 1024 + 1024],
                            512, 3, -6.0, True, first_waits=W(w2_lw[m]))
                if h < NHEADS:
                    for m in (2 * h, 2 * h + 1):
                        buf = m % 2
                        w2_lw[m] = dma("sync", w1stage[buf][:],
                                       w2t[m * 128 : (m + 1) * 128, :],
                                       waits=W(("dve", stage_free[buf])),
                                       sem="lw%d" % buf)

            def emit_prologue_early(h):
                par = h % 2
                chn = h // 2
                xcol = chn * 2048
                st = {}
                def ones_fn(e, par=par):
                    last = None
                    for t in range(NT):
                        last = e.memset(v_aug[par][:, t * 66 + 64 : t * 66 + 65], 1.0)
                    return last
                ones_c = op("vector", ones_fn, waits=W(("pe", head_end_pe[par])), inc="dve")
                d1_last = 0
                for p in range(8):
                    sbk = p % 2
                    def p1(e, p=p, sbk=sbk, par=par, xcol=xcol):
                        last = None
                        for i in range(2):
                            t = 2 * p + i
                            last = e.matmul(
                                out=psum_S[sbk][:, 512 + i * 256 : 512 + i * 256 + 192],
                                lhsT=featbuf[:, xcol + t * 128 : xcol + t * 128 + 128],
                                rhs=wqkv[par][:], start=True, stop=True)
                        return last
                    c_p1 = op("tensor", p1,
                              waits=W(("dve", featbuf_ready),
                                      ("dve", slotB_free[sbk]), ("act", slotB_act[sbk]),
                                      ("dve", ones_c)),
                              inc="pe")
                    def d1(e, p=p, sbk=sbk, par=par):
                        tb = 2 * p
                        ps3 = psum_S[sbk][:, 512:1024].rearrange("p (s c) -> p s c", s=2)
                        e.tensor_copy(out=kc_bf[:, tb * 64 : tb * 64 + 128],
                                      in_=ps3[:, :, 0:64])
                        e.tensor_copy(
                            out=v_aug[par][:].rearrange("p (s c) -> p s c", c=66)[
                                :, tb : tb + 2, 0:64],
                            in_=ps3[:, :, 64:128])
                        return e.tensor_copy(out=qc_bf[:, tb * 64 : tb * 64 + 128],
                                             in_=ps3[:, :, 128:192])
                    c_d1 = op("vector", d1, waits=W(("pe", c_p1)), inc="dve")
                    slotB_free[sbk] = c_d1
                    d1_last = c_d1
                c_sqk = op("scalar", lambda e: e.activation(
                    out=qsq[:], in_=kc_bf[:], func=AF.Square),
                    waits=W(("dve", d1_last)), inc="act")
                c_rdk = op("vector", lambda e: e.tensor_reduce(
                    out=svec[:, 0:16],
                    in_=qsq[:].rearrange("p (s c) -> p s c", c=64),
                    axis=mybir.AxisListType.X, op=ALU.add),
                    waits=W(("act", c_sqk)), inc="dve")
                c_sqq = op("scalar", lambda e: e.activation(
                    out=qsq[:], in_=qc_bf[:], func=AF.Square),
                    waits=W(("dve", c_rdk)), inc="act")
                c_rdq = op("vector", lambda e: e.tensor_reduce(
                    out=svec[:, 16:32],
                    in_=qsq[:].rearrange("p (s c) -> p s c", c=64),
                    axis=mybir.AxisListType.X, op=ALU.add),
                    waits=W(("act", c_sqq)), inc="dve")
                def a2a(e):
                    e.activation(out=svec[:, 96:112], in_=svec[:, 0:16], func=AF.Ln,
                                 bias=EPS_ATTN, scale=1.0 / 64.0)
                    return e.activation(out=svec[:, 112:128], in_=svec[:, 16:32],
                                        func=AF.Ln, bias=EPS_ATTN, scale=1.0 / 64.0)
                op("scalar", a2a, waits=W(("dve", c_rdq)), inc="act")
                def a2b(e):
                    e.activation(out=svec[:, 32:48], in_=svec[:, 96:112], func=AF.Exp,
                                 bias=0.0, scale=-0.5)
                    return e.activation(out=svec[:, 48:64], in_=svec[:, 112:128],
                                        func=AF.Exp, bias=0.0, scale=-0.5)
                c_a2 = op("scalar", a2b, inc="act")
                def d2(e):
                    last = None
                    for t in range(16):
                        e.scalar_tensor_tensor(out=kc_bf[:, t * 64 : t * 64 + 64],
                                               in0=kc_bf[:, t * 64 : t * 64 + 64],
                                               scalar=svec[:, 32 + t : 33 + t],
                                               in1=wkb_sb[:], op0=ALU.mult, op1=ALU.mult)
                        last = e.scalar_tensor_tensor(out=qc_bf[:, t * 64 : t * 64 + 64],
                                                      in0=qc_bf[:, t * 64 : t * 64 + 64],
                                                      scalar=svec[:, 48 + t : 49 + t],
                                                      in1=wqb_sb[:], op0=ALU.mult, op1=ALU.mult)
                    return last
                st["d2"] = op("vector", d2, waits=W(("act", c_a2)), inc="dve")
                prolog_state[h] = st

            def emit_prologue_late(h):
                par = h % 2
                chn = h // 2
                xcol = chn * 2048
                st = prolog_state[h]
                c_d2 = st["d2"]
                tr_done = 0
                for srcT, dstT in ((kc_bf, kT_bf), (qc_bf, qT_bf)):
                    for g in range(4):
                        pb = g % 2
                        def p2(e, g=g, pb=pb, srcT=srcT):
                            last = None
                            for t in range(4 * g, 4 * g + 4):
                                last = e.matmul(
                                    out=psum_m[pb][0:64, (t % 4) * 128 : (t % 4) * 128 + 128],
                                    lhsT=srcT[:, t * 64 : t * 64 + 64],
                                    rhs=ident_bf[:], start=True, stop=True)
                            return last
                        c_p2 = op("tensor", p2,
                                  waits=W(("dve", c_d2), ("dve", pm_free[pb]["dve"]),
                                          ("act", pm_free[pb]["act"])),
                                  inc="pe")
                        def d3(e, g=g, pb=pb, dstT=dstT):
                            return e.tensor_copy(
                                out=dstT[0:64, 4 * g * 128 : 4 * g * 128 + 512],
                                in_=psum_m[pb][0:64, 0:512])
                        c_d3 = op("vector", d3, waits=W(("pe", c_p2)), inc="dve")
                        pm_free[pb] = {"dve": c_d3, "act": pm_free[pb]["act"]}
                        tr_done = c_d3
                kr_done = 0
                for j in range(4):
                    pb = j % 2
                    c_p3 = op("tensor", lambda e, j=j, pb=pb, par=par, xcol=xcol: e.matmul(
                        out=psum_m[pb][:, 0:512], lhsT=wkrt_bf[par][:],
                        rhs=featbuf[:, xcol + j * 512 : xcol + j * 512 + 512],
                        start=True, stop=True),
                        waits=W(("dve", pm_free[pb]["dve"]), ("act", pm_free[pb]["act"])),
                        inc="pe")
                    c_d4 = op("vector", lambda e, j=j, pb=pb: e.tensor_copy(
                        out=kT_bf[64:128, j * 512 : j * 512 + 512],
                        in_=psum_m[pb][64:128, 0:512]),
                        waits=W(("pe", c_p3)), inc="dve")
                    pm_free[pb] = {"dve": c_d4, "act": pm_free[pb]["act"]}
                    kr_done = c_d4
                qr_done = 0
                for j in range(4):
                    pb = j % 2
                    c_p4 = op("tensor", lambda e, j=j, pb=pb: e.matmul(
                        out=psum_m[pb][:, 0:512], lhsT=wqrt_bf[0:64, :],
                        rhs=qT_bf[0:64, j * 512 : j * 512 + 512], start=True, stop=True),
                        waits=W(("dve", pm_free[pb]["dve"]), ("act", pm_free[pb]["act"]),
                                ("dve", tr_done)),
                        inc="pe")
                    c_d5 = op("vector", lambda e, j=j, pb=pb: e.tensor_copy(
                        out=qT_bf[64:128, j * 512 : j * 512 + 512],
                        in_=psum_m[pb][64:128, 0:512]),
                        waits=W(("pe", c_p4)), inc="dve")
                    pm_free[pb] = {"dve": c_d5, "act": pm_free[pb]["act"]}
                    qr_done = c_d5
                st["kr"] = kr_done
                st["qr"] = qr_done

            if NHEADS:
                emit_w2(0)
                emit_prologue_early(0)
                emit_prologue_late(0)
            sseg = [0]
            sseg_free = [{"act": cnt["act"]}, {"act": cnt["act"]}]
            for h in range(NHEADS):
                emit_w2(h + 1)
                kr_done = prolog_state[h]["kr"]
                qr_done = prolog_state[h]["qr"]
                pctx_free_ref = [pctx_free]
                emit_ctx_extract = make_ctx_extract(h, pm_free, pctx_free_ref, h % 2)
                last_ed1 = 0
                for Hf in range(2):
                    jmax = 8 if Hf == 0 else 16
                    pend = None
                    for j in range(jmax):
                        qlo = max(128 * j, 1024 * Hf)
                        off = qlo - 1024 * Hf
                        jb = j % 2
                        segs = []
                        p = off
                        while p < 1024:
                            q = 512 if p < 512 else 1024
                            segs.append((p, q))
                            p = q
                        has_diag = (Hf == 0) or (j >= 8)
                        c_ae = None
                        for (p, q) in segs:
                            sbk = sseg[0] % 2
                            sseg[0] += 1
                            diag_here = has_diag and (p == off)
                            def pS_fn(e, j=j, p=p, q=q, Hf=Hf, sbk=sbk, diag=diag_here):
                                last = e.matmul(out=psum_S[sbk][:, 0 : q - p],
                                                lhsT=kT_bf[:, j * 128 : j * 128 + 128],
                                                rhs=qT_bf[:, 1024 * Hf + p : 1024 * Hf + q],
                                                start=True, stop=True)
                                if diag:
                                    last = e.matmul(out=psum_S[sbk][:, 0:128],
                                                    lhsT=maskdT_sb[:], rhs=ident_bf[:],
                                                    start=False, stop=True,
                                                    skip_group_check=True)
                                return last
                            c_ps = op("tensor", pS_fn,
                                      waits=W(("dve", kr_done), ("dve", qr_done),
                                              ("act", sseg_free[sbk]["act"]),
                                              ("dve", pS2_free[sbk])),
                                      inc="pe")
                            c_ae = op("scalar", lambda e, jb=jb, p=p, q=q, sbk=sbk:
                                      e.activation(
                                out=PT[jb][:, p:q], in_=psum_S[sbk][:, 0 : q - p],
                                func=AF.Exp, bias=-CSH, scale=0.125),
                                waits=W(("pe", c_ps)), inc="act")
                            sseg_free[sbk] = {"act": c_ae}
                        if pend is not None:
                            last_ed1 = emit_ctx_extract(*pend, last_ed1)
                        pend = (Hf, j, jb, tuple(segs), c_ae, jmax)
                    last_ed1 = emit_ctx_extract(*pend, last_ed1)
                    pctx_free = {"dve": cnt["dve"], "act": cnt["act"]}
                    if Hf == 0 and h + 1 < NHEADS:
                        emit_prologue_early(h + 1)
                head_end_pe[h % 2] = cnt["pe"]
                if h + 1 < NHEADS:
                    emit_prologue_late(h + 1)

            if DBG_ATTN:
                attn_done = max(attn_done, nonlocal_state["attn_done"])
            if TAP == "attn":
                for t in range(NT):
                    buf = t % 2
                    cc = op("vector", lambda e, t=t, buf=buf: e.tensor_copy(
                        out=ln_b[buf][:], in_=attn_bf[:, t * 1024 : t * 1024 + 1024]),
                        waits=W(("dve", attn_done), lnb_free_st[buf],
                                ("dve", lnb_free_d[buf])),
                        inc="dve")
                    lnb_free_d[buf] = cc
                    lnb_free_st[buf] = dma("scalar",
                                           tap_out[s, t * 128 : t * 128 + 128, :],
                                           ln_b[buf][:], waits=W(("dve", cc)),
                                           sem="sb%d" % buf)

            yT_written = 0
            for t in range(NT if DBG_LN1 else 0):
                buf = t % 2
                lx = dma("sync", ln_a[buf][:], x_tok[s, t * 128 : t * 128 + 128, :],
                         waits=W(("dve", lna_free[buf])), sem="la%d" % buf)
                c_add = op("vector", lambda e, buf=buf, t=t: e.tensor_tensor(
                    out=ln_b[buf][:], in0=ln_a[buf][:],
                    in1=attn_bf[:, t * 1024 : t * 1024 + 1024], op=ALU.add),
                    waits=W(lx, ("dve", attn_done), lnb_free_st[buf],
                            ("dve", lnb_free_d[buf])),
                    inc="dve")
                lna_free[buf] = c_add
                op("vector", lambda e, buf=buf: e.reduce_sum(
                    out=svec[:, 80:81], in_=ln_b[buf][:], axis=mybir.AxisListType.X),
                    inc="dve")
                op("vector", lambda e, buf=buf: e.scalar_tensor_tensor(
                    out=ybf_tile[:], in0=ln_b[buf][:], scalar=1.0, in1=ln_b[buf][:],
                    op0=ALU.mult, op1=ALU.mult, accum_out=svec[:, 81:82]),
                    waits=W(("pe", ybf_free)), inc="dve")
                def stats_a(e):
                    e.tensor_scalar(out=svec[:, 82:83], in0=svec[:, 80:81], scalar1=1.0 / D,
                                    scalar2=0.0, op0=ALU.mult, op1=ALU.add)
                    return e.tensor_scalar(out=svec[:, 83:84], in0=svec[:, 81:82], scalar1=1.0 / D,
                                           scalar2=0.0, op0=ALU.mult, op1=ALU.add)
                op("vector", stats_a, inc="dve")
                op("vector", lambda e: e.tensor_tensor(out=svec[:, 84:85], in0=svec[:, 82:83],
                                                       in1=svec[:, 82:83], op=ALU.mult),
                   inc="dve")
                c_stats = op("vector", lambda e: e.tensor_tensor(
                    out=svec[:, 85:86], in0=svec[:, 83:84], in1=svec[:, 84:85], op=ALU.subtract),
                    inc="dve")
                c_r1 = op("scalar", lambda e: e.activation(
                    out=svec[:, 86:87], in_=svec[:, 85:86], func=AF.Ln, bias=EPS_LN, scale=1.0),
                    waits=W(("dve", c_stats)), inc="act")
                c_rstd = op("scalar", lambda e: e.activation(
                    out=svec[:, 87:88], in_=svec[:, 86:87], func=AF.Exp, bias=0.0, scale=-0.5),
                    inc="act")
                c_apply = op("vector", lambda e, buf=buf: e.tensor_scalar(
                    out=ln_b[buf][:], in0=ln_b[buf][:], scalar1=svec[:, 82:83],
                    scalar2=svec[:, 87:88], op0=ALU.subtract, op1=ALU.mult),
                    waits=W(("act", c_rstd)), inc="dve")
                if TAP == "y":
                    c_sp = dma("scalar", tap_out[s, t * 128 : t * 128 + 128, :],
                               ln_b[buf][:], waits=W(("dve", c_apply)), sem="sb%d" % buf)
                else:
                    c_sp = dma("scalar", y_spill[t * 128 : t * 128 + 128, :],
                               ln_b[buf][:], waits=W(("dve", c_apply)), sem="sb%d" % buf)
                spill_cnt[t] = c_sp
                c_ybf = op("vector", lambda e, buf=buf: e.tensor_copy(
                    out=ybf_tile[:], in_=ln_b[buf][:]),
                    waits=W(("dve", c_apply), ("pe", ybf_free)), inc="dve")
                lnb_free_st[buf] = c_sp
                lnb_free_d[buf] = c_ybf
                for grp in range(2):
                    pb = grp
                    def ptr_fn(e, grp=grp, pb=pb):
                        last = None
                        for dc in range(4 * grp, 4 * grp + 4):
                            last = e.matmul(
                                out=psum_m[pb][:, (dc % 4) * 128 : (dc % 4) * 128 + 128],
                                lhsT=ybf_tile[:, dc * 128 : dc * 128 + 128],
                                rhs=ident_bf[:], start=True, stop=True)
                        return last
                    c_ptr = op("tensor", ptr_fn,
                               waits=W(("dve", c_ybf), ("dve", pm_free[pb]["dve"]),
                                       ("act", pm_free[pb]["act"])),
                               inc="pe")
                    def dtr(e, grp=grp, pb=pb, t=t):
                        return e.tensor_copy(
                            out=featbuf[:].rearrange("p (c n) -> p c n", n=2048)[
                                :, 4 * grp : 4 * grp + 4, t * 128 : t * 128 + 128],
                            in_=psum_m[pb][:, 0:512])
                    c_dtr = op("vector", dtr, waits=W(("pe", c_ptr)), inc="dve")
                    pm_free[pb] = {"dve": c_dtr, "act": pm_free[pb]["act"]}
                    yT_written = c_dtr
                ybf_free = cnt["pe"]

            def emit_f2(th, par):
                nonlocal lnb_free_st, lnb_free_d, lna_free, ybf_free
                thc = th * 256
                last_mm2 = 0
                oq_last = 0
                for dout in range(8):
                    ob = dout % 4
                    pslice = [psum_ctx[:, 0:256], psum_ctx[:, 512:768],
                              psum_S[0][:, 0:256], psum_S[0][:, 512:768]][ob]
                    def pmm2(e, pslice=pslice, dout=dout, par=par):
                        last = None
                        for m in range(32):
                            last = e.matmul(
                                out=pslice,
                                lhsT=w2qT[:, m * 1024 + dout * 128 : m * 1024 + dout * 128 + 128],
                                rhs=hqT[par][:, m * 256 : m * 256 + 256],
                                start=(m == 0), stop=(m == 31))
                        return last
                    c_mm2 = op("tensor", pmm2,
                               waits=W(("dve", hq_done[par]),
                                       ("dve", pm2_free.get(ob, 0))),
                               inc="pe")
                    last_mm2 = c_mm2
                    wslot = dout % 2
                    c_ev = op("vector", lambda e, pslice=pslice, dout=dout, wslot=wslot:
                              e.tensor_scalar(out=owork[:, wslot * 256 : wslot * 256 + 256],
                                              in0=pslice,
                                              scalar1=b2q_sb[:, dout : dout + 1],
                                              scalar2=0.0, op0=ALU.add, op1=ALU.add),
                              waits=W(("pe", c_mm2), ("dve", oq_free[wslot])), inc="dve")
                    c_oq = op("vector", lambda e, wslot=wslot, dout=dout: e.tensor_copy(
                        out=oq_all[:, dout * 256 : dout * 256 + 256],
                        in_=owork[:, wslot * 256 : wslot * 256 + 256]), inc="dve")
                    oq_free[wslot] = c_oq
                    pm2_free[ob] = c_oq
                    oq_last = c_oq
                hq_rd[par] = last_mm2
                pS2_free[0] = max(pS2_free[0], oq_last)
                for tk in range(2):
                    tb = th * 2 + tk
                    def ptr2(e, tk=tk):
                        last = None
                        for dout in range(8):
                            last = e.matmul(
                                out=psum_m[dout // 4][:, (dout % 4) * 128 : (dout % 4) * 128 + 128],
                                lhsT=oq_all[:, dout * 256 + tk * 128 : dout * 256 + tk * 128 + 128],
                                rhs=ident_h[:], start=True, stop=True)
                        return last
                    c_tr2 = op("tensor", ptr2,
                               waits=W(("dve", oq_last),
                                       ("act", relu_cnt[0]), ("act", relu_cnt[1]),
                                       ("act", relu_cnt[2]), ("act", relu_cnt[3]),
                                       ("dve", ps2_tr_free[0])),
                               inc="pe")
                    ly = dma("scalar", ln_a[tk][:], y_spill[tb * 128 : tb * 128 + 128, :],
                             waits=W(("dve", lna_free[tk]), spill_cnt[tb]),
                             sem="la%d" % tk)
                    c_s2 = op("vector", lambda e, tk=tk: e.scalar_tensor_tensor(
                        out=ln_b[tk][:, 0:512], in0=psum_m[0][:, 0:512], scalar=1.0,
                        in1=ln_a[tk][:, 0:512], op0=ALU.mult, op1=ALU.add),
                        waits=W(("pe", c_tr2), ly, lnb_free_st[tk],
                                ("dve", lnb_free_d[tk])),
                        inc="dve")
                    c_s2b = op("vector", lambda e, tk=tk: e.scalar_tensor_tensor(
                        out=ln_b[tk][:, 512:1024], in0=psum_m[1][:, 0:512], scalar=1.0,
                        in1=ln_a[tk][:, 512:1024], op0=ALU.mult, op1=ALU.add),
                        inc="dve")
                    ps2_tr_free[0] = c_s2b
                    lna_free[tk] = c_s2b
                    op("vector", lambda e, tk=tk: e.reduce_sum(
                        out=svec[:, 88:89], in_=ln_b[tk][:], axis=mybir.AxisListType.X),
                        inc="dve")
                    c_sq2 = op("vector", lambda e, tk=tk: e.scalar_tensor_tensor(
                        out=ybf_tile[:], in0=ln_b[tk][:], scalar=1.0, in1=ln_b[tk][:],
                        op0=ALU.mult, op1=ALU.mult, accum_out=svec[:, 89:90]),
                        waits=W(("pe", ybf_free)), inc="dve")
                    def stats2_a(e):
                        e.tensor_scalar(out=svec[:, 90:91], in0=svec[:, 88:89],
                                        scalar1=1.0 / D, scalar2=0.0, op0=ALU.mult, op1=ALU.add)
                        return e.tensor_scalar(out=svec[:, 91:92], in0=svec[:, 89:90],
                                               scalar1=1.0 / D, scalar2=0.0, op0=ALU.mult, op1=ALU.add)
                    op("vector", stats2_a, inc="dve")
                    op("vector", lambda e: e.tensor_tensor(out=svec[:, 92:93], in0=svec[:, 90:91],
                                                           in1=svec[:, 90:91], op=ALU.mult),
                       inc="dve")
                    c_sts = op("vector", lambda e: e.tensor_tensor(
                        out=svec[:, 93:94], in0=svec[:, 91:92], in1=svec[:, 92:93],
                        op=ALU.subtract), inc="dve")
                    op("scalar", lambda e: e.activation(
                        out=svec[:, 94:95], in_=svec[:, 93:94], func=AF.Ln,
                        bias=EPS_LN, scale=1.0), waits=W(("dve", c_sts)), inc="act")
                    c_rs2 = op("scalar", lambda e: e.activation(
                        out=svec[:, 95:96], in_=svec[:, 94:95], func=AF.Exp,
                        bias=0.0, scale=-0.5), inc="act")
                    c_ap2 = op("vector", lambda e, tk=tk: e.tensor_scalar(
                        out=ln_b[tk][:], in0=ln_b[tk][:], scalar1=svec[:, 90:91],
                        scalar2=svec[:, 95:96], op0=ALU.subtract, op1=ALU.mult),
                        waits=W(("act", c_rs2)), inc="dve")
                    lnb_free_d[tk] = c_ap2
                    lnb_free_st[tk] = dma("scalar",
                                          out_tok[s, tb * 128 : tb * 128 + 128, :],
                                          ln_b[tk][:], waits=W(("dve", c_ap2)),
                                          sem="sb%d" % tk)
            oq_free = [0, 0]
            ps2_tr_free = [0]

            w1s_free = [cnt["dve"], cnt["dve"]]
            w1st_ld = {0: None, 1: None}
            hwork_free = [0, 0, 0, 0]
            relu_cnt = [0, 0, 0, 0]
            hq_done = [0, 0]
            hq_rd = [0, 0]
            pend_mm2 = None
            for th in range(8 if DBG_FFN else 0):
                thc = th * 256
                par = th % 2
                last_hq = 0
                for m in range(32):
                    wbuf = m % 2
                    if s == 0 and th == 0:
                        lw = dma("sync", w1stage[wbuf][:], w1tl[m],
                                 waits=W(("dve", w1s_free[wbuf])), sem="lw%d" % wbuf)
                        c_cw = op("vector", lambda e, wbuf=wbuf: e.tensor_copy(
                            out=w1bf[wbuf][:], in_=w1stage[wbuf][:]),
                            waits=W(lw, w1c_done.get(m - 2, ("wc0", 0)),
                                    ("pe", w1st_ld[wbuf] or 0)), inc="dve")
                        w1c_done[m] = dma("scalar", w1bf_dram[m], w1bf[wbuf][:],
                                          waits=W(("dve", c_cw)), sem="wc%d" % wbuf)
                        w1s_free[wbuf] = c_cw
                    else:
                        lw = dma("sync", w1bf[wbuf][:], w1bf_dram[m],
                                 waits=W(("pe", w1st_ld[wbuf] or 0),
                                         ("wc%d" % wbuf, cnt["wc%d" % wbuf])),
                                 sem="lw%d" % wbuf)
                        c_cw = None
                    slot = m % 4
                    def mm1_slot(slot):
                        return [psum_m[0][:, 0:256], psum_m[1][:, 0:256],
                                psum_S[1][:, 0:256], psum_S[1][:, 512:768]][slot]
                    def pmm1(e, wbuf=wbuf, slot=slot, thc=thc):
                        last = None
                        for dcc in range(8):
                            last = e.matmul(
                                out=mm1_slot(slot),
                                lhsT=w1bf[wbuf][:, dcc * 128 : dcc * 128 + 128],
                                rhs=featbuf[:, dcc * 2048 + thc : dcc * 2048 + thc + 256],
                                start=(dcc == 0), stop=(dcc == 7))
                        return last
                    mm1_waits = W(("dve", yT_written), ("act", relu_cnt[slot]),
                                  ("pe", hq_rd[par]), ("dve", ps2_tr_free[0]))
                    if c_cw is not None:
                        mm1_waits += W(("dve", c_cw))
                    else:
                        mm1_waits += W(lw)
                    c_mm1 = op("tensor", pmm1, waits=mm1_waits, inc="pe")
                    w1st_ld[wbuf] = c_mm1
                    c_rel = op("scalar", lambda e, slot=slot, m=m: e.activation(
                        out=hwork[:, slot * 256 : slot * 256 + 256],
                        in_=mm1_slot(slot),
                        func=AF.Relu, bias=b1_sb[:, m : m + 1], scale=1.0),
                        waits=W(("pe", c_mm1), ("dve", hwork_free[slot])),
                        inc="act")
                    relu_cnt[slot] = c_rel
                    if m % 2 == 1:
                        p0 = (m - 1) % 4
                        c_hq = quant_exact(
                            hwork[:, p0 * 256 : p0 * 256 + 512],
                            hqT[par][:, (m - 1) * 256 : (m - 1) * 256 + 512],
                            512, 3, -6.0, False, first_waits=W(("act", c_rel)))
                        hwork_free[p0] = c_hq
                        hwork_free[p0 + 1] = c_hq
                        last_hq = c_hq
                hq_done[par] = last_hq

                if pend_mm2 is not None:
                    emit_f2(*pend_mm2)
                pend_mm2 = (th, par)
            if DBG_FFN:
                emit_f2(*pend_mm2)
            featbuf_pe_free = cnt["pe"]

        op("sync", lambda e: e.nop(),
           waits=W(("sb0", cnt["sb0"]), ("sb1", cnt["sb1"])))

        ENGS = ["sync", "scalar", "vector", "tensor", "gpsimd"]
        emitted_waits = {e: {} for e in ENGS}

        OWN_SEM = {"vector": "dve", "scalar": "act", "gpsimd": "gps"}

        def replay(eng_name):
            ew = emitted_waits[eng_name]
            own = OWN_SEM.get(eng_name)
            def body(eng):
                for (en, fn, waits, inc, amt) in OPS:
                    if en != eng_name:
                        continue
                    allw = list(waits)
                    if own is not None and inc == own:
                        allw.append((own, cnt_at[id(fn)]))
                    for (sname, val) in allw:
                        if val > ew.get(sname, 0):
                            eng.wait_ge(sems[sname], val)
                            ew[sname] = val
                    instr = fn(eng)
                    if inc is not None:
                        assert instr is not None, f"op on {eng_name} returned no instruction"
                        instr.then_inc(sems[inc], amt, skip_validation=(amt == 16))
            return body

        for en in ENGS:
            if any(o[0] == en for o in OPS):
                getattr(block, en)(replay(en))

    return nc


def _host_inputs(inputs, core):
    x = np.asarray(inputs["L0_attn_x"])
    xs = np.ascontiguousarray(x[2 * core : 2 * core + 2])
    xf = np.ascontiguousarray(xs.transpose(0, 2, 1))
    W_DQ = np.asarray(inputs["L0_attn_W_DQ"]); W_UQ = np.asarray(inputs["L0_attn_W_UQ"])
    W_DKV = np.asarray(inputs["L0_attn_W_DKV"]); W_UKV = np.asarray(inputs["L0_attn_W_UKV"])
    W_QR = np.asarray(inputs["L0_attn_W_QR"]); W_KR = np.asarray(inputs["L0_attn_W_KR"])
    W_O = np.asarray(inputs["L0_attn_W_O"])
    qnw = np.asarray(inputs["L0_attn_q_norm_w"]); knw = np.asarray(inputs["L0_attn_k_norm_w"])
    w1 = np.asarray(inputs["L0_ffn_w1"]); b1 = np.asarray(inputs["L0_ffn_b1"])
    w2 = np.asarray(inputs["L0_ffn_w2"]); b2 = np.asarray(inputs["L0_ffn_b2"])

    tile_pc = lambda a, P=128: np.ascontiguousarray(
        a.reshape(a.shape[0] // P, P, a.shape[1]).transpose(1, 0, 2).reshape(P, -1))
    def pad_rt(wt):
        half = np.concatenate([np.zeros((64, 64), np.float32),
                               np.ascontiguousarray(wt.T)], axis=1)
        return np.concatenate([half, half], axis=0)

    def par_rt(wt, par):
        out = np.zeros((128, 128), np.float32)
        out[par * 64 : par * 64 + 64, 64:128] = np.ascontiguousarray(wt.T)
        return out
    d = {
        "x_tok": xs,
        "x_feat": xf,
        "wdq": tile_pc(W_DQ),
        "wuqt": tile_pc(np.ascontiguousarray(W_UQ.T)),
        "wdkv": tile_pc(W_DKV),
        "wukvt": tile_pc(np.ascontiguousarray(W_UKV.T)),
        "wqrt_p": pad_rt(W_QR),
        "wkrt_e": par_rt(W_KR, 0),
        "wkrt_o": par_rt(W_KR, 1),
        "wot": np.ascontiguousarray(W_O.T),
        "wq_b": np.ascontiguousarray(np.broadcast_to(qnw, (128, 64))),
        "wk_b": np.ascontiguousarray(np.broadcast_to(knw, (128, 64))),
        "w1tl": np.ascontiguousarray(
            w1.T.reshape(8, 128, 32, 128).transpose(2, 1, 0, 3).reshape(32, 128, 1024)),
        "w2t": np.ascontiguousarray(w2.T),
        "b1_t": np.ascontiguousarray(b1.reshape(32, 128).T),
        "b2_t": np.ascontiguousarray(b2.reshape(8, 128).T),
        "identf": np.eye(128, dtype=np.float32),
    }
    d = {k: np.ascontiguousarray(np.asarray(v, dtype=np.float32)) for k, v in d.items()}
    import ml_dtypes
    mk = np.where(np.arange(128)[None, :] >= np.arange(128)[:, None],
                  np.float32(0.0), np.float32(-1e30)).astype(ml_dtypes.bfloat16)
    d["maskd"] = np.ascontiguousarray(mk)
    d["maskdT"] = np.ascontiguousarray(mk.T)
    return d


def kernel(**inputs):
    nc = build_program()
    in_maps = [_host_inputs(inputs, c) for c in range(8)]
    res = run_bass_kernel_spmd(nc, in_maps, core_ids=list(range(8)))
    out = np.concatenate([r["out_tok"] for r in res.results], axis=0)
    if TAP is not None:
        kernel.tap = np.concatenate([r["tap"] for r in res.results], axis=0)
    return out


# version 06
# speedup vs baseline: 1.8562x; 1.3610x over previous
import contextlib
import numpy as np
import concourse.bass as bass
import concourse.mybir as mybir
from concourse.bass_utils import run_bass_kernel_spmd

F32 = mybir.dt.float32
BF16 = mybir.dt.bfloat16
F16 = mybir.dt.float16
F8 = mybir.dt.float8e4
I32 = mybir.dt.int32
AF = mybir.ActivationFunctionType
ALU = mybir.AluOpType

LN2 = 0.6931471805599453
INVLN2 = 1.4426950408889634
EPS_ATTN = 1.1920929e-07
EPS_LN = 1e-5
TINY = 1e-38
CSH = 20.0
T = 2048
D = 1024
NT = 16
NSEQ = 2

TAP = None
import os
DBG_HEADS = int(os.environ.get("MK_HEADS", "16"))
DBG_SEQ = int(os.environ.get("MK_SEQ", str(NSEQ)))
DBG_FFN = os.environ.get("MK_FFN", "1") == "1"
DBG_LN1 = os.environ.get("MK_LN1", "1") == "1"
DBG_ATTN = os.environ.get("MK_ATTN", "1") == "1"
DBG_HSTAGE = int(os.environ.get("MK_HSTAGE", "9"))
DBG_G = int(os.environ.get("MK_G", "4"))
DBG_NOA1 = os.environ.get("MK_NOA1", "0") == "1"
DBG_NOD1 = os.environ.get("MK_NOD1", "0") == "1"
DBG_D1SEL = int(os.environ.get("MK_D1SEL", "7"))
DBG_NOGPS = os.environ.get("MK_NOGPS", "0") == "1"
DBG_SERIAL = os.environ.get("MK_SERIAL", "0") != "0"


def build_program():
    nc = bass.Bass("TRN2", target_bir_lowering=False, debug=False)

    x_tok = nc.dram_tensor("x_tok", [NSEQ, T, D], F32, kind="ExternalInput").ap()
    x_feat = nc.dram_tensor("x_feat", [NSEQ, D, T], F32, kind="ExternalInput").ap()
    wdq = nc.dram_tensor("wdq", [128, 256], F32, kind="ExternalInput").ap()
    wuqt = nc.dram_tensor("wuqt", [128, 256], F32, kind="ExternalInput").ap()
    wdkv = nc.dram_tensor("wdkv", [128, 512], F32, kind="ExternalInput").ap()
    wukvt = nc.dram_tensor("wukvt", [128, 1024], F32, kind="ExternalInput").ap()
    wqrt_p = nc.dram_tensor("wqrt_p", [128, 128], F32, kind="ExternalInput").ap()
    wkrt_e = nc.dram_tensor("wkrt_e", [128, 128], F32, kind="ExternalInput").ap()
    wkrt_o = nc.dram_tensor("wkrt_o", [128, 128], F32, kind="ExternalInput").ap()
    wot = nc.dram_tensor("wot", [64, 64], F32, kind="ExternalInput").ap()
    wq_b = nc.dram_tensor("wq_b", [128, 64], F32, kind="ExternalInput").ap()
    wk_b = nc.dram_tensor("wk_b", [128, 64], F32, kind="ExternalInput").ap()
    w1tl = nc.dram_tensor("w1tl", [32, 128, 1024], F32, kind="ExternalInput").ap()
    w2t = nc.dram_tensor("w2t", [4096, D], F32, kind="ExternalInput").ap()
    b1_t = nc.dram_tensor("b1_t", [128, 32], F32, kind="ExternalInput").ap()
    b2_t = nc.dram_tensor("b2_t", [128, 8], F32, kind="ExternalInput").ap()
    identf = nc.dram_tensor("identf", [128, 128], F32, kind="ExternalInput").ap()
    maskd = nc.dram_tensor("maskd", [128, 128], BF16, kind="ExternalInput").ap()
    maskdT = nc.dram_tensor("maskdT", [128, 128], BF16, kind="ExternalInput").ap()
    out_tok = nc.dram_tensor("out_tok", [NSEQ, T, D], F32, kind="ExternalOutput").ap()
    y_spill = nc.dram_tensor("y_spill", [T, D], F32, kind="Internal").ap()
    weff_scr = nc.dram_tensor("weff_scr", [64, 192], F32, kind="Internal").ap()
    w1bf_dram = nc.dram_tensor("w1bf_dram", [32, 128, 1024], BF16, kind="Internal").ap()
    tap_out = None
    if TAP is not None:
        tap_out = nc.dram_tensor("tap", [NSEQ, T, D], F32, kind="ExternalOutput").ap()

    ctx = contextlib.ExitStack()
    with ctx:
        sb = lambda name, shape, dt: ctx.enter_context(nc.sbuf_tensor(name, shape, dt))
        psb = lambda name, shape: ctx.enter_context(nc.psum_tensor(name, shape, F32))

        cpool = sb("cpool", [128, 16], F32)
        ident_bf = sb("ident_bf", [128, 128], BF16)
        ident_h = sb("ident_h", [128, 128], F16)
        maskd_sb = sb("maskd_sb", [128, 128], BF16)
        maskdT_sb = sb("maskdT_sb", [128, 128], BF16)
        wqb_sb = sb("wqb_sb", [128, 64], F32)
        wkb_sb = sb("wkb_sb", [128, 64], F32)
        wdq_bf = sb("wdq_bf", [128, 256], BF16)
        wuqt_bf = sb("wuqt_bf", [128, 256], BF16)
        wdkv_bf = sb("wdkv_bf", [128, 512], BF16)
        wukvt_bf = sb("wukvt_bf", [128, 1024], BF16)
        wqrt_bf = sb("wqrt_bf", [128, 128], BF16)
        wkrt_bf = [sb("wkrt_bf0", [128, 128], BF16), sb("wkrt_bf1", [128, 128], BF16)]
        wot_bf = sb("wot_bf", [64, 64], BF16)
        wqkv = [sb("wqkv0", [128, 192], BF16), sb("wqkv1", [128, 192], BF16)]
        b1_sb = sb("b1_sb", [128, 32], F32)
        b2_sb = sb("b2_sb", [128, 8], F32)
        b2q_sb = sb("b2q_sb", [128, 8], F32)

        featbuf = sb("featbuf", [128, 8 * T], BF16)
        attn_bf = sb("attn_bf", [128, NT * D], BF16)
        w2qT = sb("w2qT", [128, 32 * D], F8)
        hqT = [sb("hqT0", [128, 32 * 256], F8), sb("hqT1", [128, 32 * 256], F8)]

        kT_bf = sb("kT_bf", [128, T], BF16)
        qT_bf = sb("qT_bf", [128, T], BF16)
        v_aug = [sb("v_aug0", [128, NT * 66], BF16), sb("v_aug1", [128, NT * 66], BF16)]
        kc_bf = sb("kc_bf", [128, NT * 64], BF16)
        qc_bf = sb("qc_bf", [128, NT * 64], BF16)
        svec = sb("svec", [128, 128], F32)
        PT = [sb("PT0", [128, 1024], BF16), sb("PT1", [128, 1024], BF16)]
        ctxTa_bf = sb("ctxTa_bf", [128, 128], BF16)
        qsq = sb("qsq", [128, 1024], BF16)

        ln_a = [sb("ln_a0", [128, D], F32), sb("ln_a1", [128, D], F32)]
        ln_b = [sb("ln_b0", [128, D], F32), sb("ln_b1", [128, D], F32)]
        ybf_tile = sb("ybf_tile", [128, D], BF16)
        w1stage = [sb("w1stage0", [128, 1024], F32), sb("w1stage1", [128, 1024], F32)]
        w1bf = [sb("w1bf0", [128, 1024], BF16), sb("w1bf1", [128, 1024], BF16)]
        hwork = sb("hwork", [128, 1024], F32)
        ql = [sb("ql0", [128, 512], F32), sb("ql1", [128, 512], F32)]
        qi = [sb("qi0", [128, 512], I32), sb("qi1", [128, 512], I32)]
        qsc = [sb("qsc0", [128, 512], F32), sb("qsc1", [128, 512], F32)]
        qinv = [sb("qinv0", [128, 512], F32), sb("qinv1", [128, 512], F32)]
        qv = [sb("qv0", [128, 512], F32), sb("qv1", [128, 512], F32)]
        qc = [sb("qc0", [128, 512], F32), sb("qc1", [128, 512], F32)]
        b2h = sb("b2h", [128, 8], F16)
        oq_all = sb("oq_all", [128, 2048], F16)
        owork = sb("owork", [128, 512], F32)

        psum_S = [psb("psum_S0", [128, 1024]), psb("psum_S1", [128, 1024])]
        psum_ctx = psb("psum_ctx", [128, 1024])
        psum_m = [psb("psum_m0", [128, 512]), psb("psum_m1", [128, 512])]

        CONSTS = sorted({TINY, -3 * LN2, 3 * LN2, LN2, -LN2, EPS_ATTN,
                         EPS_LN, 0.125, -CSH, -0.5, -1.0, 1.0 / 64.0, 0.5, 1.0, 0.0})
        for j, v in enumerate(CONSTS):
            nc.const_aps.register(F32, float(v), cpool[:, j : j + 1])

        sems = {}
        for s in ["ld", "lw0", "lw1", "la0", "la1", "sb0", "sb1", "wc0", "wc1",
                  "pe", "act", "dve", "gps"]:
            sems[s] = ctx.enter_context(nc.semaphore("sem_" + s))
        cnt = {s: 0 for s in sems}

        block = ctx.enter_context(nc.Block())
        OPS = []

        cnt_at = {}

        def op(eng, fn, waits=(), inc=None, amt=1):
            waits = [(sn, v) for (sn, v) in waits if v > 0]
            if DBG_SERIAL:
                mode = os.environ.get("MK_SERIAL", "0")
                DMAS = ("ld", "lw0", "lw1", "la0", "la1", "sb0", "sb1")
                if mode == "1":
                    waits = [(sn, v) for (sn, v) in cnt.items() if v > 0]
                elif mode == "2":
                    if eng in ("sync", "scalar") and fn.__name__ == "<lambda>" and inc in DMAS:
                        waits = waits + [(sn, cnt[sn]) for sn in ("pe", "act", "dve", "gps") if cnt[sn] > 0]
                    else:
                        waits = waits + [(sn, cnt[sn]) for sn in DMAS if cnt[sn] > 0]
                elif mode == "3":
                    if eng == "tensor":
                        waits = waits + [(sn, cnt[sn]) for sn in ("act", "dve", "gps") if cnt[sn] > 0]
                    elif eng in ("vector", "scalar", "gpsimd"):
                        waits = waits + [("pe", cnt["pe"])] if cnt["pe"] > 0 else waits
                elif mode == "4":
                    if eng == "vector" and cnt["act"] > 0:
                        waits = waits + [("act", cnt["act"])]
                    elif eng == "scalar" and inc == "act" and cnt["dve"] > 0:
                        waits = waits + [("dve", cnt["dve"])]
            OPS.append((eng, fn, waits, inc, amt))
            if inc is not None:
                cnt_at[id(fn)] = cnt[inc]
                cnt[inc] += amt
                return cnt[inc]
            return None

        def dma(queue, out_ap, in_ap, waits=(), sem="ld"):
            c = op(queue, lambda e, o=out_ap, i=in_ap: e.dma_start(out=o, in_=i),
                   waits=waits, inc=sem, amt=16)
            return (sem, c)

        W = lambda *pairs: list(pairs)

        def memset_consts(e):
            last = None
            for j, v in enumerate(CONSTS):
                last = e.memset(cpool[:, j : j + 1], float(v))
            return last
        op("vector", memset_consts, inc="dve")

        dma("sync", maskd_sb[:], maskd)
        dma("sync", maskdT_sb[:], maskdT)
        dma("sync", wqb_sb[:], wq_b)
        dma("sync", wkb_sb[:], wk_b)
        dma("sync", b1_sb[:], b1_t)
        dma("sync", b2_sb[:], b2_t)
        dma("sync", w1stage[0][:, 0:128], identf)
        dma("sync", w1stage[1][:, 0:256], wdq)
        dma("sync", w1stage[1][:, 256:512], wuqt)
        dma("sync", ln_a[0][:, 0:512], wdkv)
        dma("sync", ln_a[1][:, 0:1024], wukvt)
        dma("sync", ln_b[0][:, 0:128], wqrt_p)
        dma("sync", ln_b[0][:, 128:256], wkrt_e)
        dma("sync", ln_b[0][:, 320:448], wkrt_o)
        ld_w = dma("sync", ln_b[0][0:64, 256:320], wot)[1]

        def cast_weights(e):
            e.tensor_copy(out=ident_bf[:], in_=w1stage[0][:, 0:128])
            e.tensor_copy(out=ident_h[:], in_=w1stage[0][:, 0:128])
            e.tensor_copy(out=wdq_bf[:], in_=w1stage[1][:, 0:256])
            e.tensor_copy(out=wuqt_bf[:], in_=w1stage[1][:, 256:512])
            e.tensor_copy(out=wdkv_bf[:], in_=ln_a[0][:, 0:512])
            e.tensor_copy(out=wukvt_bf[:], in_=ln_a[1][:, 0:1024])
            e.tensor_copy(out=wqrt_bf[:], in_=ln_b[0][:, 0:128])
            e.tensor_copy(out=wkrt_bf[0][:], in_=ln_b[0][:, 128:256])
            e.tensor_copy(out=wkrt_bf[1][:], in_=ln_b[0][:, 320:448])
            return e.tensor_copy(out=wot_bf[:], in_=ln_b[0][0:64, 256:320])
        c_cast = op("vector", cast_weights, waits=W(("ld", ld_w)), inc="dve")

        def weff(e):
            for c in range(4):
                e.matmul(out=psum_m[0][0:64, 0:64], lhsT=wdq_bf[:, c * 64 : c * 64 + 64],
                         rhs=wuqt_bf[:, c * 64 : c * 64 + 64], start=(c == 0), stop=(c == 3))
            last = None
            for c in range(8):
                last = e.matmul(out=psum_m[0][0:64, 64:192],
                                lhsT=wdkv_bf[:, c * 64 : c * 64 + 64],
                                rhs=wukvt_bf[:, c * 128 : c * 128 + 128],
                                start=(c == 0), stop=(c == 7))
            return last
        c_weff = op("tensor", weff, waits=W(("dve", c_cast)), inc="pe")

        weff_f32 = ln_b[1]
        def weff_evac(e):
            e.memset(wqkv[0][:], 0.0)
            e.memset(wqkv[1][:], 0.0)
            return e.tensor_copy(out=weff_f32[0:64, 0:192], in_=psum_m[0][0:64, 0:192])
        c_we1 = op("vector", weff_evac, waits=W(("pe", c_weff)), inc="dve")
        c_we2 = dma("sync", weff_scr, weff_f32[0:64, 0:192], waits=W(("dve", c_we1)))
        c_we3 = dma("sync", weff_f32[64:128, 0:192], weff_scr, waits=W(c_we2))
        def weff_cast(e):
            e.tensor_copy(out=wqkv[0][0:64, 128:192], in_=weff_f32[0:64, 0:64])
            e.tensor_copy(out=wqkv[0][0:64, 0:128], in_=weff_f32[0:64, 64:192])
            e.tensor_copy(out=wqkv[1][64:128, 128:192], in_=weff_f32[64:128, 0:64])
            return e.tensor_copy(out=wqkv[1][64:128, 0:128], in_=weff_f32[64:128, 64:192])
        c_weffe = op("vector", weff_cast, waits=W(c_we3), inc="dve")

        QSTATE = [{"dve": 0, "gps": 0, "act": 0}, {"dve": 0, "gps": 0, "act": 0}]
        QPAR = [0]

        def quant_exact(src_ap, dst_ap, n, man, clamp, signed, first_waits=()):
            par = QPAR[0]
            QPAR[0] ^= 1
            QS = QSTATE[par]
            qls, qis = ql[par][:, 0:n], qi[par][:, 0:n]
            qscs, qinvs = qsc[par][:, 0:n], qinv[par][:, 0:n]
            qvs, qcs = qv[par][:, 0:n], qc[par][:, 0:n]
            qes = qls
            qd = qvs
            fw = list(first_waits) + [("dve", QS["dve"]), ("gps", QS["gps"]),
                                      ("act", QS["act"])]
            if signed:
                a1 = op("scalar", lambda e, sp=src_ap, o=qd: e.activation(
                    out=o, in_=sp, func=AF.Square), waits=fw, inc="act")
                a2 = op("scalar", lambda e, o=qls, i=qd: e.activation(
                    out=o, in_=i, func=AF.Ln, bias=TINY, scale=1.0),
                    waits=W(("act", a1)), inc="act")
                mulc, backc = INVLN2 / 2.0, 2.0 * LN2
            else:
                a2 = op("scalar", lambda e, sp=src_ap, o=qls: e.activation(
                    out=o, in_=sp, func=AF.Ln, bias=TINY, scale=1.0), waits=fw, inc="act")
                mulc, backc = INVLN2, LN2
            geng, gsem = ("vector", "dve") if DBG_NOGPS else ("gpsimd", "gps")
            g1 = op(geng, lambda e, o=qis, i=qls, m=mulc: e.tensor_scalar(
                out=o, in0=i, scalar1=m, scalar2=0.0, op0=ALU.mult, op1=ALU.add),
                waits=W(("act", a2), ("dve", QS["dve"])), inc=gsem)
            d1 = op("vector", lambda e, o=qcs, i0=qis, i1=qls, b=backc: e.scalar_tensor_tensor(
                out=o, in0=i0, scalar=b, in1=i1, op0=ALU.mult, op1=ALU.is_gt),
                waits=W((gsem, g1)), inc="dve")
            d2 = op("vector", lambda e, o=qes, i0=qcs, i1=qis: e.scalar_tensor_tensor(
                out=o, in0=i0, scalar=-1.0, in1=i1, op0=ALU.mult, op1=ALU.add),
                inc="dve")
            if clamp is not None:
                d2 = op("vector", lambda e, o=qes, c=float(clamp): e.tensor_scalar(
                    out=o, in0=o, scalar1=c, scalar2=0.0, op0=ALU.max, op1=ALU.add),
                    inc="dve")
            exp_waits = W(("dve", d2))
            bias_sc = -float(man) * LN2
            a3 = op("scalar", lambda e, o=qinvs, i=qes, b=-bias_sc: e.activation(
                out=o, in_=i, func=AF.Exp, bias=b, scale=-LN2), waits=exp_waits, inc="act")
            a4 = op("scalar", lambda e, o=qscs, i=qes, b=bias_sc: e.activation(
                out=o, in_=i, func=AF.Exp, bias=b, scale=LN2), inc="act")
            d3 = op("vector", lambda e, o=qvs, i0=src_ap, i1=qinvs: e.tensor_tensor(
                out=o, in0=i0, in1=i1, op=ALU.mult), waits=W(("act", a3)), inc="dve")
            g4 = op(geng, lambda e, o=qis, i=qvs: e.tensor_scalar(
                out=o, in0=i, scalar1=1.0, scalar2=0.0, op0=ALU.mult, op1=ALU.add),
                waits=W(("dve", d3)), inc=gsem)
            d5 = op("vector", lambda e, o=qcs, i0=qis, i1=qvs: e.scalar_tensor_tensor(
                out=o, in0=i0, scalar=0.5, in1=i1, op0=ALU.add, op1=ALU.is_le),
                waits=W((gsem, g4)), inc="dve")
            if signed:
                op("vector", lambda e, o=qes, i0=qis, i1=qvs: e.scalar_tensor_tensor(
                    out=o, in0=i0, scalar=-0.5, in1=i1, op0=ALU.add, op1=ALU.is_ge),
                    waits=W(("act", a4)), inc="dve")
                op("vector", lambda e, o=qcs, i0=qcs, i1=qes: e.tensor_tensor(
                    out=o, in0=i0, in1=i1, op=ALU.subtract), inc="dve")
            d6 = op("vector", lambda e, o=qvs, i0=qis, i1=qcs: e.tensor_tensor(
                out=o, in0=i0, in1=i1, op=ALU.add), inc="dve")
            d7 = op("vector", lambda e, o=dst_ap, i0=qvs, i1=qscs: e.tensor_tensor(
                out=o, in0=i0, in1=i1, op=ALU.mult), waits=W(("act", a4)), inc="dve")
            QS["dve"] = d7
            QS["gps"] = 0 if DBG_NOGPS else g4
            QS["act"] = a4
            return d7

        stage_free = [c_weffe, c_weffe]
        w2_lw = {}
        c_bh = op("vector", lambda e: e.tensor_copy(out=b2h[:], in_=b2_sb[:]),
                  waits=W(("ld", ld_w)), inc="dve")
        b2q_done = op("vector", lambda e: e.tensor_copy(out=b2q_sb[:], in_=b2h[:]),
                      inc="dve")

        lna_free = [c_cast, c_cast]
        lnb_free_st = [("sb0", 0), ("sb1", 0)]
        lnb_free_d = [c_cast, c_cast]
        hwork_free = [0, 0]
        pm2_free = {}
        pS2_free = [0, 0]
        ybf_free = 0
        featbuf_pe_free = 0
        spill_cnt = {}
        w1c_done = {}

        for s in range(DBG_SEQ):
            last_cast = 0
            for half in range(16):
                c, hh = divmod(half, 2)
                buf = half % 2
                lx = dma("sync", ln_a[buf][:, 0:1024],
                         x_feat[s, c * 128 : c * 128 + 128, hh * 1024 : hh * 1024 + 1024],
                         waits=W(("dve", lna_free[buf])), sem="la%d" % buf)
                cc = op("vector", lambda e,
                        o=featbuf[:, c * 2048 + hh * 1024 : c * 2048 + hh * 1024 + 1024],
                        i=ln_a[buf][:, 0:1024]: e.tensor_copy(out=o, in_=i),
                        waits=W(lx, ("pe", featbuf_pe_free)), inc="dve")
                lna_free[buf] = cc
                last_cast = cc
            featbuf_ready = last_cast

            pm_free = [{"dve": cnt["dve"], "act": cnt["act"]} for _ in range(2)]
            pctx_free = {"dve": cnt["dve"], "act": cnt["act"]}
            pS_free = [{"act": cnt["act"]}, {"act": cnt["act"]}]
            attn_done = 0
            if not DBG_ATTN:
                attn_done = cnt["dve"]

            def make_ctx_extract(h, pm_free, pctx_free_ref, hpar):
                def emit_ctx_extract(Hf, j, jb, segs, c_ae, jmax, last_ed1):
                    def pC_fn(e, j=j, jb=jb, segs=segs, jmax=jmax):
                        last = None
                        for (p, q) in segs:
                            last = e.matmul(out=psum_ctx[0:65, p:q],
                                            lhsT=v_aug[hpar][:, j * 66 : j * 66 + 65],
                                            rhs=PT[jb][:, p:q],
                                            start=(j == 0), stop=(j == jmax - 1),
                                            skip_group_check=True)
                        return last
                    waits_pc = W(("act", c_ae), ("dve", last_ed1))
                    if j == 0:
                        waits_pc += W(("dve", pctx_free_ref[0]["dve"]),
                                      ("act", pctx_free_ref[0]["act"]))
                    c_pc = op("tensor", pC_fn, waits=waits_pc, inc="pe")
                    t_sub = j if (Hf == 0 or j >= 8) else None
                    if t_sub is None:
                        return last_ed1
                    eb = t_sub % 2
                    off = 128 * t_sub - 1024 * Hf
                    def ed1(e, off=off):
                        return e.tensor_copy(out=ctxTa_bf[0:65, :],
                                             in_=psum_ctx[0:65, off : off + 128])
                    c_ed1 = op("vector", ed1, waits=W(("pe", c_pc)), inc="dve")
                    def ep(e, eb=eb):
                        e.matmul(out=psum_m[eb][:, 0:1],
                                 lhsT=ctxTa_bf[0:65, :],
                                 rhs=ident_bf[0:65, 64:65], start=True, stop=True)
                        return e.matmul(out=psum_m[eb][:, 64:128],
                                        lhsT=ctxTa_bf[0:64, :], rhs=wot_bf[:],
                                        start=True, stop=True)
                    c_ep = op("tensor", ep,
                              waits=W(("dve", c_ed1), ("dve", pm_free[eb]["dve"]),
                                      ("act", pm_free[eb]["act"])),
                              inc="pe")
                    op("vector", lambda e, eb=eb: e.reciprocal(
                        out=svec[:, 64:65], in_=psum_m[eb][:, 0:1]),
                        waits=W(("pe", c_ep)), inc="dve")
                    c_ed2 = op("vector", lambda e, eb=eb, t_g=t_sub, h=h:
                               e.tensor_single_scalar(
                        out=attn_bf[:, t_g * 1024 + h * 64 : t_g * 1024 + h * 64 + 64],
                        in_=psum_m[eb][:, 64:128], scalar=svec[:, 64:65], op=ALU.mult),
                        inc="dve")
                    pm_free[eb] = {"dve": c_ed2, "act": pm_free[eb]["act"]}
                    nonlocal_state["attn_done"] = c_ed2
                    return c_ed1
                return emit_ctx_extract
            nonlocal_state = {"attn_done": 0}

            NHEADS = DBG_HEADS if DBG_ATTN else 0
            slotB_free = [cnt["dve"], cnt["dve"]]
            slotB_act = [cnt["act"], cnt["act"]]
            head_end_pe = [0, 0]
            prolog_state = {}

            def emit_w2(h):
                if s != 0:
                    return
                if h >= 1:
                    for m in (2 * h - 2, 2 * h - 1):
                        buf = m % 2
                        stage_free[buf] = quant_exact(
                            w1stage[buf][:, 0:512],
                            w2qT[:, m * 1024 : m * 1024 + 512],
                            512, 3, -6.0, True, first_waits=W(w2_lw[m]))
                        stage_free[buf] = quant_exact(
                            w1stage[buf][:, 512:1024],
                            w2qT[:, m * 1024 + 512 : m * 1024 + 1024],
                            512, 3, -6.0, True, first_waits=W(w2_lw[m]))
                if h < NHEADS:
                    for m in (2 * h, 2 * h + 1):
                        buf = m % 2
                        w2_lw[m] = dma("sync", w1stage[buf][:],
                                       w2t[m * 128 : (m + 1) * 128, :],
                                       waits=W(("dve", stage_free[buf])),
                                       sem="lw%d" % buf)

            def emit_prologue_early(h):
                par = h % 2
                chn = h // 2
                xcol = chn * 2048
                st = {}
                def ones_fn(e, par=par):
                    last = None
                    for t in range(NT):
                        last = e.memset(v_aug[par][:, t * 66 + 64 : t * 66 + 65], 1.0)
                    return last
                ones_c = op("vector", ones_fn, waits=W(("pe", head_end_pe[par])), inc="dve")
                d1_last = 0
                for p in range(8):
                    sbk = p % 2
                    def p1(e, p=p, sbk=sbk, par=par, xcol=xcol):
                        last = None
                        for i in range(2):
                            t = 2 * p + i
                            last = e.matmul(
                                out=psum_S[sbk][:, 512 + i * 256 : 512 + i * 256 + 192],
                                lhsT=featbuf[:, xcol + t * 128 : xcol + t * 128 + 128],
                                rhs=wqkv[par][:], start=True, stop=True)
                        return last
                    c_p1 = op("tensor", p1,
                              waits=W(("dve", featbuf_ready),
                                      ("dve", slotB_free[sbk]), ("act", slotB_act[sbk]),
                                      ("dve", ones_c)),
                              inc="pe")
                    def d1(e, p=p, sbk=sbk, par=par):
                        tb = 2 * p
                        ps3 = psum_S[sbk][:, 512:1024].rearrange("p (s c) -> p s c", s=2)
                        e.tensor_copy(out=kc_bf[:, tb * 64 : tb * 64 + 128],
                                      in_=ps3[:, :, 0:64])
                        e.tensor_copy(
                            out=v_aug[par][:].rearrange("p (s c) -> p s c", c=66)[
                                :, tb : tb + 2, 0:64],
                            in_=ps3[:, :, 64:128])
                        return e.tensor_copy(out=qc_bf[:, tb * 64 : tb * 64 + 128],
                                             in_=ps3[:, :, 128:192])
                    c_d1 = op("vector", d1, waits=W(("pe", c_p1)), inc="dve")
                    slotB_free[sbk] = c_d1
                    d1_last = c_d1
                c_sqk = op("scalar", lambda e: e.activation(
                    out=qsq[:], in_=kc_bf[:], func=AF.Square),
                    waits=W(("dve", d1_last)), inc="act")
                c_rdk = op("vector", lambda e: e.tensor_reduce(
                    out=svec[:, 0:16],
                    in_=qsq[:].rearrange("p (s c) -> p s c", c=64),
                    axis=mybir.AxisListType.X, op=ALU.add),
                    waits=W(("act", c_sqk)), inc="dve")
                c_sqq = op("scalar", lambda e: e.activation(
                    out=qsq[:], in_=qc_bf[:], func=AF.Square),
                    waits=W(("dve", c_rdk)), inc="act")
                c_rdq = op("vector", lambda e: e.tensor_reduce(
                    out=svec[:, 16:32],
                    in_=qsq[:].rearrange("p (s c) -> p s c", c=64),
                    axis=mybir.AxisListType.X, op=ALU.add),
                    waits=W(("act", c_sqq)), inc="dve")
                def a2a(e):
                    e.activation(out=svec[:, 96:112], in_=svec[:, 0:16], func=AF.Ln,
                                 bias=EPS_ATTN, scale=1.0 / 64.0)
                    return e.activation(out=svec[:, 112:128], in_=svec[:, 16:32],
                                        func=AF.Ln, bias=EPS_ATTN, scale=1.0 / 64.0)
                op("scalar", a2a, waits=W(("dve", c_rdq)), inc="act")
                def a2b(e):
                    e.activation(out=svec[:, 32:48], in_=svec[:, 96:112], func=AF.Exp,
                                 bias=0.0, scale=-0.5)
                    return e.activation(out=svec[:, 48:64], in_=svec[:, 112:128],
                                        func=AF.Exp, bias=0.0, scale=-0.5)
                c_a2 = op("scalar", a2b, inc="act")
                def d2(e):
                    last = None
                    for t in range(16):
                        e.scalar_tensor_tensor(out=kc_bf[:, t * 64 : t * 64 + 64],
                                               in0=kc_bf[:, t * 64 : t * 64 + 64],
                                               scalar=svec[:, 32 + t : 33 + t],
                                               in1=wkb_sb[:], op0=ALU.mult, op1=ALU.mult)
                        last = e.scalar_tensor_tensor(out=qc_bf[:, t * 64 : t * 64 + 64],
                                                      in0=qc_bf[:, t * 64 : t * 64 + 64],
                                                      scalar=svec[:, 48 + t : 49 + t],
                                                      in1=wqb_sb[:], op0=ALU.mult, op1=ALU.mult)
                    return last
                st["d2"] = op("vector", d2, waits=W(("act", c_a2)), inc="dve")
                prolog_state[h] = st

            def emit_prologue_late(h):
                par = h % 2
                chn = h // 2
                xcol = chn * 2048
                st = prolog_state[h]
                c_d2 = st["d2"]
                tr_done = 0
                for srcT, dstT in ((kc_bf, kT_bf), (qc_bf, qT_bf)):
                    for g in range(4):
                        pb = g % 2
                        def p2(e, g=g, pb=pb, srcT=srcT):
                            last = None
                            for t in range(4 * g, 4 * g + 4):
                                last = e.matmul(
                                    out=psum_m[pb][0:64, (t % 4) * 128 : (t % 4) * 128 + 128],
                                    lhsT=srcT[:, t * 64 : t * 64 + 64],
                                    rhs=ident_bf[:], start=True, stop=True)
                            return last
                        c_p2 = op("tensor", p2,
                                  waits=W(("dve", c_d2), ("dve", pm_free[pb]["dve"]),
                                          ("act", pm_free[pb]["act"])),
                                  inc="pe")
                        def d3(e, g=g, pb=pb, dstT=dstT):
                            return e.tensor_copy(
                                out=dstT[0:64, 4 * g * 128 : 4 * g * 128 + 512],
                                in_=psum_m[pb][0:64, 0:512])
                        c_d3 = op("vector", d3, waits=W(("pe", c_p2)), inc="dve")
                        pm_free[pb] = {"dve": c_d3, "act": pm_free[pb]["act"]}
                        tr_done = c_d3
                kr_done = 0
                for j in range(4):
                    pb = j % 2
                    c_p3 = op("tensor", lambda e, j=j, pb=pb, par=par, xcol=xcol: e.matmul(
                        out=psum_m[pb][:, 0:512], lhsT=wkrt_bf[par][:],
                        rhs=featbuf[:, xcol + j * 512 : xcol + j * 512 + 512],
                        start=True, stop=True),
                        waits=W(("dve", pm_free[pb]["dve"]), ("act", pm_free[pb]["act"])),
                        inc="pe")
                    c_d4 = op("vector", lambda e, j=j, pb=pb: e.tensor_copy(
                        out=kT_bf[64:128, j * 512 : j * 512 + 512],
                        in_=psum_m[pb][64:128, 0:512]),
                        waits=W(("pe", c_p3)), inc="dve")
                    pm_free[pb] = {"dve": c_d4, "act": pm_free[pb]["act"]}
                    kr_done = c_d4
                qr_done = 0
                for j in range(4):
                    pb = j % 2
                    c_p4 = op("tensor", lambda e, j=j, pb=pb: e.matmul(
                        out=psum_m[pb][:, 0:512], lhsT=wqrt_bf[0:64, :],
                        rhs=qT_bf[0:64, j * 512 : j * 512 + 512], start=True, stop=True),
                        waits=W(("dve", pm_free[pb]["dve"]), ("act", pm_free[pb]["act"]),
                                ("dve", tr_done)),
                        inc="pe")
                    c_d5 = op("vector", lambda e, j=j, pb=pb: e.tensor_copy(
                        out=qT_bf[64:128, j * 512 : j * 512 + 512],
                        in_=psum_m[pb][64:128, 0:512]),
                        waits=W(("pe", c_p4)), inc="dve")
                    pm_free[pb] = {"dve": c_d5, "act": pm_free[pb]["act"]}
                    qr_done = c_d5
                st["kr"] = kr_done
                st["qr"] = qr_done

            if NHEADS:
                emit_w2(0)
                emit_prologue_early(0)
                emit_prologue_late(0)
            sseg = [0]
            sseg_free = [{"act": cnt["act"]}, {"act": cnt["act"]}]
            for h in range(NHEADS):
                emit_w2(h + 1)
                kr_done = prolog_state[h]["kr"]
                qr_done = prolog_state[h]["qr"]
                pctx_free_ref = [pctx_free]
                emit_ctx_extract = make_ctx_extract(h, pm_free, pctx_free_ref, h % 2)
                last_ed1 = 0
                for Hf in range(2):
                    jmax = 8 if Hf == 0 else 16
                    pend = None
                    for j in range(jmax):
                        qlo = max(128 * j, 1024 * Hf)
                        off = qlo - 1024 * Hf
                        jb = j % 2
                        segs = []
                        p = off
                        while p < 1024:
                            q = 512 if p < 512 else 1024
                            segs.append((p, q))
                            p = q
                        has_diag = (Hf == 0) or (j >= 8)
                        c_ae = None
                        for (p, q) in segs:
                            sbk = sseg[0] % 2
                            sseg[0] += 1
                            diag_here = has_diag and (p == off)
                            def pS_fn(e, j=j, p=p, q=q, Hf=Hf, sbk=sbk, diag=diag_here):
                                last = e.matmul(out=psum_S[sbk][:, 0 : q - p],
                                                lhsT=kT_bf[:, j * 128 : j * 128 + 128],
                                                rhs=qT_bf[:, 1024 * Hf + p : 1024 * Hf + q],
                                                start=True, stop=True)
                                if diag:
                                    last = e.matmul(out=psum_S[sbk][:, 0:128],
                                                    lhsT=maskdT_sb[:], rhs=ident_bf[:],
                                                    start=False, stop=True,
                                                    skip_group_check=True)
                                return last
                            c_ps = op("tensor", pS_fn,
                                      waits=W(("dve", kr_done), ("dve", qr_done),
                                              ("act", sseg_free[sbk]["act"]),
                                              ("dve", pS2_free[sbk])),
                                      inc="pe")
                            c_ae = op("scalar", lambda e, jb=jb, p=p, q=q, sbk=sbk:
                                      e.activation(
                                out=PT[jb][:, p:q], in_=psum_S[sbk][:, 0 : q - p],
                                func=AF.Exp, bias=-CSH, scale=0.125),
                                waits=W(("pe", c_ps)), inc="act")
                            sseg_free[sbk] = {"act": c_ae}
                        if pend is not None:
                            last_ed1 = emit_ctx_extract(*pend, last_ed1)
                        pend = (Hf, j, jb, tuple(segs), c_ae, jmax)
                    last_ed1 = emit_ctx_extract(*pend, last_ed1)
                    pctx_free = {"dve": cnt["dve"], "act": cnt["act"]}
                    if Hf == 0 and h + 1 < NHEADS:
                        emit_prologue_early(h + 1)
                head_end_pe[h % 2] = cnt["pe"]
                if h + 1 < NHEADS:
                    emit_prologue_late(h + 1)

            if DBG_ATTN:
                attn_done = max(attn_done, nonlocal_state["attn_done"])
            if TAP == "attn":
                for t in range(NT):
                    buf = t % 2
                    cc = op("vector", lambda e, t=t, buf=buf: e.tensor_copy(
                        out=ln_b[buf][:], in_=attn_bf[:, t * 1024 : t * 1024 + 1024]),
                        waits=W(("dve", attn_done), lnb_free_st[buf],
                                ("dve", lnb_free_d[buf])),
                        inc="dve")
                    lnb_free_d[buf] = cc
                    lnb_free_st[buf] = dma("scalar",
                                           tap_out[s, t * 128 : t * 128 + 128, :],
                                           ln_b[buf][:], waits=W(("dve", cc)),
                                           sem="sb%d" % buf)

            yT_written = 0
            for t in range(NT if DBG_LN1 else 0):
                buf = t % 2
                lx = dma("sync", ln_a[buf][:], x_tok[s, t * 128 : t * 128 + 128, :],
                         waits=W(("dve", lna_free[buf])), sem="la%d" % buf)
                c_add = op("vector", lambda e, buf=buf, t=t: e.tensor_tensor(
                    out=ln_b[buf][:], in0=ln_a[buf][:],
                    in1=attn_bf[:, t * 1024 : t * 1024 + 1024], op=ALU.add),
                    waits=W(lx, ("dve", attn_done), lnb_free_st[buf],
                            ("dve", lnb_free_d[buf])),
                    inc="dve")
                lna_free[buf] = c_add
                op("vector", lambda e, buf=buf: e.reduce_sum(
                    out=svec[:, 80:81], in_=ln_b[buf][:], axis=mybir.AxisListType.X),
                    inc="dve")
                op("vector", lambda e, buf=buf: e.scalar_tensor_tensor(
                    out=ybf_tile[:], in0=ln_b[buf][:], scalar=1.0, in1=ln_b[buf][:],
                    op0=ALU.mult, op1=ALU.mult, accum_out=svec[:, 81:82]),
                    waits=W(("pe", ybf_free)), inc="dve")
                def stats_a(e):
                    e.tensor_scalar(out=svec[:, 82:83], in0=svec[:, 80:81], scalar1=1.0 / D,
                                    scalar2=0.0, op0=ALU.mult, op1=ALU.add)
                    return e.tensor_scalar(out=svec[:, 83:84], in0=svec[:, 81:82], scalar1=1.0 / D,
                                           scalar2=0.0, op0=ALU.mult, op1=ALU.add)
                op("vector", stats_a, inc="dve")
                op("vector", lambda e: e.tensor_tensor(out=svec[:, 84:85], in0=svec[:, 82:83],
                                                       in1=svec[:, 82:83], op=ALU.mult),
                   inc="dve")
                c_stats = op("vector", lambda e: e.tensor_tensor(
                    out=svec[:, 85:86], in0=svec[:, 83:84], in1=svec[:, 84:85], op=ALU.subtract),
                    inc="dve")
                c_r1 = op("scalar", lambda e: e.activation(
                    out=svec[:, 86:87], in_=svec[:, 85:86], func=AF.Ln, bias=EPS_LN, scale=1.0),
                    waits=W(("dve", c_stats)), inc="act")
                c_rstd = op("scalar", lambda e: e.activation(
                    out=svec[:, 87:88], in_=svec[:, 86:87], func=AF.Exp, bias=0.0, scale=-0.5),
                    inc="act")
                c_apply = op("vector", lambda e, buf=buf: e.tensor_scalar(
                    out=ln_b[buf][:], in0=ln_b[buf][:], scalar1=svec[:, 82:83],
                    scalar2=svec[:, 87:88], op0=ALU.subtract, op1=ALU.mult),
                    waits=W(("act", c_rstd)), inc="dve")
                if TAP == "y":
                    c_sp = dma("scalar", tap_out[s, t * 128 : t * 128 + 128, :],
                               ln_b[buf][:], waits=W(("dve", c_apply)), sem="sb%d" % buf)
                else:
                    c_sp = dma("scalar", y_spill[t * 128 : t * 128 + 128, :],
                               ln_b[buf][:], waits=W(("dve", c_apply)), sem="sb%d" % buf)
                spill_cnt[t] = c_sp
                c_ybf = op("vector", lambda e, buf=buf: e.tensor_copy(
                    out=ybf_tile[:], in_=ln_b[buf][:]),
                    waits=W(("dve", c_apply), ("pe", ybf_free)), inc="dve")
                lnb_free_st[buf] = c_sp
                lnb_free_d[buf] = c_ybf
                for grp in range(2):
                    pb = grp
                    def ptr_fn(e, grp=grp, pb=pb):
                        last = None
                        for dc in range(4 * grp, 4 * grp + 4):
                            last = e.matmul(
                                out=psum_m[pb][:, (dc % 4) * 128 : (dc % 4) * 128 + 128],
                                lhsT=ybf_tile[:, dc * 128 : dc * 128 + 128],
                                rhs=ident_bf[:], start=True, stop=True)
                        return last
                    c_ptr = op("tensor", ptr_fn,
                               waits=W(("dve", c_ybf), ("dve", pm_free[pb]["dve"]),
                                       ("act", pm_free[pb]["act"])),
                               inc="pe")
                    def dtr(e, grp=grp, pb=pb, t=t):
                        return e.tensor_copy(
                            out=featbuf[:].rearrange("p (c n) -> p c n", n=2048)[
                                :, 4 * grp : 4 * grp + 4, t * 128 : t * 128 + 128],
                            in_=psum_m[pb][:, 0:512])
                    c_dtr = op("vector", dtr, waits=W(("pe", c_ptr)), inc="dve")
                    pm_free[pb] = {"dve": c_dtr, "act": pm_free[pb]["act"]}
                    yT_written = c_dtr
                ybf_free = cnt["pe"]

            def emit_f2(th, par):
                nonlocal lnb_free_st, lnb_free_d, lna_free, ybf_free
                thc = th * 256
                last_mm2 = 0
                oq_last = 0
                for dout in range(8):
                    ob = dout % 4
                    pslice = [psum_ctx[:, 0:256], psum_ctx[:, 512:768],
                              psum_S[0][:, 0:256], psum_S[0][:, 512:768]][ob]
                    def pmm2(e, pslice=pslice, dout=dout, par=par):
                        last = None
                        for m in range(32):
                            last = e.matmul(
                                out=pslice,
                                lhsT=w2qT[:, m * 1024 + dout * 128 : m * 1024 + dout * 128 + 128],
                                rhs=hqT[par][:, m * 256 : m * 256 + 256],
                                start=(m == 0), stop=(m == 31))
                        return last
                    c_mm2 = op("tensor", pmm2,
                               waits=W(("dve", hq_done[par]),
                                       ("dve", pm2_free.get(ob, 0))),
                               inc="pe")
                    last_mm2 = c_mm2
                    wslot = dout % 2
                    c_ev = op("vector", lambda e, pslice=pslice, dout=dout, wslot=wslot:
                              e.tensor_scalar(out=owork[:, wslot * 256 : wslot * 256 + 256],
                                              in0=pslice,
                                              scalar1=b2q_sb[:, dout : dout + 1],
                                              scalar2=0.0, op0=ALU.add, op1=ALU.add),
                              waits=W(("pe", c_mm2), ("dve", oq_free[wslot])), inc="dve")
                    c_oq = op("vector", lambda e, wslot=wslot, dout=dout: e.tensor_copy(
                        out=oq_all[:, dout * 256 : dout * 256 + 256],
                        in_=owork[:, wslot * 256 : wslot * 256 + 256]), inc="dve")
                    oq_free[wslot] = c_oq
                    pm2_free[ob] = c_oq
                    oq_last = c_oq
                hq_rd[par] = last_mm2
                pS2_free[0] = max(pS2_free[0], oq_last)
                for tk in range(2):
                    tb = th * 2 + tk
                    def ptr2(e, tk=tk):
                        last = None
                        for dout in range(8):
                            last = e.matmul(
                                out=psum_m[dout // 4][:, (dout % 4) * 128 : (dout % 4) * 128 + 128],
                                lhsT=oq_all[:, dout * 256 + tk * 128 : dout * 256 + tk * 128 + 128],
                                rhs=ident_h[:], start=True, stop=True)
                        return last
                    c_tr2 = op("tensor", ptr2,
                               waits=W(("dve", oq_last),
                                       ("act", relu_cnt[0]), ("act", relu_cnt[1]),
                                       ("act", relu_cnt[2]), ("act", relu_cnt[3]),
                                       ("dve", ps2_tr_free[0])),
                               inc="pe")
                    ly = dma("scalar", ln_a[tk][:], y_spill[tb * 128 : tb * 128 + 128, :],
                             waits=W(("dve", lna_free[tk]), spill_cnt[tb]),
                             sem="la%d" % tk)
                    c_s2 = op("vector", lambda e, tk=tk: e.scalar_tensor_tensor(
                        out=ln_b[tk][:, 0:512], in0=psum_m[0][:, 0:512], scalar=1.0,
                        in1=ln_a[tk][:, 0:512], op0=ALU.mult, op1=ALU.add),
                        waits=W(("pe", c_tr2), ly, lnb_free_st[tk],
                                ("dve", lnb_free_d[tk])),
                        inc="dve")
                    c_s2b = op("vector", lambda e, tk=tk: e.scalar_tensor_tensor(
                        out=ln_b[tk][:, 512:1024], in0=psum_m[1][:, 0:512], scalar=1.0,
                        in1=ln_a[tk][:, 512:1024], op0=ALU.mult, op1=ALU.add),
                        inc="dve")
                    ps2_tr_free[0] = c_s2b
                    lna_free[tk] = c_s2b
                    op("vector", lambda e, tk=tk: e.reduce_sum(
                        out=svec[:, 88:89], in_=ln_b[tk][:], axis=mybir.AxisListType.X),
                        inc="dve")
                    c_sq2 = op("vector", lambda e, tk=tk: e.scalar_tensor_tensor(
                        out=ybf_tile[:], in0=ln_b[tk][:], scalar=1.0, in1=ln_b[tk][:],
                        op0=ALU.mult, op1=ALU.mult, accum_out=svec[:, 89:90]),
                        waits=W(("pe", ybf_free)), inc="dve")
                    def stats2_a(e):
                        e.tensor_scalar(out=svec[:, 90:91], in0=svec[:, 88:89],
                                        scalar1=1.0 / D, scalar2=0.0, op0=ALU.mult, op1=ALU.add)
                        return e.tensor_scalar(out=svec[:, 91:92], in0=svec[:, 89:90],
                                               scalar1=1.0 / D, scalar2=0.0, op0=ALU.mult, op1=ALU.add)
                    op("vector", stats2_a, inc="dve")
                    op("vector", lambda e: e.tensor_tensor(out=svec[:, 92:93], in0=svec[:, 90:91],
                                                           in1=svec[:, 90:91], op=ALU.mult),
                       inc="dve")
                    c_sts = op("vector", lambda e: e.tensor_tensor(
                        out=svec[:, 93:94], in0=svec[:, 91:92], in1=svec[:, 92:93],
                        op=ALU.subtract), inc="dve")
                    op("scalar", lambda e: e.activation(
                        out=svec[:, 94:95], in_=svec[:, 93:94], func=AF.Ln,
                        bias=EPS_LN, scale=1.0), waits=W(("dve", c_sts)), inc="act")
                    c_rs2 = op("scalar", lambda e: e.activation(
                        out=svec[:, 95:96], in_=svec[:, 94:95], func=AF.Exp,
                        bias=0.0, scale=-0.5), inc="act")
                    c_ap2 = op("vector", lambda e, tk=tk: e.tensor_scalar(
                        out=ln_b[tk][:], in0=ln_b[tk][:], scalar1=svec[:, 90:91],
                        scalar2=svec[:, 95:96], op0=ALU.subtract, op1=ALU.mult),
                        waits=W(("act", c_rs2)), inc="dve")
                    lnb_free_d[tk] = c_ap2
                    lnb_free_st[tk] = dma("scalar",
                                          out_tok[s, tb * 128 : tb * 128 + 128, :],
                                          ln_b[tk][:], waits=W(("dve", c_ap2)),
                                          sem="sb%d" % tk)
            oq_free = [0, 0]
            ps2_tr_free = [0]

            w1s_free = [cnt["dve"], cnt["dve"]]
            w1st_ld = {0: None, 1: None}
            hwork_free = [0, 0, 0, 0]
            relu_cnt = [0, 0, 0, 0]
            hq_done = [0, 0]
            hq_rd = [0, 0]
            pend_mm2 = None
            for th in range(8 if DBG_FFN else 0):
                thc = th * 256
                par = th % 2
                last_hq = 0
                for m in range(32):
                    wbuf = m % 2
                    if s == 0 and th == 0:
                        lw = dma("sync", w1stage[wbuf][:], w1tl[m],
                                 waits=W(("dve", w1s_free[wbuf])), sem="lw%d" % wbuf)
                        c_cw = op("vector", lambda e, wbuf=wbuf: e.tensor_copy(
                            out=w1bf[wbuf][:], in_=w1stage[wbuf][:]),
                            waits=W(lw, w1c_done.get(m - 2, ("wc0", 0)),
                                    ("pe", w1st_ld[wbuf] or 0)), inc="dve")
                        w1c_done[m] = dma("scalar", w1bf_dram[m], w1bf[wbuf][:],
                                          waits=W(("dve", c_cw)), sem="wc%d" % wbuf)
                        w1s_free[wbuf] = c_cw
                    else:
                        lw = dma("sync", w1bf[wbuf][:], w1bf_dram[m],
                                 waits=W(("pe", w1st_ld[wbuf] or 0),
                                         ("wc%d" % wbuf, cnt["wc%d" % wbuf])),
                                 sem="lw%d" % wbuf)
                        c_cw = None
                    slot = m % 4
                    def mm1_slot(slot):
                        return [psum_m[0][:, 0:256], psum_m[1][:, 0:256],
                                psum_S[1][:, 0:256], psum_S[1][:, 512:768]][slot]
                    def pmm1(e, wbuf=wbuf, slot=slot, thc=thc):
                        last = None
                        for dcc in range(8):
                            last = e.matmul(
                                out=mm1_slot(slot),
                                lhsT=w1bf[wbuf][:, dcc * 128 : dcc * 128 + 128],
                                rhs=featbuf[:, dcc * 2048 + thc : dcc * 2048 + thc + 256],
                                start=(dcc == 0), stop=(dcc == 7))
                        return last
                    mm1_waits = W(("dve", yT_written), ("act", relu_cnt[slot]),
                                  ("pe", hq_rd[par]), ("dve", ps2_tr_free[0]))
                    if c_cw is not None:
                        mm1_waits += W(("dve", c_cw))
                    else:
                        mm1_waits += W(lw)
                    c_mm1 = op("tensor", pmm1, waits=mm1_waits, inc="pe")
                    w1st_ld[wbuf] = c_mm1
                    c_rel = op("scalar", lambda e, slot=slot, m=m: e.activation(
                        out=hwork[:, slot * 256 : slot * 256 + 256],
                        in_=mm1_slot(slot),
                        func=AF.Relu, bias=b1_sb[:, m : m + 1], scale=1.0),
                        waits=W(("pe", c_mm1), ("dve", hwork_free[slot])),
                        inc="act")
                    relu_cnt[slot] = c_rel
                    if m % 2 == 1:
                        p0 = (m - 1) % 4
                        c_hq = quant_exact(
                            hwork[:, p0 * 256 : p0 * 256 + 512],
                            hqT[par][:, (m - 1) * 256 : (m - 1) * 256 + 512],
                            512, 3, -6.0, False, first_waits=W(("act", c_rel)))
                        hwork_free[p0] = c_hq
                        hwork_free[p0 + 1] = c_hq
                        last_hq = c_hq
                hq_done[par] = last_hq

                if pend_mm2 is not None:
                    emit_f2(*pend_mm2)
                pend_mm2 = (th, par)
            if DBG_FFN:
                emit_f2(*pend_mm2)
            featbuf_pe_free = cnt["pe"]

        op("sync", lambda e: e.nop(),
           waits=W(("sb0", cnt["sb0"]), ("sb1", cnt["sb1"])))

        ENGS = ["sync", "scalar", "vector", "tensor", "gpsimd"]
        emitted_waits = {e: {} for e in ENGS}

        OWN_SEM = {"vector": "dve", "scalar": "act", "gpsimd": "gps"}

        def replay(eng_name):
            ew = emitted_waits[eng_name]
            own = OWN_SEM.get(eng_name)
            def body(eng):
                for (en, fn, waits, inc, amt) in OPS:
                    if en != eng_name:
                        continue
                    allw = list(waits)
                    if own is not None and inc == own:
                        allw.append((own, cnt_at[id(fn)]))
                    for (sname, val) in allw:
                        if val > ew.get(sname, 0):
                            eng.wait_ge(sems[sname], val)
                            ew[sname] = val
                    instr = fn(eng)
                    if inc is not None:
                        assert instr is not None, f"op on {eng_name} returned no instruction"
                        instr.then_inc(sems[inc], amt, skip_validation=(amt == 16))
            return body

        for en in ENGS:
            if any(o[0] == en for o in OPS):
                getattr(block, en)(replay(en))

    return nc


def _host_inputs(inputs, core):
    x = np.asarray(inputs["L0_attn_x"])
    xs = np.ascontiguousarray(x[2 * core : 2 * core + 2])
    xf = np.ascontiguousarray(xs.transpose(0, 2, 1))
    W_DQ = np.asarray(inputs["L0_attn_W_DQ"]); W_UQ = np.asarray(inputs["L0_attn_W_UQ"])
    W_DKV = np.asarray(inputs["L0_attn_W_DKV"]); W_UKV = np.asarray(inputs["L0_attn_W_UKV"])
    W_QR = np.asarray(inputs["L0_attn_W_QR"]); W_KR = np.asarray(inputs["L0_attn_W_KR"])
    W_O = np.asarray(inputs["L0_attn_W_O"])
    qnw = np.asarray(inputs["L0_attn_q_norm_w"]); knw = np.asarray(inputs["L0_attn_k_norm_w"])
    w1 = np.asarray(inputs["L0_ffn_w1"]); b1 = np.asarray(inputs["L0_ffn_b1"])
    w2 = np.asarray(inputs["L0_ffn_w2"]); b2 = np.asarray(inputs["L0_ffn_b2"])

    tile_pc = lambda a, P=128: np.ascontiguousarray(
        a.reshape(a.shape[0] // P, P, a.shape[1]).transpose(1, 0, 2).reshape(P, -1))
    def pad_rt(wt):
        half = np.concatenate([np.zeros((64, 64), np.float32),
                               np.ascontiguousarray(wt.T)], axis=1)
        return np.concatenate([half, half], axis=0)

    def par_rt(wt, par):
        out = np.zeros((128, 128), np.float32)
        out[par * 64 : par * 64 + 64, 64:128] = np.ascontiguousarray(wt.T)
        return out
    d = {
        "x_tok": xs,
        "x_feat": xf,
        "wdq": tile_pc(W_DQ),
        "wuqt": tile_pc(np.ascontiguousarray(W_UQ.T)),
        "wdkv": tile_pc(W_DKV),
        "wukvt": tile_pc(np.ascontiguousarray(W_UKV.T)),
        "wqrt_p": pad_rt(W_QR),
        "wkrt_e": par_rt(W_KR, 0),
        "wkrt_o": par_rt(W_KR, 1),
        "wot": np.ascontiguousarray(W_O.T),
        "wq_b": np.ascontiguousarray(np.broadcast_to(qnw, (128, 64))),
        "wk_b": np.ascontiguousarray(np.broadcast_to(knw, (128, 64))),
        "w1tl": np.ascontiguousarray(
            w1.T.reshape(8, 128, 32, 128).transpose(2, 1, 0, 3).reshape(32, 128, 1024)),
        "w2t": np.ascontiguousarray(w2.T),
        "b1_t": np.ascontiguousarray(b1.reshape(32, 128).T),
        "b2_t": np.ascontiguousarray(b2.reshape(8, 128).T),
        "identf": np.eye(128, dtype=np.float32),
    }
    d = {k: np.ascontiguousarray(np.asarray(v, dtype=np.float32)) for k, v in d.items()}
    import ml_dtypes
    mk = np.where(np.arange(128)[None, :] >= np.arange(128)[:, None],
                  np.float32(0.0), np.float32(-1e30)).astype(ml_dtypes.bfloat16)
    d["maskd"] = np.ascontiguousarray(mk)
    d["maskdT"] = np.ascontiguousarray(mk.T)
    return d


def kernel(**inputs):
    nc = build_program()
    in_maps = [_host_inputs(inputs, c) for c in range(8)]
    res = run_bass_kernel_spmd(nc, in_maps, core_ids=list(range(8)))
    out = np.concatenate([r["out_tok"] for r in res.results], axis=0)
    if TAP is not None:
        kernel.tap = np.concatenate([r["tap"] for r in res.results], axis=0)
    return out
